# version 02
# baseline (speedup 1.0000x reference)
import math
import numpy as np
import ml_dtypes
from contextlib import ExitStack

import concourse.bass as bass
import concourse.mybir as mybir
from concourse.bass import IndirectOffsetOnAxis
from concourse.bass_utils import run_bass_kernel_spmd

F32 = mybir.dt.float32
BF16 = mybir.dt.bfloat16
I32 = mybir.dt.int32
AF = mybir.ActivationFunctionType
ALU = mybir.AluOpType
AX = mybir.AxisListType

B, T, C = 2, 4096, 2048
D = 128
NKV = 4
CKV = 512
TOK = 1024
NTT = TOK // 128
NTG = T // 512
NQT = T // 128
KC = C // 128
NG = 8
NE = 32
FI = 768
MASKVAL = -3.0e38
INV2048 = 1.0 / 2048.0
SCQ = 1.0 / math.sqrt(128.0)

ENGS = ("tensor", "vector", "scalar", "gpsimd", "sync")


class Sem:
    def __init__(self, handle):
        self.h = handle
        self.n = 0


class Sched:

    def __init__(self):
        self.ops = {e: [] for e in ENGS}
        self.wmax = {e: {} for e in ENGS}

    def add(self, eng, fn=None, waits=(), incs=(), drain=False):
        wl = []
        for (sm, cnt) in waits:
            if cnt is None or cnt <= 0:
                continue
            key = id(sm)
            if cnt <= self.wmax[eng].get(key, 0):
                continue
            self.wmax[eng][key] = cnt
            wl.append((sm, cnt))
        for (sm, k) in incs:
            sm.n += k
        self.ops[eng].append((fn, wl, list(incs), drain))

    def flush(self, block):
        def mk(eng_name):
            def run(e):
                for (fn, wl, il, drain) in self.ops[eng_name]:
                    for (sm, cnt) in wl:
                        e.wait_ge(sm.h, cnt)
                    if drain:
                        e.drain()
                    inst = fn(e) if fn is not None else None
                    for (sm, k) in il:
                        if inst is None:
                            inst = e.nop()
                        inst.then_inc(sm.h, k)
                        inst = None
            return run

        for eng_name in ENGS:
            if self.ops[eng_name]:
                getattr(block, eng_name)(mk(eng_name))


def build_program():
    nc = bass.Bass("TRN2", target_bir_lowering=False, debug=False)

    d_x = nc.dram_tensor("x_b", [T, C], F32, kind="ExternalInput").ap()
    d_xown = nc.dram_tensor("xown", [TOK, C], F32, kind="ExternalInput").ap()
    d_wq = nc.dram_tensor("wq", [C, 512], F32, kind="ExternalInput").ap()
    d_bq = nc.dram_tensor("bq", [512], F32, kind="ExternalInput").ap()
    d_wk = nc.dram_tensor("wk", [C, 128], F32, kind="ExternalInput").ap()
    d_bk = nc.dram_tensor("bk", [128], F32, kind="ExternalInput").ap()
    d_wv = nc.dram_tensor("wv", [C, 128], F32, kind="ExternalInput").ap()
    d_bv = nc.dram_tensor("bv", [128], F32, kind="ExternalInput").ap()
    d_wo = nc.dram_tensor("wo", [CKV, C], F32, kind="ExternalInput").ap()
    d_bo = nc.dram_tensor("bo", [C], F32, kind="ExternalInput").ap()
    d_gwt = nc.dram_tensor("gwt", [C, NE], F32, kind="ExternalInput").ap()
    d_gup = nc.dram_tensor("gup", [NE, C, 2 * FI], F32, kind="ExternalInput").ap()
    d_dnt = nc.dram_tensor("dnt", [NE, FI, C], F32, kind="ExternalInput").ap()
    d_mg = nc.dram_tensor("mg", [NE], I32, kind="ExternalInput").ap()
    d_de = nc.dram_tensor("de", [NG], I32, kind="ExternalInput").ap()
    d_ident = nc.dram_tensor("identh", [128, 128], BF16, kind="ExternalInput").ap()
    d_masks = nc.dram_tensor("masks", [128, 4 * 512], F32, kind="ExternalInput").ap()
    d_ct = nc.dram_tensor("ct", [128, T], F32, kind="ExternalInput").ap()
    d_st = nc.dram_tensor("st", [128, T], F32, kind="ExternalInput").ap()
    d_ones = nc.dram_tensor("onescol", [1, 128], F32, kind="ExternalInput").ap()
    d_rbgu = nc.dram_tensor("rbgu", [128, KC], F32, kind="ExternalInput").ap()
    d_rbdn = nc.dram_tensor("rbdn", [128, 6], F32, kind="ExternalInput").ap()
    d_rby = nc.dram_tensor("rby", [128, 32], F32, kind="ExternalInput").ap()
    d_jc = nc.dram_tensor("jconst", [1, 1], F32, kind="ExternalInput").ap()
    d_swap = nc.dram_tensor("swaph", [128, 128], F32, kind="ExternalInput").ap()
    d_out = nc.dram_tensor("out", [TOK, C], F32, kind="ExternalOutput").ap()
    DBG = False
    if DBG:
        d_dbg_q = nc.dram_tensor("dbg_q", [128, T], F32, kind="ExternalOutput").ap()
        d_dbg_k = nc.dram_tensor("dbg_k", [128, T], F32, kind="ExternalOutput").ap()
        d_dbg_v = nc.dram_tensor("dbg_v", [128, T], F32, kind="ExternalOutput").ap()
        d_dbg_yh = nc.dram_tensor("dbg_yh", [128, T], F32, kind="ExternalOutput").ap()
        d_dbg_x2 = nc.dram_tensor("dbg_x2", [128, NTT * C], F32, kind="ExternalOutput").ap()
        d_dbg_w = nc.dram_tensor("dbg_w", [128, NTT * NG], F32, kind="ExternalOutput").ap()

    d_yb = nc.dram_tensor("ybounce", [T, D], BF16, kind="Internal")
    d_ybg = nc.dram_tensor("ybgather", [4 * T, D], BF16, kind="Internal")

    gup_flat = d_gup.rearrange("e c f -> (e c) f")
    dnt_flat = d_dnt.rearrange("e i c -> (e i) c")

    octx = ExitStack()
    with octx:
        def sb(name, shape, dt):
            return octx.enter_context(nc.sbuf_tensor(name, shape, dt))

        def sem(name):
            return Sem(octx.enter_context(nc.semaphore(name)))

        s_cpl = sem("s_cpl")
        s_csp = sem("s_csp")
        s_wq = sem("s_wq")
        s_xs = [sem(f"s_x{i}") for i in range(4)]
        s_tb = [sem(f"s_tb{i}") for i in range(2)]
        s_xsv = sem("s_xsv")
        s_pe = sem("s_pe")
        s_dve = sem("s_dve")
        s_act = sem("s_act")
        s_gate = sem("s_gate")
        s_up = sem("s_up")
        s_dn = sem("s_dn")
        s_yb = sem("s_yb")
        s_cc = sem("s_cc")
        s_yr = sem("s_yr")
        s_out = sem("s_out")
        s_dbg = sem("s_dbg")
        s_dbg2 = sem("s_dbg2")

        ident = sb("ident", [128, 128], BF16)
        swap_sb = sb("swap_sb", [128, 128], F32)
        bq_4 = sb("bq4", [128, 4], F32)
        bq_col = sb("bq_col", [128, 1], F32)
        bk_col = sb("bk_col", [128, 1], F32)
        bv_col = sb("bv_col", [128, 1], F32)
        sc_col = sb("sc_col", [128, 1], F32)
        eps_col = sb("eps_col", [128, 1], F32)
        nh_col = sb("nh_col", [128, 1], F32)
        ones_sb = sb("ones_sb", [1, 128], F32)
        def_sb = sb("def_sb", [1, NG], F32)
        de_sb = sb("de_sb", [1, NG], I32)
        mg_sb = sb("mg_sb", [1, NE], I32)
        mgf_sb = sb("mgf_sb", [1, NE], F32)
        de2048r = sb("de2048r", [1, NG], F32)
        de768r = sb("de768r", [1, NG], F32)
        de2048 = sb("de2048", [128, NG], F32)
        de768 = sb("de768", [128, NG], F32)
        mg_b = sb("mg_b", [128, NE], F32)
        og = sb("og", [128, NG * NE], F32)
        gcol = sb("gcol", [128, 1], F32)
        rbgu = sb("rbgu_s", [128, KC], F32)
        rbdn = sb("rbdn_s", [128, 6], F32)
        rby = sb("rby_s", [128, 32], F32)
        jc_sb = sb("jc_sb", [1, 1], F32)
        jcol = sb("jcol", [128, 1], F32)
        idxgu_f = sb("idxgu_f", [128, NG * KC], F32)
        idxgu = sb("idxgu", [128, NG * KC], I32)
        idxdn_f = sb("idxdn_f", [128, NG * 6], F32)
        idxdn = sb("idxdn", [128, NG * 6], I32)
        idxy_f = sb("idxy_f", [128, 32], F32)
        idxy = sb("idxy", [128, 32], I32)
        w_sb = sb("w_sb", [128, NTT * NG], F32)
        gwt_sb = sb("gwt_sb", [128, KC * NE], BF16)

        xsave = sb("xsave", [128, NTT * C], F32)

        sch = Sched()

        def gp_dma(out_ap, in_ap, s, w=()):
            sch.add("gpsimd", lambda e, o=out_ap, i=in_ap: e.dma_start(out=o, in_=i),
                    waits=w, incs=[(s, 16)])

        def sp_dma(out_ap, in_ap, s, w=(), slow=False):
            sch.add("sync", lambda e, o=out_ap, i=in_ap, slow=slow: e.dma_start(
                out=o, in_=i, allow_slow_non_contiguous=slow),
                waits=w, incs=[(s, 16)])

        sp_dma(ident[:, :], d_ident[:, :], s_csp)
        sp_dma(swap_sb[:, :], d_swap[:, :], s_csp)
        sp_dma(bq_4[:, :], d_bq.rearrange("(g d) -> d g", d=128), s_csp, slow=True)
        sp_dma(bk_col[:, :], d_bk[:, None], s_csp, slow=True)
        sp_dma(bv_col[:, :], d_bv[:, None], s_csp, slow=True)
        sp_dma(ones_sb[:, :], d_ones[:, :], s_csp)
        sp_dma(de_sb[:, :], d_de[None, :], s_csp)
        sp_dma(mg_sb[:, :], d_mg[None, :], s_csp)
        sp_dma(rbgu[:, :], d_rbgu[:, :], s_csp)
        sp_dma(rbdn[:, :], d_rbdn[:, :], s_csp)
        sp_dma(rby[:, :], d_rby[:, :], s_csp)
        sp_dma(jc_sb[:, :], d_jc[:, :], s_csp)
        CSP_T1 = s_csp.n
        gp_dma(gwt_sb[:, :].rearrange("p (c e) -> p c e", c=KC),
               d_gwt.rearrange("(c p) e -> p c e", p=128), s_cpl)

        sch.add("vector", lambda e: e.memset(sc_col[:, :], SCQ))
        sch.add("vector", lambda e: e.memset(eps_col[:, :], 1e-5))
        sch.add("vector", lambda e: e.memset(nh_col[:, :], -0.5))
        sch.add("vector", lambda e: e.reduce_sum(bq_col[:, :], bq_4[:, :], axis=AX.X),
                waits=[(s_csp, CSP_T1)])
        sch.add("vector", lambda e: e.tensor_scalar_mul(bq_col[:, :], bq_col[:, :], SCQ),
                drain=True)
        sch.add("vector", lambda e: e.tensor_copy(def_sb[:, :], de_sb[:, :]),
                waits=[(s_csp, CSP_T1)])
        sch.add("vector", lambda e: e.tensor_scalar_mul(de2048r[:, :], def_sb[:, :],
                                                        2048.0), drain=True)
        sch.add("vector", lambda e: e.tensor_scalar_mul(de768r[:, :], def_sb[:, :],
                                                        768.0))
        sch.add("vector", lambda e: e.tensor_copy(mgf_sb[:, :], mg_sb[:, :]),
                waits=[(s_csp, CSP_T1)], drain=True, incs=[(s_dve, 1)])
        dve_rows = s_dve.n

        init_ctx = ExitStack()
        p_bc = init_ctx.enter_context(nc.psum_tensor("p_bc", [128, 64], F32))
        sch.add("tensor", lambda e: e.matmul(p_bc[:, 0:NG], ones_sb[:, :],
                                             de2048r[:, :], start=True, stop=True),
                waits=[(s_dve, dve_rows)])
        sch.add("tensor", lambda e: e.matmul(p_bc[:, 8:8 + NG], ones_sb[:, :],
                                             de768r[:, :], start=True, stop=True))
        sch.add("tensor", lambda e: e.matmul(p_bc[:, 16:16 + NE], ones_sb[:, :],
                                             mgf_sb[:, :], start=True, stop=True))
        sch.add("tensor", lambda e: e.matmul(p_bc[:, 48:49], ones_sb[:, :],
                                             jc_sb[:, :], start=True, stop=True),
                incs=[(s_pe, 1)])
        pe_bc = s_pe.n
        sch.add("vector", lambda e: e.tensor_copy(de2048[:, :], p_bc[:, 0:NG]),
                waits=[(s_pe, pe_bc)])
        sch.add("vector", lambda e: e.tensor_copy(de768[:, :], p_bc[:, 8:8 + NG]))
        sch.add("vector", lambda e: e.tensor_copy(mg_b[:, :], p_bc[:, 16:16 + NE]))
        sch.add("vector", lambda e: e.tensor_copy(jcol[:, :], p_bc[:, 48:49]))
        for g in range(NG):
            sch.add("vector", lambda e, g=g: e.memset(gcol[:, :], float(g)),
                    drain=True)
            sch.add("vector", lambda e, g=g: e.tensor_scalar(
                out=og[:, g * NE:(g + 1) * NE], in0=mg_b[:, :], scalar1=gcol[:, :],
                scalar2=None, op0=ALU.is_equal), drain=True)
        for g in range(NG):
            sch.add("vector", lambda e, g=g: e.tensor_scalar(
                out=idxgu_f[:, g * KC:(g + 1) * KC], in0=rbgu[:, :],
                scalar1=de2048[:, g:g + 1], scalar2=None, op0=ALU.add),
                drain=(g == 0))
            sch.add("vector", lambda e, g=g: e.tensor_scalar(
                out=idxdn_f[:, g * 6:(g + 1) * 6], in0=rbdn[:, :],
                scalar1=de768[:, g:g + 1], scalar2=None, op0=ALU.add))
        sch.add("vector", lambda e: e.tensor_scalar(
            out=idxy_f[:, :], in0=rby[:, :], scalar1=jcol[:, :], scalar2=None,
            op0=ALU.add))
        sch.add("vector", lambda e: e.tensor_copy(idxgu[:, :], idxgu_f[:, :]),
                drain=True)
        sch.add("vector", lambda e: e.tensor_copy(idxdn[:, :], idxdn_f[:, :]))
        sch.add("vector", lambda e: e.tensor_copy(idxy[:, :], idxy_f[:, :]),
                incs=[(s_dve, 1)])
        IDX_DONE = s_dve.n

        with nc.Block() as blk:
            sch.flush(blk)
        init_ctx.close()

        scA = ExitStack()
        qT = scA.enter_context(nc.sbuf_tensor("qT", [128, T], BF16))
        kT = scA.enter_context(nc.sbuf_tensor("kT", [128, T], BF16))
        v_sb = scA.enter_context(nc.sbuf_tensor("v_sb", [128, T], BF16))
        p1 = ExitStack()
        with p1:
            def sb1(name, shape, dt):
                return p1.enter_context(nc.sbuf_tensor(name, shape, dt))

            scr2048 = sb1("scr2048", [128, C], F32)
            wqacc = scr2048
            xsq = scr2048
            wqeff = sb1("wqeff", [128, C], BF16)
            wk_sb = sb1("wk_sb", [128, C], BF16)
            wv_sb = sb1("wv_sb", [128, C], BF16)
            xring = [sb1(f"xr{i}", [128, C], F32) for i in range(4)]
            xln = [sb1(f"xln{i}", [128, C], BF16) for i in range(2)]
            xlnT = [sb1(f"xlnT{i}", [128, KC * 512], BF16) for i in range(2)]
            ct_r = [sb1(f"ct_r{i}", [128, 512], F32) for i in range(2)]
            st_r = [sb1(f"st_r{i}", [128, 512], F32) for i in range(2)]
            qf = sb1("qf", [128, 512], F32)
            kf = sb1("kf", [128, 512], F32)
            vtmp = sb1("vtmp", [128, 512], BF16)
            rt = [sb1(f"rt{i}", [128, 512], F32) for i in range(2)]
            stat = sb1("stat", [128, 8], F32)

            p_tp = [p1.enter_context(nc.psum_tensor(f"p_tp{i}", [128, 512], F32))
                    for i in range(2)]
            p_sw = [p1.enter_context(nc.psum_tensor(f"p_sw{i}", [128, 512], F32))
                    for i in range(2)]
            p_q = p1.enter_context(nc.psum_tensor("p_q", [128, 512], F32))
            p_k = p1.enter_context(nc.psum_tensor("p_k", [128, 512], F32))
            p_v = p1.enter_context(nc.psum_tensor("p_v", [128, 512], F32))
            p_vt = p1.enter_context(nc.psum_tensor("p_vt", [128, 512], F32))

            sch = Sched()

            for g in range(4):
                src = d_wq[:, 128 * g:128 * (g + 1)].rearrange(
                    "(fc p) d -> p fc d", p=128)
                dst = wqacc[:, :].rearrange("p (fc d) -> p fc d", fc=KC)
                if g == 0:
                    sch.add("gpsimd", lambda e, s=src, dst=dst: e.dma_start(
                        out=dst, in_=s), incs=[(s_wq, 16)])
                else:
                    sch.add("gpsimd", lambda e, s=src, dst=dst: e.dma_start(
                        out=dst, in_=s, accum_op=ALU.add),
                        waits=[(s_wq, s_wq.n)], incs=[(s_wq, 16)])
            WQ_T = s_wq.n
            gp_dma(wk_sb[:, :].rearrange("p (fc d) -> p fc d", fc=KC),
                   d_wk.rearrange("(fc p) d -> p fc d", p=128), s_cpl)
            gp_dma(wv_sb[:, :].rearrange("p (fc d) -> p fc d", fc=KC),
                   d_wv.rearrange("(fc p) d -> p fc d", p=128), s_cpl)
            CPL_T2 = s_cpl.n
            CSP_T2 = s_csp.n
            for tt in range(NTT):
                sp_dma(xsave[:, tt * C:(tt + 1) * C],
                       d_xown[128 * tt:128 * (tt + 1), :], s_xsv)
            XSV_T = s_xsv.n

            sch.add("vector", lambda e: e.tensor_copy(wqeff[:, :], wqacc[:, :]),
                    waits=[(s_wq, WQ_T)], incs=[(s_dve, 1)])
            DVE_WQ = s_dve.n

            apply_done = {}
            swap_used = {}
            tp_last = {}
            xlnT_evac = {}
            qkv_mm = {}
            qkv_evac = {}
            v_evac = {}
            rope_done = {}
            xload = {}

            for tg in range(NTG):
                tslot = tg % 2
                wtb = [(s_dve, rope_done[tg - 2])] if tg >= 2 else []
                sch.add("sync", lambda e, tg=tg, tslot=tslot: e.dma_start(
                    out=ct_r[tslot][:, :], in_=d_ct[:, tg * 512:(tg + 1) * 512]),
                    waits=wtb, incs=[(s_tb[tslot], 16)])
                sch.add("sync", lambda e, tg=tg, tslot=tslot: e.dma_start(
                    out=st_r[tslot][:, :], in_=d_st[:, tg * 512:(tg + 1) * 512]),
                    incs=[(s_tb[tslot], 16)])
                tb_cnt = s_tb[tslot].n
                for i in range(4):
                    t = 4 * tg + i
                    slot = t % 4
                    w = [(s_act, apply_done[t - 4])] if t >= 4 else []
                    ssem = s_xs[slot]
                    sch.add("sync", lambda e, t=t, slot=slot: e.dma_start(
                        out=xring[slot][:, :], in_=d_x[128 * t:128 * (t + 1), :]),
                        waits=w, incs=[(ssem, 16)])
                    xload[t] = (ssem, ssem.n)

                for i in range(4):
                    t = 4 * tg + i
                    ssem, scnt = xload[t]
                    xt = xring[t % 4]
                    sch.add("vector", lambda e, xt=xt: e.reduce_sum(
                        stat[:, 0:1], xt[:, :], axis=AX.X, negate=True),
                        waits=[(ssem, scnt)])
                    sch.add("scalar", lambda e, xt=xt: e.activation(
                        out=xsq[:, :], in_=xt[:, :], func=AF.Square,
                        accum_out=stat[:, 1:2]),
                        waits=[(ssem, scnt), (s_dve, DVE_WQ)],
                        incs=[(s_act, 1)])
                    a_sq = s_act.n
                    sch.add("vector", lambda e: e.tensor_scalar_mul(
                        stat[:, 2:3], stat[:, 0:1], INV2048), drain=True)
                    sch.add("vector", lambda e: e.tensor_scalar_mul(
                        stat[:, 3:4], stat[:, 1:2], INV2048), waits=[(s_act, a_sq)])
                    sch.add("vector", lambda e: e.tensor_mul(
                        stat[:, 4:5], stat[:, 2:3], stat[:, 2:3]), drain=True)
                    sch.add("vector", lambda e: e.tensor_sub(
                        stat[:, 5:6], stat[:, 3:4], stat[:, 4:5]), drain=True,
                        incs=[(s_dve, 1)])
                    d_var = s_dve.n
                    sch.add("scalar", lambda e: e.activation(
                        out=stat[:, 6:7], in_=stat[:, 5:6], func=AF.Ln,
                        bias=eps_col[:, :]), waits=[(s_dve, d_var)])
                    sch.add("scalar", lambda e: e.activation(
                        out=stat[:, 6:7], in_=stat[:, 6:7], func=AF.Exp,
                        scale=nh_col[:, :]), drain=True, incs=[(s_act, 1)])
                    a_rstd = s_act.n
                    sch.add("vector", lambda e: e.tensor_mul(
                        stat[:, 7:8], stat[:, 2:3], stat[:, 6:7]),
                        waits=[(s_act, a_rstd)], drain=True, incs=[(s_dve, 1)])
                    d_nmr = s_dve.n
                    xo = xln[t % 2]
                    wap = [(s_dve, d_nmr)]
                    if t >= 2:
                        wap.append((s_pe, tp_last[t - 2]))
                    sch.add("scalar", lambda e, xt=xt, xo=xo: e.activation(
                        out=xo[:, :], in_=xt[:, :], func=AF.Identity,
                        bias=stat[:, 7:8], scale=stat[:, 6:7]),
                        waits=wap, drain=True, incs=[(s_act, 1)])
                    apply_done[t] = s_act.n

                    for fq in range(4):
                        bank = p_tp[(t * 4 + fq) % 2]
                        w = [(s_act, apply_done[t])]
                        pidx = t * 4 + fq - 2
                        if pidx >= 0:
                            w.append((s_dve, xlnT_evac[pidx]))
                        for fs in range(4):
                            fc = fq * 4 + fs
                            sch.add("tensor", lambda e, xo=xo, fc=fc, fs=fs, bank=bank:
                                    e.matmul(bank[:, fs * 128:(fs + 1) * 128],
                                             xo[:, fc * 128:(fc + 1) * 128],
                                             ident[:, :], start=True, stop=True),
                                    waits=w if fs == 0 else [],
                                    incs=[(s_pe, 1)] if fs == 3 else [])
                        pe_c = s_pe.n
                        xTd = xlnT[tg % 2]

                        def ev(e, xTd=xTd, bank=bank, fq=fq, i=i):
                            r = None
                            for fs in range(4):
                                fc = fq * 4 + fs
                                r = e.tensor_copy(
                                    xTd[:, fc * 512 + i * 128: fc * 512 + (i + 1) * 128],
                                    bank[:, fs * 128:(fs + 1) * 128])
                            return r
                        sch.add("vector", ev, waits=[(s_pe, pe_c)],
                                incs=[(s_dve, 1)])
                        xlnT_evac[t * 4 + fq] = s_dve.n
                    tp_last[t] = s_pe.n
                XLNT_TG = s_dve.n

                xTd = xlnT[tg % 2]
                w0 = [(s_dve, XLNT_TG), (s_dve, DVE_WQ), (s_cpl, CPL_T2)]
                if tg >= 1:
                    w0.append((s_act, qkv_evac[tg - 1]))
                for (psum, wmat, nm) in ((p_q, wqeff, "q"), (p_k, wk_sb, "k"),
                                         (p_v, wv_sb, "v")):
                    for kc in range(KC):
                        sch.add("tensor", lambda e, psum=psum, wmat=wmat, kc=kc,
                                xTd=xTd: e.matmul(
                                    psum[:, :], wmat[:, kc * 128:(kc + 1) * 128],
                                    xTd[:, kc * 512:(kc + 1) * 512],
                                    start=(kc == 0), stop=(kc == KC - 1)),
                                waits=w0 if kc == 0 else [],
                                incs=[(s_pe, 1)] if kc == KC - 1 else [])
                    qkv_mm[(tg, nm)] = s_pe.n
                    w0 = []

                wqe = [(s_pe, qkv_mm[(tg, "q")])]
                if tg >= 1:
                    wqe.append((s_dve, rope_done[tg - 1]))
                sch.add("scalar", lambda e: e.activation(
                    out=qf[:, :], in_=p_q[:, :], func=AF.Identity,
                    bias=bq_col[:, :], scale=sc_col[:, :]),
                    waits=wqe, incs=[(s_act, 1)])
                a_q = s_act.n
                sch.add("scalar", lambda e: e.activation(
                    out=kf[:, :], in_=p_k[:, :], func=AF.Identity, bias=bk_col[:, :]),
                    waits=[(s_pe, qkv_mm[(tg, "k")])], incs=[(s_act, 1)])
                a_k = s_act.n
                sch.add("scalar", lambda e: e.activation(
                    out=vtmp[:, :], in_=p_v[:, :], func=AF.Identity, bias=bv_col[:, :]),
                    waits=[(s_pe, qkv_mm[(tg, "v")])], incs=[(s_act, 1)])
                a_v = s_act.n
                qkv_evac[tg] = s_act.n

                w = [(s_act, a_v)]
                if tg >= 1:
                    w.append((s_dve, v_evac[tg - 1]))
                for i in range(4):
                    sch.add("tensor", lambda e, i=i: e.matmul(
                        p_vt[:, i * 128:(i + 1) * 128],
                        vtmp[:, i * 128:(i + 1) * 128], ident[:, :],
                        start=True, stop=True), waits=w if i == 0 else [],
                        incs=[(s_pe, 1)] if i == 3 else [])
                pe_vt = s_pe.n

                def ev_v(e, tg=tg):
                    r = None
                    for i in range(4):
                        tglob = 4 * tg + i
                        r = e.tensor_copy(v_sb[:, tglob * 128:(tglob + 1) * 128],
                                          p_vt[:, i * 128:(i + 1) * 128])
                    return r
                sch.add("vector", ev_v, waits=[(s_pe, pe_vt)], incs=[(s_dve, 1)])
                v_evac[tg] = s_dve.n

                cts = ct_r[tslot][:, :]
                sts = st_r[tslot][:, :]
                for si, (srcf, dst, a_cnt) in enumerate(((qf, qT, a_q),
                                                         (kf, kT, a_k))):
                    swb = p_sw[si]
                    w = [(s_act, a_cnt)]
                    if tg >= 1:
                        w.append((s_dve, swap_used[(tg - 1, si)]))
                    sch.add("tensor", lambda e, srcf=srcf, swb=swb: e.matmul(
                        swb[:, :], swap_sb[:, :], srcf[:, :],
                        start=True, stop=True), waits=w, incs=[(s_pe, 1)])
                    pe_sw = s_pe.n
                    sch.add("vector", lambda e, srcf=srcf, cts=cts: e.tensor_mul(
                        rt[0][:, :], srcf[:, :], cts),
                        waits=[(s_act, a_cnt), (s_tb[tslot], tb_cnt)], drain=True)
                    sch.add("vector", lambda e, swb=swb, sts=sts: e.tensor_mul(
                        rt[1][:, :], swb[:, :], sts), waits=[(s_pe, pe_sw)],
                        incs=[(s_dve, 1)])
                    swap_used[(tg, si)] = s_dve.n
                    sch.add("vector", lambda e, dst=dst, tg=tg: e.tensor_add(
                        dst[:, tg * 512:(tg + 1) * 512], rt[0][:, :], rt[1][:, :]),
                        drain=True, incs=[(s_dve, 1)])
                rope_done[tg] = s_dve.n

            with nc.Block() as blk:
                sch.flush(blk)

        gate_buf = octx.enter_context(nc.sbuf_tensor("gate_buf", [128, KC * FI], BF16, side="right"))
        p2 = ExitStack()
        with p2:
            def sb2(name, shape, dt):
                return p2.enter_context(nc.sbuf_tensor(name, shape, dt))

            y_h = sb2("y_h", [128, T], BF16)
            masks = sb2("masks_s", [128, 4 * 512], F32)
            Sst = [sb2(f"Sst{i}", [128, T], F32) for i in range(2)]
            Pst = [sb2(f"Pst{i}", [128, T], BF16) for i in range(2)]
            PTst = [sb2(f"PTst{i}", [128, T], BF16) for i in range(2)]
            mx = [sb2(f"mx{i}", [128, 8], F32) for i in range(2)]
            nmf = [sb2(f"nmf{i}", [128, 1], F32) for i in range(2)]
            ls = [sb2(f"ls{i}", [128, 8], F32) for i in range(2)]
            rinv = [sb2(f"rinv{i}", [128, 1], F32) for i in range(2)]

            p_S = [p2.enter_context(nc.psum_tensor(f"p_S{i}", [128, 512], F32))
                   for i in range(4)]
            p_PT = [p2.enter_context(nc.psum_tensor(f"p_PT{i}", [128, 512], F32))
                    for i in range(2)]
            p_y = [p2.enter_context(nc.psum_tensor(f"p_y{i}", [128, 128], F32))
                   for i in range(2)]

            sch = Sched()
            sp_dma(masks[:, :], d_masks[:, :], s_csp)
            CSP_T3 = s_csp.n

            for c in range(KC):
                sch.add("gpsimd", lambda e, c=c: e.indirect_dma_start(
                    out=gate_buf[:, c * FI:(c + 1) * FI], out_offset=None,
                    in_=gup_flat,
                    in_offset=IndirectOffsetOnAxis(idxgu[:, c:c + 1], 0)),
                    waits=[(s_dve, IDX_DONE)] if c == 0 else [],
                    incs=[(s_gate, 16)])

            sbank_free = {}
            s_done = {}
            mask_done = {}
            s_evac = {}
            fmax_done = {}
            exp_done = {}
            t_done = {}
            ptv_done = {}
            av_done = {}
            rinv_done = {}
            yev_done = {}

            def emit_Sphase(qt):
                par = qt % 2
                nkc = qt // 4 + 1
                for kc in range(nkc):
                    bid = par * 2 + (kc % 2)
                    bank = p_S[bid]
                    w = []
                    if bid in sbank_free:
                        w.append(sbank_free[bid])
                    sch.add("tensor", lambda e, qt=qt, kc=kc, bank=bank: e.matmul(
                        bank[:, :], qT[:, qt * 128:(qt + 1) * 128],
                        kT[:, kc * 512:(kc + 1) * 512], start=True, stop=True),
                        waits=w, incs=[(s_pe, 1)])
                    s_done[(qt, kc)] = s_pe.n
                    dg = (kc == qt // 4)
                    if dg:
                        m = qt % 4
                        sch.add("vector", lambda e, bank=bank, m=m: e.tensor_add(
                            bank[:, :], bank[:, :], masks[:, m * 512:(m + 1) * 512]),
                            waits=[(s_pe, s_done[(qt, kc)]), (s_csp, CSP_T3)],
                            drain=True)
                    sch.add("vector", lambda e, bank=bank, kc=kc, par=par:
                            e.reduce_max(mx[par][:, kc:kc + 1], bank[:, :],
                                         axis=AX.X),
                            waits=[(s_pe, s_done[(qt, kc)])], drain=dg,
                            incs=[(s_dve, 1)])
                    mask_done[(qt, kc)] = s_dve.n
                    sch.add("scalar", lambda e, bank=bank, kc=kc, par=par: e.copy(
                        out=Sst[par][:, kc * 512:(kc + 1) * 512], in_=bank[:, :]),
                        waits=[(s_dve, mask_done[(qt, kc)])], incs=[(s_act, 1)])
                    s_evac[(qt, kc)] = s_act.n
                    sbank_free[bid] = (s_act, s_act.n)
                sch.add("vector", lambda e, par=par, nkc=nkc: e.reduce_max(
                    nmf[par][:, :], mx[par][:, 0:nkc], axis=AX.X, negate=True),
                    drain=True, incs=[(s_dve, 1)])
                fmax_done[qt] = s_dve.n

            def emit_exp(qt):
                par = qt % 2
                nkc = qt // 4 + 1
                for kc in range(nkc):
                    sch.add("scalar", lambda e, kc=kc, par=par: e.activation(
                        out=Pst[par][:, kc * 512:(kc + 1) * 512],
                        in_=Sst[par][:, kc * 512:(kc + 1) * 512], func=AF.Exp,
                        bias=nmf[par][:, :], accum_out=ls[par][:, kc:kc + 1]),
                        waits=[(s_dve, fmax_done[qt])], drain=(kc == 0),
                        incs=[(s_act, 1)])
                    exp_done[(qt, kc)] = s_act.n

            def emit_lsum(qt):
                par = qt % 2
                nkc = qt // 4 + 1
                w = [(s_act, exp_done[(qt, nkc - 1)])]
                if qt >= 2:
                    w.append((s_act, yev_done[qt - 2]))
                sch.add("vector", lambda e, par=par, nkc=nkc: e.reduce_sum(
                    rinv[par][:, :], ls[par][:, 0:nkc], axis=AX.X), waits=w)
                sch.add("vector", lambda e, par=par: e.reciprocal(
                    rinv[par][:, :], rinv[par][:, :]), drain=True,
                    incs=[(s_dve, 1)])
                rinv_done[qt] = s_dve.n

            ptbank_free = {}

            def emit_T(qt):
                par = qt % 2
                nkc = qt // 4 + 1
                for kc in range(nkc):
                    bid = kc % 2
                    bank = p_PT[bid]
                    w = [(s_act, exp_done[(qt, kc)])]
                    if bid in ptbank_free:
                        w.append(ptbank_free[bid])
                    for b4 in range(4):
                        sch.add("tensor", lambda e, par=par, kc=kc, b4=b4, bank=bank:
                                e.matmul(bank[:, b4 * 128:(b4 + 1) * 128],
                                         Pst[par][:, kc * 512 + b4 * 128:
                                                   kc * 512 + (b4 + 1) * 128],
                                         ident[:, :], start=True, stop=True),
                                waits=w if b4 == 0 else [],
                                incs=[(s_pe, 1)] if b4 == 3 else [])
                    t_done[(qt, kc)] = s_pe.n
                    sch.add("vector", lambda e, par=par, kc=kc, bank=bank:
                            e.tensor_copy(PTst[par][:, kc * 512:(kc + 1) * 512],
                                          bank[:, :]),
                            waits=[(s_pe, t_done[(qt, kc)])], incs=[(s_dve, 1)])
                    ptv_done[(qt, kc)] = s_dve.n
                    ptbank_free[bid] = (s_dve, s_dve.n)

            def emit_av(qt):
                par = qt % 2
                nkc = qt // 4 + 1
                first = True
                for kc in range(nkc):
                    w = [(s_dve, ptv_done[(qt, kc)])]
                    if first and qt >= 2:
                        w.append((s_act, yev_done[qt - 2]))
                    for b4 in range(4):
                        blk_i = kc * 4 + b4
                        last = (kc == nkc - 1 and b4 == 3)
                        sch.add("tensor", lambda e, par=par, blk_i=blk_i,
                                st_=first, last=last: e.matmul(
                                    p_y[par][:, :],
                                    PTst[par][:, blk_i * 128:(blk_i + 1) * 128],
                                    v_sb[:, blk_i * 128:(blk_i + 1) * 128],
                                    start=st_, stop=last),
                                waits=w if b4 == 0 else [],
                                incs=[(s_pe, 1)] if last else [])
                        first = False
                    w = []
                av_done[qt] = s_pe.n

            def emit_yevac(qt):
                par = qt % 2
                sch.add("scalar", lambda e, par=par, qt=qt: e.mul(
                    out=y_h[:, qt * 128:(qt + 1) * 128], in_=p_y[par][:, :],
                    mul=rinv[par][:, :]),
                    waits=[(s_pe, av_done[qt]), (s_dve, rinv_done[qt])],
                    incs=[(s_act, 1)])
                yev_done[qt] = s_act.n

            emit_Sphase(0)
            for qt in range(NQT):
                if qt + 1 < NQT:
                    emit_Sphase(qt + 1)
                emit_exp(qt)
                emit_lsum(qt)
                emit_T(qt)
                emit_av(qt)
                emit_yevac(qt)

            sch.add("sync", lambda e: e.dma_start(
                out=d_yb.ap().rearrange("(i p) d -> p i d", p=128),
                in_=y_h[:, :].rearrange("p (i d) -> p i d", i=NQT)),
                waits=[(s_act, yev_done[NQT - 1])], incs=[(s_yb, 16)])
            YB_T = s_yb.n
            if DBG:
                gp_dma(d_dbg_q[:, :], qT[:, :], s_dbg, w=[(s_act, yev_done[NQT - 1])])
                gp_dma(d_dbg_k[:, :], kT[:, :], s_dbg)
                gp_dma(d_dbg_v[:, :], v_sb[:, :], s_dbg)
                gp_dma(d_dbg_yh[:, :], y_h[:, :], s_dbg)
                sch.add("gpsimd", None, waits=[(s_dbg, s_dbg.n)])

            with nc.Block() as blk:
                sch.flush(blk)
        scA.close()

        p2b = ExitStack()
        with p2b:
            def sb2b(name, shape, dt):
                return p2b.enter_context(nc.sbuf_tensor(name, shape, dt))

            wo_sb = sb2b("wo_sb", [128, 4 * C], BF16)
            bo_b = sb2b("bo_b", [128, C], F32)
            bo_row = bo_b[0:1, :]
            yown = sb2b("yown", [128, 4 * TOK], BF16)
            yT = sb2b("yT", [128, 4 * TOK], BF16)
            tmpo = sb2b("tmpo", [128, C], F32)

            p_bo = p2b.enter_context(nc.psum_tensor("p_bo", [128, 512], F32))
            p_yt = [p2b.enter_context(nc.psum_tensor(f"p_yt{i}", [128, 128], F32))
                    for i in range(2)]
            p_op = p2b.enter_context(nc.psum_tensor("p_op", [128, C], F32))

            sch = Sched()
            gp_dma(wo_sb[:, :].rearrange("p (c d) -> p c d", c=4),
                   d_wo.rearrange("(c p) d -> p c d", p=128), s_cpl)
            CPL_T4 = s_cpl.n
            sp_dma(bo_row, d_bo[None, :], s_csp)
            CSP_T4 = s_csp.n

            sch.add("gpsimd", lambda e: e.collective_compute(
                "AllGather", ALU.bypass,
                replica_groups=[[0, 1, 2, 3], [4, 5, 6, 7]],
                ins=[d_yb.ap().opt()], outs=[d_ybg.ap().opt()]),
                waits=[(s_yb, YB_T)], incs=[(s_cc, 1)])

            for col in range(32):
                sch.add("gpsimd", lambda e, col=col: e.indirect_dma_start(
                    out=yown[:, col * 128:(col + 1) * 128], out_offset=None,
                    in_=d_ybg.ap(),
                    in_offset=IndirectOffsetOnAxis(idxy[:, col:col + 1], 0)),
                    waits=[(s_cc, 1)] if col == 0 else [], incs=[(s_yr, 16)])
            YR_T = s_yr.n

            dve_bo = 0
            for nx in range(4):
                w = [(s_csp, CSP_T4)]
                if nx > 0:
                    w.append((s_dve, dve_bo))
                sch.add("tensor", lambda e, nx=nx: e.matmul(
                    p_bo[:, :], ones_sb[:, :], bo_b[0:1, nx * 512:(nx + 1) * 512],
                    start=True, stop=True), waits=w, incs=[(s_pe, 1)])
                pe_c = s_pe.n
                sch.add("vector", lambda e, nx=nx: e.tensor_copy(
                    bo_b[:, nx * 512:(nx + 1) * 512], p_bo[:, :]),
                    waits=[(s_pe, pe_c)], incs=[(s_dve, 1)])
                dve_bo = s_dve.n

            yt_evac = {}
            for col in range(32):
                bank = p_yt[col % 2]
                w = [(s_yr, YR_T)]
                if col >= 2:
                    w.append((s_dve, yt_evac[col - 2]))
                hh, tt = col // NTT, col % NTT
                sch.add("tensor", lambda e, col=col, bank=bank: e.matmul(
                    bank[:, :], yown[:, col * 128:(col + 1) * 128], ident[:, :],
                    start=True, stop=True), waits=w, incs=[(s_pe, 1)])
                pe_c = s_pe.n
                sch.add("vector", lambda e, hh=hh, tt=tt, bank=bank: e.tensor_copy(
                    yT[:, hh * TOK + tt * 128: hh * TOK + (tt + 1) * 128],
                    bank[:, :]), waits=[(s_pe, pe_c)], incs=[(s_dve, 1)])
                yt_evac[col] = s_dve.n

            tmpo_done = {}
            x2_done = {}
            for tt in range(NTT):
                w = [(s_dve, yt_evac[3 * NTT + tt]), (s_cpl, CPL_T4)]
                if tt >= 1:
                    w.append((s_dve, tmpo_done[tt - 1]))
                for hh in range(4):
                    for nx in range(4):
                        sch.add("tensor", lambda e, hh=hh, tt=tt, nx=nx: e.matmul(
                            p_op[:, nx * 512:(nx + 1) * 512],
                            yT[:, hh * TOK + tt * 128: hh * TOK + (tt + 1) * 128],
                            wo_sb[:, hh * C + nx * 512: hh * C + (nx + 1) * 512],
                            start=(hh == 0), stop=(hh == 3)),
                            waits=w if (hh == 0 and nx == 0) else [],
                            incs=[(s_pe, 1)] if (hh == 3 and nx == 3) else [])
                pe_op = s_pe.n
                sch.add("vector", lambda e: e.tensor_add(
                    tmpo[:, :], p_op[:, :], bo_b[:, :]),
                    waits=[(s_pe, pe_op), (s_dve, dve_bo), (s_xsv, XSV_T)],
                    drain=True, incs=[(s_dve, 1)])
                tmpo_done[tt] = s_dve.n
                sch.add("vector", lambda e, tt=tt: e.tensor_add(
                    xsave[:, tt * C:(tt + 1) * C], xsave[:, tt * C:(tt + 1) * C],
                    tmpo[:, :]), drain=True, incs=[(s_dve, 1)])
                x2_done[tt] = s_dve.n
            X2_ALL = s_dve.n
            if DBG:
                sp_dma(d_dbg_x2[:, :], xsave[:, :], s_dbg2, w=[(s_dve, X2_ALL)])
                sch.add("sync", None, waits=[(s_dbg2, s_dbg2.n)])

            with nc.Block() as blk:
                sch.flush(blk)

        up_buf = octx.enter_context(nc.sbuf_tensor("up_buf", [128, KC * FI], BF16, side="right"))
        dn_buf = octx.enter_context(nc.sbuf_tensor("dn_buf", [128, 6 * C], BF16, side="right"))
        xln2T = octx.enter_context(nc.sbuf_tensor("xln2T", [128, KC * TOK], BF16, side="right"))
        p3 = ExitStack()
        with p3:
            def sb3(name, shape, dt):
                return p3.enter_context(nc.sbuf_tensor(name, shape, dt))

            xsq2 = sb3("xsq2", [128, C], F32)
            xln2 = [sb3(f"xln2{i}", [128, C], BF16) for i in range(2)]
            stat2 = sb3("stat2", [128, 8], F32)
            pgate = sb3("pgate", [128, NE], F32)
            m8 = sb3("m8", [128, 8], F32)
            prm = sb3("prm", [128, NE], F32)
            ptop = sb3("ptop", [128, NE], F32)
            wtmp = sb3("wtmp", [128, NE], F32)
            nmx2 = sb3("nmx2", [128, 2], F32)
            den = sb3("den", [128, 1], F32)

            p_t3 = [p3.enter_context(nc.psum_tensor(f"p_t3{i}", [128, 512], F32))
                    for i in range(2)]
            p_gl = p3.enter_context(nc.psum_tensor("p_gl", [128, NE], F32))

            sch = Sched()

            for c in range(KC):
                sch.add("gpsimd", lambda e, c=c: e.indirect_dma_start(
                    out=up_buf[:, c * FI:(c + 1) * FI], out_offset=None,
                    in_=gup_flat,
                    in_offset=IndirectOffsetOnAxis(idxgu[:, c:c + 1], 0),
                    element_offset=FI),
                    incs=[(s_up, 16)])
            for c in range(6):
                sch.add("gpsimd", lambda e, c=c: e.indirect_dma_start(
                    out=dn_buf[:, c * C:(c + 1) * C], out_offset=None,
                    in_=dnt_flat,
                    in_offset=IndirectOffsetOnAxis(idxdn[:, c:c + 1], 0)),
                    incs=[(s_dn, 16)])

            apply2 = {}
            t3_last = {}
            xl2T_evac = {}
            for tt in range(NTT):
                xt = xsave[:, tt * C:(tt + 1) * C]
                sch.add("vector", lambda e, xt=xt: e.reduce_sum(
                    stat2[:, 0:1], xt, axis=AX.X, negate=True),
                    waits=[(s_dve, X2_ALL)])
                sch.add("scalar", lambda e, xt=xt: e.activation(
                    out=xsq2[:, :], in_=xt, func=AF.Square,
                    accum_out=stat2[:, 1:2]), waits=[(s_dve, X2_ALL)],
                    incs=[(s_act, 1)])
                a_sq = s_act.n
                sch.add("vector", lambda e: e.tensor_scalar_mul(
                    stat2[:, 2:3], stat2[:, 0:1], INV2048), drain=True)
                sch.add("vector", lambda e: e.tensor_scalar_mul(
                    stat2[:, 3:4], stat2[:, 1:2], INV2048), waits=[(s_act, a_sq)])
                sch.add("vector", lambda e: e.tensor_mul(
                    stat2[:, 4:5], stat2[:, 2:3], stat2[:, 2:3]), drain=True)
                sch.add("vector", lambda e: e.tensor_sub(
                    stat2[:, 5:6], stat2[:, 3:4], stat2[:, 4:5]), drain=True,
                    incs=[(s_dve, 1)])
                d_var = s_dve.n
                sch.add("scalar", lambda e: e.activation(
                    out=stat2[:, 6:7], in_=stat2[:, 5:6], func=AF.Ln,
                    bias=eps_col[:, :]), waits=[(s_dve, d_var)])
                sch.add("scalar", lambda e: e.activation(
                    out=stat2[:, 6:7], in_=stat2[:, 6:7], func=AF.Exp,
                    scale=nh_col[:, :]), drain=True, incs=[(s_act, 1)])
                a_rstd = s_act.n
                sch.add("vector", lambda e: e.tensor_mul(
                    stat2[:, 7:8], stat2[:, 2:3], stat2[:, 6:7]),
                    waits=[(s_act, a_rstd)], drain=True, incs=[(s_dve, 1)])
                d_nmr = s_dve.n
                xo = xln2[tt % 2]
                wap2 = [(s_dve, d_nmr)]
                if tt >= 2:
                    wap2.append((s_pe, t3_last[tt - 2]))
                sch.add("scalar", lambda e, xt=xt, xo=xo: e.activation(
                    out=xo[:, :], in_=xt, func=AF.Identity,
                    bias=stat2[:, 7:8], scale=stat2[:, 6:7]),
                    waits=wap2, drain=True, incs=[(s_act, 1)])
                apply2[tt] = s_act.n

                for fq in range(4):
                    bank = p_t3[fq % 2]
                    w = [(s_act, apply2[tt])]
                    pidx = tt * 4 + fq - 2
                    if pidx >= 0:
                        w.append((s_dve, xl2T_evac[pidx]))
                    for fs in range(4):
                        fc = fq * 4 + fs
                        sch.add("tensor", lambda e, xo=xo, fc=fc, fs=fs, bank=bank:
                                e.matmul(bank[:, fs * 128:(fs + 1) * 128],
                                         xo[:, fc * 128:(fc + 1) * 128],
                                         ident[:, :], start=True, stop=True),
                                waits=w if fs == 0 else [],
                                incs=[(s_pe, 1)] if fs == 3 else [])
                    pe_c = s_pe.n

                    def ev2(e, bank=bank, fq=fq, tt=tt):
                        r = None
                        for fs in range(4):
                            fc = fq * 4 + fs
                            r = e.tensor_copy(
                                xln2T[:, fc * TOK + tt * 128: fc * TOK + (tt + 1) * 128],
                                bank[:, fs * 128:(fs + 1) * 128])
                        return r
                    sch.add("vector", ev2, waits=[(s_pe, pe_c)], incs=[(s_dve, 1)])
                    xl2T_evac[tt * 4 + fq] = s_dve.n
                t3_last[tt] = s_pe.n
            XL2T_ALL = s_dve.n

            a_pg_prev = 0
            d_sub_prev = 0
            for tt in range(NTT):
                w = [(s_dve, xl2T_evac[tt * 4 + 3]), (s_cpl, s_cpl.n)]
                if a_pg_prev:
                    w.append((s_act, a_pg_prev))
                for kc in range(KC):
                    sch.add("tensor", lambda e, kc=kc, tt=tt: e.matmul(
                        p_gl[:, :],
                        xln2T[:, kc * TOK + tt * 128: kc * TOK + (tt + 1) * 128],
                        gwt_sb[:, kc * NE:(kc + 1) * NE],
                        start=(kc == 0), stop=(kc == KC - 1)),
                        waits=w if kc == 0 else [],
                        incs=[(s_pe, 1)] if kc == KC - 1 else [])
                pe_g = s_pe.n
                par = tt % 2
                sch.add("vector", lambda e, par=par: e.reduce_max(
                    nmx2[:, par:par + 1], p_gl[:, :], axis=AX.X, negate=True),
                    waits=[(s_pe, pe_g)], incs=[(s_dve, 1)])
                d_nm = s_dve.n
                w = [(s_dve, d_nm)]
                if d_sub_prev:
                    w.append((s_dve, d_sub_prev))
                sch.add("scalar", lambda e, par=par: e.activation(
                    out=pgate[:, :], in_=p_gl[:, :], func=AF.Exp,
                    bias=nmx2[:, par:par + 1]), waits=w, incs=[(s_act, 1)])
                a_pg = s_act.n
                a_pg_prev = a_pg
                sch.add("vector", lambda e: e.max(out=m8[:, :], in_=pgate[:, :]),
                        waits=[(s_act, a_pg)], drain=True)
                sch.add("vector", lambda e: e.match_replace(
                    out=prm[:, :], in_to_replace=m8[:, :], in_values=pgate[:, :],
                    imm_value=0.0), drain=True)
                sch.add("vector", lambda e: e.tensor_sub(
                    ptop[:, :], pgate[:, :], prm[:, :]), drain=True,
                    incs=[(s_dve, 1)])
                d_sub_prev = s_dve.n
                sch.add("vector", lambda e: e.reduce_sum(
                    den[:, :], ptop[:, :], axis=AX.X), drain=True)
                sch.add("vector", lambda e: e.reciprocal(den[:, :], den[:, :]),
                        drain=True)
                for g in range(NG):
                    sch.add("vector", lambda e, g=g: e.tensor_mul(
                        wtmp[:, :], ptop[:, :], og[:, g * NE:(g + 1) * NE]),
                        drain=True)
                    sch.add("vector", lambda e, g=g, tt=tt: e.reduce_sum(
                        w_sb[:, tt * NG + g: tt * NG + g + 1], wtmp[:, :],
                        axis=AX.X), drain=True)
                sch.add("vector", lambda e, tt=tt: e.tensor_scalar(
                    out=w_sb[:, tt * NG:(tt + 1) * NG],
                    in0=w_sb[:, tt * NG:(tt + 1) * NG], scalar1=den[:, :],
                    scalar2=None, op0=ALU.mult), drain=True, incs=[(s_dve, 1)])
            GATE_ALL = s_dve.n
            if DBG:
                sp_dma(d_dbg_w[:, :], w_sb[:, :], s_dbg2, w=[(s_dve, GATE_ALL)])
                sch.add("sync", None, waits=[(s_dbg2, s_dbg2.n)])

            with nc.Block() as blk:
                sch.flush(blk)

        p4 = ExitStack()
        with p4:
            def sb4(name, shape, dt):
                return p4.enter_context(nc.sbuf_tensor(name, shape, dt))

            sil_sb = sb4("sil_sb", [128, NTT * FI], BF16)
            act_sb = [sb4(f"act{i}", [128, FI], BF16) for i in range(2)]
            actT = [sb4(f"actT{i}", [128, FI], BF16) for i in range(2)]

            p_h = p4.enter_context(nc.psum_tensor("p_h", [128, FI], F32))
            p_at = p4.enter_context(nc.psum_tensor("p_at", [128, FI], F32))
            p_dn = p4.enter_context(nc.psum_tensor("p_dn", [128, C], F32))

            sch = Sched()

            sil_done = {}
            stt_done = {}
            att_done = {}
            atev_done = {}
            dn_done = {}
            yadd_done = {}
            gate_mm = {}
            up_mm = {}

            for g in range(NG):
                for tt in range(NTT):
                    even = (tt % 2 == 0)
                    psum = p_h if even else p_dn
                    w = [(s_gate, 256 * (g + 1)), (s_dve, XL2T_ALL)]
                    if g > 0:
                        if even:
                            w.append((s_dve, stt_done[(g - 1, NTT - 1)]))
                        else:
                            w.append((s_dve, yadd_done[(g - 1, NTT - 1)]))
                    if tt >= 2:
                        w.append((s_act, sil_done[(g, tt - 2)]))
                    for kc in range(KC):
                        for (o0, nn) in ((0, 512), (512, 256)):
                            sch.add("tensor", lambda e, psum=psum, tt=tt, kc=kc,
                                    o0=o0, nn=nn: e.matmul(
                                        psum[:, o0:o0 + nn],
                                        xln2T[:, kc * TOK + tt * 128:
                                              kc * TOK + (tt + 1) * 128],
                                        gate_buf[:, kc * FI + o0: kc * FI + o0 + nn],
                                        start=(kc == 0), stop=(kc == KC - 1)),
                                    waits=w if (kc == 0 and o0 == 0) else [],
                                    incs=[(s_pe, 1)] if (kc == KC - 1 and o0 == 512)
                                    else [])
                    gate_mm[(g, tt)] = s_pe.n
                    sch.add("scalar", lambda e, psum=psum, tt=tt: e.activation(
                        out=sil_sb[:, tt * FI:(tt + 1) * FI], in_=psum[:, 0:FI],
                        func=AF.Silu), waits=[(s_pe, gate_mm[(g, tt)])],
                        incs=[(s_act, 1)])
                    sil_done[(g, tt)] = s_act.n
                if g + 1 < NG:
                    for c in range(KC):
                        sch.add("gpsimd", lambda e, g=g, c=c: e.indirect_dma_start(
                            out=gate_buf[:, c * FI:(c + 1) * FI], out_offset=None,
                            in_=gup_flat,
                            in_offset=IndirectOffsetOnAxis(
                                idxgu[:, (g + 1) * KC + c:(g + 1) * KC + c + 1], 0)),
                            waits=[(s_pe, gate_mm[(g, NTT - 1)])] if c == 0 else [],
                            incs=[(s_gate, 16)])

                for tt in range(NTT):
                    w = [(s_up, 256 * (g + 1)), (s_act, sil_done[(g, NTT - 1)])]
                    if tt > 0:
                        w.append((s_dve, stt_done[(g, tt - 1)]))
                    elif g > 0:
                        w.append((s_dve, stt_done[(g - 1, NTT - 1)]))
                    for kc in range(KC):
                        for (o0, nn) in ((0, 512), (512, 256)):
                            sch.add("tensor", lambda e, tt=tt, kc=kc, o0=o0, nn=nn:
                                    e.matmul(
                                        p_h[:, o0:o0 + nn],
                                        xln2T[:, kc * TOK + tt * 128:
                                              kc * TOK + (tt + 1) * 128],
                                        up_buf[:, kc * FI + o0: kc * FI + o0 + nn],
                                        start=(kc == 0), stop=(kc == KC - 1)),
                                    waits=w if (kc == 0 and o0 == 0) else [],
                                    incs=[(s_pe, 1)] if (kc == KC - 1 and o0 == 512)
                                    else [])
                    up_mm[(g, tt)] = s_pe.n
                    ab = act_sb[tt % 2]
                    sch.add("vector", lambda e, tt=tt, g=g, ab=ab:
                            e.scalar_tensor_tensor(
                                out=ab[:, :], in0=sil_sb[:, tt * FI:(tt + 1) * FI],
                                scalar=w_sb[:, tt * NG + g: tt * NG + g + 1],
                                in1=p_h[:, :], op0=ALU.mult, op1=ALU.mult),
                            waits=[(s_pe, up_mm[(g, tt)]), (s_dve, GATE_ALL)],
                            drain=True, incs=[(s_dve, 1)])
                    stt_done[(g, tt)] = s_dve.n
                    w = [(s_dve, stt_done[(g, tt)])]
                    prev = (g, tt - 1) if tt > 0 else (g - 1, NTT - 1)
                    if prev in atev_done:
                        w.append((s_dve, atev_done[prev]))
                    for b6 in range(6):
                        sch.add("tensor", lambda e, ab=ab, b6=b6: e.matmul(
                            p_at[:, b6 * 128:(b6 + 1) * 128],
                            ab[:, b6 * 128:(b6 + 1) * 128], ident[:, :],
                            start=True, stop=True),
                            waits=w if b6 == 0 else [],
                            incs=[(s_pe, 1)] if b6 == 5 else [])
                    att_done[(g, tt)] = s_pe.n
                    aT = actT[tt % 2]
                    sch.add("vector", lambda e, aT=aT: e.tensor_copy(
                        aT[:, :], p_at[:, :]), waits=[(s_pe, att_done[(g, tt)])],
                        incs=[(s_dve, 1)])
                    atev_done[(g, tt)] = s_dve.n
                    w = [(s_dn, 96 * (g + 1)), (s_dve, atev_done[(g, tt)])]
                    prev = (g, tt - 1) if tt > 0 else (g - 1, NTT - 1)
                    if prev in yadd_done:
                        w.append((s_dve, yadd_done[prev]))
                    if tt == 0:
                        w.append((s_act, sil_done[(g, NTT - 1)]))
                    for ofc in range(4):
                        for ic in range(6):
                            sch.add("tensor", lambda e, aT=aT, ofc=ofc, ic=ic:
                                    e.matmul(
                                        p_dn[:, ofc * 512:(ofc + 1) * 512],
                                        aT[:, ic * 128:(ic + 1) * 128],
                                        dn_buf[:, ic * C + ofc * 512:
                                               ic * C + (ofc + 1) * 512],
                                        start=(ic == 0), stop=(ic == 5)),
                                    waits=w if (ofc == 0 and ic == 0) else [],
                                    incs=[(s_pe, 1)] if ic == 5 else [])
                        dn_done[(g, tt, ofc)] = s_pe.n
                    for ofc in range(4):
                        sch.add("vector", lambda e, tt=tt, ofc=ofc: e.tensor_add(
                            xsave[:, tt * C + ofc * 512: tt * C + (ofc + 1) * 512],
                            xsave[:, tt * C + ofc * 512: tt * C + (ofc + 1) * 512],
                            p_dn[:, ofc * 512:(ofc + 1) * 512]),
                            waits=[(s_pe, dn_done[(g, tt, ofc)])],
                            incs=[(s_dve, 1)] if ofc == 3 else [])
                    yadd_done[(g, tt)] = s_dve.n
                if g + 1 < NG:
                    for c in range(KC):
                        sch.add("gpsimd", lambda e, g=g, c=c: e.indirect_dma_start(
                            out=up_buf[:, c * FI:(c + 1) * FI], out_offset=None,
                            in_=gup_flat,
                            in_offset=IndirectOffsetOnAxis(
                                idxgu[:, (g + 1) * KC + c:(g + 1) * KC + c + 1], 0),
                            element_offset=FI),
                            waits=[(s_pe, up_mm[(g, NTT - 1)])] if c == 0 else [],
                            incs=[(s_up, 16)])
                    for c in range(6):
                        sch.add("gpsimd", lambda e, g=g, c=c: e.indirect_dma_start(
                            out=dn_buf[:, c * C:(c + 1) * C], out_offset=None,
                            in_=dnt_flat,
                            in_offset=IndirectOffsetOnAxis(
                                idxdn[:, (g + 1) * 6 + c:(g + 1) * 6 + c + 1], 0)),
                            waits=[(s_pe, dn_done[(g, NTT - 1, 3)])] if c == 0 else [],
                            incs=[(s_dn, 16)])

            for tt in range(NTT):
                sch.add("sync", lambda e, tt=tt: e.dma_start(
                    out=d_out[tt * 128:(tt + 1) * 128, :],
                    in_=xsave[:, tt * C:(tt + 1) * C]),
                    waits=[(s_dve, yadd_done[(NG - 1, tt)])], incs=[(s_out, 16)])
            sch.add("sync", None, waits=[(s_out, s_out.n)])

            with nc.Block() as blk:
                sch.flush(blk)

    return nc


def _host_constants(j):
    perm = np.concatenate([np.arange(0, 128, 2), np.arange(1, 128, 2)])
    ident = np.eye(128, dtype=np.float32).astype(ml_dtypes.bfloat16)
    r = np.arange(128)[:, None]
    c = np.arange(512)[None, :]
    masks = np.concatenate(
        [np.where(c <= r + 128 * m, np.float32(0.0), np.float32(MASKVAL)).astype(np.float32)
         for m in range(4)], axis=1)
    inv_freq = (1.0 / (10000.0 ** (np.arange(64, dtype=np.float32) * 2.0 / 128.0))
                ).astype(np.float32)
    emb = (np.arange(T, dtype=np.float32)[:, None] * inv_freq[None, :]).astype(np.float32)
    cosT = np.cos(emb).T.astype(np.float32)
    sinT = np.sin(emb).T.astype(np.float32)
    ct = np.ascontiguousarray(np.concatenate([cosT, cosT], axis=0))
    st = np.ascontiguousarray(np.concatenate([-sinT, sinT], axis=0))
    swap = np.zeros((128, 128), dtype=np.float32)
    swap[np.arange(128), (np.arange(128) + 64) % 128] = 1.0
    ones = np.ones((1, 128), dtype=np.float32)
    rbgu = (np.arange(128)[:, None] + 128 * np.arange(KC)[None, :]).astype(np.float32)
    rbdn = (np.arange(128)[:, None] + 128 * np.arange(6)[None, :]).astype(np.float32)
    hhtt = np.arange(32)
    rby = (np.arange(128)[:, None] +
           ((hhtt // NTT) * T + (hhtt % NTT) * 128)[None, :]).astype(np.float32)
    jc = np.full((1, 1), float(j * TOK), dtype=np.float32)
    return perm, ident, masks, ct, st, ones, rbgu, rbdn, rby, jc, swap


def kernel(L0_attn_x, L0_attn_Wq, L0_attn_bq, L0_attn_Wk, L0_attn_bk,
           L0_attn_Wv, L0_attn_bv, L0_attn_Wo, L0_attn_bo,
           L0_ffn_gate_weight, L0_ffn_gate_up_proj, L0_ffn_down_proj,
           L0_ffn_merge_groups, L0_ffn_dominant_experts):
    x = np.asarray(L0_attn_x, dtype=np.float32)
    Wq = np.asarray(L0_attn_Wq, dtype=np.float32)
    bq = np.asarray(L0_attn_bq, dtype=np.float32)
    Wk = np.asarray(L0_attn_Wk, dtype=np.float32)
    bk = np.asarray(L0_attn_bk, dtype=np.float32)
    Wv = np.asarray(L0_attn_Wv, dtype=np.float32)
    bv = np.asarray(L0_attn_bv, dtype=np.float32)
    Wo = np.ascontiguousarray(np.asarray(L0_attn_Wo, dtype=np.float32))
    bo = np.ascontiguousarray(np.asarray(L0_attn_bo, dtype=np.float32))
    gw = np.asarray(L0_ffn_gate_weight, dtype=np.float32)
    gup = np.asarray(L0_ffn_gate_up_proj, dtype=np.float32)
    dnp = np.asarray(L0_ffn_down_proj, dtype=np.float32)
    mg = np.ascontiguousarray(np.asarray(L0_ffn_merge_groups).astype(np.int32))
    de = np.ascontiguousarray(np.asarray(L0_ffn_dominant_experts).astype(np.int32))

    gwt = np.ascontiguousarray(gw.T)
    gup_t = np.ascontiguousarray(gup.transpose(0, 2, 1))
    dnt = np.ascontiguousarray(dnp.transpose(0, 2, 1))

    nc = build_program()

    in_maps = []
    for core in range(8):
        b, h, j = core // 4, core % 4, core % 4
        (perm, ident, masks, ct, st, ones, rbgu, rbdn, rby, jc,
         swap) = _host_constants(j)
        qcols = np.concatenate([512 * h + 128 * g + perm for g in range(4)])
        kcols = 128 * h + perm
        im = {
            "x_b": np.ascontiguousarray(x[b]),
            "xown": np.ascontiguousarray(x[b, TOK * j:TOK * (j + 1)]),
            "wq": np.ascontiguousarray(Wq[:, qcols]),
            "bq": np.ascontiguousarray(bq[qcols]),
            "wk": np.ascontiguousarray(Wk[:, kcols]),
            "bk": np.ascontiguousarray(bk[kcols]),
            "wv": np.ascontiguousarray(Wv[:, 128 * h:128 * (h + 1)]),
            "bv": np.ascontiguousarray(bv[128 * h:128 * (h + 1)]),
            "wo": Wo, "bo": bo, "gwt": gwt, "gup": gup_t, "dnt": dnt,
            "mg": mg, "de": de,
            "identh": ident, "masks": masks, "ct": ct, "st": st,
            "onescol": ones, "rbgu": rbgu, "rbdn": rbdn, "rby": rby,
            "jconst": jc, "swaph": swap,
        }
        in_maps.append(im)

    res = run_bass_kernel_spmd(nc, in_maps, core_ids=list(range(8)))

    out = np.empty((B, T, C), dtype=np.float32)
    for core in range(8):
        b, j = core // 4, core % 4
        out[b, TOK * j:TOK * (j + 1)] = res.results[core]["out"]
    return out


# version 05
# speedup vs baseline: 1.1084x; 1.1084x over previous
import math
import numpy as np
import ml_dtypes
from contextlib import ExitStack

import concourse.bass as bass
import concourse.mybir as mybir
from concourse.bass import IndirectOffsetOnAxis
from concourse.bass_utils import run_bass_kernel_spmd

F32 = mybir.dt.float32
BF16 = mybir.dt.bfloat16
I32 = mybir.dt.int32
AF = mybir.ActivationFunctionType
ALU = mybir.AluOpType
AX = mybir.AxisListType

B, T, C = 2, 4096, 2048
D = 128
NKV = 4
CKV = 512
TOK = 1024
NTT = TOK // 128
NTG = T // 512
NQT = T // 128
KC = C // 128
NG = 8
NE = 32
FI = 768
MASKVAL = -3.0e38
INV2048 = 1.0 / 2048.0
SCQ = 1.0 / math.sqrt(128.0)

ENGS = ("tensor", "vector", "scalar", "gpsimd", "sync")


class Sem:
    def __init__(self, handle):
        self.h = handle
        self.n = 0


class Sched:

    def __init__(self):
        self.ops = {e: [] for e in ENGS}
        self.wmax = {e: {} for e in ENGS}

    def add(self, eng, fn=None, waits=(), incs=(), drain=False):
        wl = []
        for (sm, cnt) in waits:
            if cnt is None or cnt <= 0:
                continue
            key = id(sm)
            if cnt <= self.wmax[eng].get(key, 0):
                continue
            self.wmax[eng][key] = cnt
            wl.append((sm, cnt))
        for (sm, k) in incs:
            sm.n += k
        self.ops[eng].append((fn, wl, list(incs), drain))

    def flush(self, block):
        def mk(eng_name):
            def run(e):
                for (fn, wl, il, drain) in self.ops[eng_name]:
                    for (sm, cnt) in wl:
                        e.wait_ge(sm.h, cnt)
                    if drain:
                        e.drain()
                    inst = fn(e) if fn is not None else None
                    for (sm, k) in il:
                        if inst is None:
                            inst = e.nop()
                        inst.then_inc(sm.h, k)
                        inst = None
            return run

        for eng_name in ENGS:
            if self.ops[eng_name]:
                getattr(block, eng_name)(mk(eng_name))


def build_program():
    nc = bass.Bass("TRN2", target_bir_lowering=False, debug=False)

    d_x = nc.dram_tensor("x_b", [T, C], F32, kind="ExternalInput").ap()
    d_xown = nc.dram_tensor("xown", [TOK, C], F32, kind="ExternalInput").ap()
    d_wq = nc.dram_tensor("wq", [C, 512], F32, kind="ExternalInput").ap()
    d_bq = nc.dram_tensor("bq", [512], F32, kind="ExternalInput").ap()
    d_wk = nc.dram_tensor("wk", [C, 128], F32, kind="ExternalInput").ap()
    d_bk = nc.dram_tensor("bk", [128], F32, kind="ExternalInput").ap()
    d_wv = nc.dram_tensor("wv", [C, 128], F32, kind="ExternalInput").ap()
    d_bv = nc.dram_tensor("bv", [128], F32, kind="ExternalInput").ap()
    d_wo = nc.dram_tensor("wo", [CKV, C], F32, kind="ExternalInput").ap()
    d_bo = nc.dram_tensor("bo", [C], F32, kind="ExternalInput").ap()
    d_gwt = nc.dram_tensor("gwt", [C, NE], F32, kind="ExternalInput").ap()
    d_gup = nc.dram_tensor("gup", [NE, C, 2 * FI], F32, kind="ExternalInput").ap()
    d_dnt = nc.dram_tensor("dnt", [NE, FI, C], F32, kind="ExternalInput").ap()
    d_mg = nc.dram_tensor("mg", [NE], I32, kind="ExternalInput").ap()
    d_de = nc.dram_tensor("de", [NG], I32, kind="ExternalInput").ap()
    d_ident = nc.dram_tensor("identh", [128, 128], BF16, kind="ExternalInput").ap()
    d_masks = nc.dram_tensor("masks", [128, 4 * 512], F32, kind="ExternalInput").ap()
    d_ct = nc.dram_tensor("ct", [128, T], F32, kind="ExternalInput").ap()
    d_st = nc.dram_tensor("st", [128, T], F32, kind="ExternalInput").ap()
    d_ones = nc.dram_tensor("onescol", [1, 128], F32, kind="ExternalInput").ap()
    d_rbgu = nc.dram_tensor("rbgu", [128, KC], F32, kind="ExternalInput").ap()
    d_rbdn = nc.dram_tensor("rbdn", [128, 6], F32, kind="ExternalInput").ap()
    d_rby = nc.dram_tensor("rby", [128, 32], F32, kind="ExternalInput").ap()
    d_jc = nc.dram_tensor("jconst", [1, 1], F32, kind="ExternalInput").ap()
    d_swap = nc.dram_tensor("swaph", [128, 128], F32, kind="ExternalInput").ap()
    d_out = nc.dram_tensor("out", [TOK, C], F32, kind="ExternalOutput").ap()
    DBG = False
    if DBG:
        d_dbg_q = nc.dram_tensor("dbg_q", [128, T], F32, kind="ExternalOutput").ap()
        d_dbg_k = nc.dram_tensor("dbg_k", [128, T], F32, kind="ExternalOutput").ap()
        d_dbg_v = nc.dram_tensor("dbg_v", [128, T], F32, kind="ExternalOutput").ap()
        d_dbg_yh = nc.dram_tensor("dbg_yh", [128, T], F32, kind="ExternalOutput").ap()
        d_dbg_x2 = nc.dram_tensor("dbg_x2", [128, NTT * C], F32, kind="ExternalOutput").ap()
        d_dbg_w = nc.dram_tensor("dbg_w", [128, NTT * NG], F32, kind="ExternalOutput").ap()

    d_yb = nc.dram_tensor("ybounce", [T, D], BF16, kind="Internal")
    d_ybg = nc.dram_tensor("ybgather", [4 * T, D], BF16, kind="Internal")

    gup_flat = d_gup.rearrange("e c f -> (e c) f")
    dnt_flat = d_dnt.rearrange("e i c -> (e i) c")

    octx = ExitStack()
    with octx:
        def sb(name, shape, dt):
            return octx.enter_context(nc.sbuf_tensor(name, shape, dt))

        def sem(name):
            return Sem(octx.enter_context(nc.semaphore(name)))

        s_cpl = sem("s_cpl")
        s_csp = sem("s_csp")
        s_wq = sem("s_wq")
        s_xs = [sem(f"s_x{i}") for i in range(4)]
        s_tb = [sem(f"s_tb{i}") for i in range(2)]
        s_xsv = sem("s_xsv")
        s_pe = sem("s_pe")
        s_dve = sem("s_dve")
        s_act = sem("s_act")
        s_gate = sem("s_gate")
        s_up = sem("s_up")
        s_dn = sem("s_dn")
        s_yb = sem("s_yb")
        s_cc = sem("s_cc")
        s_yr = sem("s_yr")
        s_out = sem("s_out")
        s_dbg = sem("s_dbg")
        s_dbg2 = sem("s_dbg2")

        ident = sb("ident", [128, 128], BF16)
        swap_sb = sb("swap_sb", [128, 128], F32)
        bq_4 = sb("bq4", [128, 4], F32)
        bq_col = sb("bq_col", [128, 1], F32)
        bk_col = sb("bk_col", [128, 1], F32)
        bv_col = sb("bv_col", [128, 1], F32)
        sc_col = sb("sc_col", [128, 1], F32)
        eps_col = sb("eps_col", [128, 1], F32)
        nh_col = sb("nh_col", [128, 1], F32)
        ones_sb = sb("ones_sb", [1, 128], F32)
        def_sb = sb("def_sb", [1, NG], F32)
        de_sb = sb("de_sb", [1, NG], I32)
        mg_sb = sb("mg_sb", [1, NE], I32)
        mgf_sb = sb("mgf_sb", [1, NE], F32)
        de2048r = sb("de2048r", [1, NG], F32)
        de768r = sb("de768r", [1, NG], F32)
        de2048 = sb("de2048", [128, NG], F32)
        de768 = sb("de768", [128, NG], F32)
        mg_b = sb("mg_b", [128, NE], F32)
        og = sb("og", [128, NG * NE], F32)
        gcol = sb("gcol", [128, 1], F32)
        rbgu = sb("rbgu_s", [128, KC], F32)
        rbdn = sb("rbdn_s", [128, 6], F32)
        rby = sb("rby_s", [128, 32], F32)
        jc_sb = sb("jc_sb", [1, 1], F32)
        jcol = sb("jcol", [128, 1], F32)
        idxgu_f = sb("idxgu_f", [128, NG * KC], F32)
        idxgu = sb("idxgu", [128, NG * KC], I32)
        idxdn_f = sb("idxdn_f", [128, NG * 6], F32)
        idxdn = sb("idxdn", [128, NG * 6], I32)
        idxy_f = sb("idxy_f", [128, 32], F32)
        idxy = sb("idxy", [128, 32], I32)
        w_sb = sb("w_sb", [128, NTT * NG], F32)
        gwt_sb = sb("gwt_sb", [128, KC * NE], BF16)

        xsave = sb("xsave", [128, NTT * C], F32)

        sch = Sched()

        def gp_dma(out_ap, in_ap, s, w=()):
            sch.add("gpsimd", lambda e, o=out_ap, i=in_ap: e.dma_start(out=o, in_=i),
                    waits=w, incs=[(s, 16)])

        def sp_dma(out_ap, in_ap, s, w=(), slow=False):
            sch.add("sync", lambda e, o=out_ap, i=in_ap, slow=slow: e.dma_start(
                out=o, in_=i, allow_slow_non_contiguous=slow),
                waits=w, incs=[(s, 16)])

        sp_dma(ident[:, :], d_ident[:, :], s_csp)
        sp_dma(swap_sb[:, :], d_swap[:, :], s_csp)
        sp_dma(bq_4[:, :], d_bq.rearrange("(g d) -> d g", d=128), s_csp, slow=True)
        sp_dma(bk_col[:, :], d_bk[:, None], s_csp, slow=True)
        sp_dma(bv_col[:, :], d_bv[:, None], s_csp, slow=True)
        sp_dma(ones_sb[:, :], d_ones[:, :], s_csp)
        sp_dma(de_sb[:, :], d_de[None, :], s_csp)
        sp_dma(mg_sb[:, :], d_mg[None, :], s_csp)
        sp_dma(rbgu[:, :], d_rbgu[:, :], s_csp)
        sp_dma(rbdn[:, :], d_rbdn[:, :], s_csp)
        sp_dma(rby[:, :], d_rby[:, :], s_csp)
        sp_dma(jc_sb[:, :], d_jc[:, :], s_csp)
        CSP_T1 = s_csp.n
        gp_dma(gwt_sb[:, :].rearrange("p (c e) -> p c e", c=KC),
               d_gwt.rearrange("(c p) e -> p c e", p=128), s_cpl)

        sch.add("vector", lambda e: e.memset(sc_col[:, :], SCQ))
        sch.add("vector", lambda e: e.memset(eps_col[:, :], 1e-5))
        sch.add("vector", lambda e: e.memset(nh_col[:, :], -0.5))
        sch.add("vector", lambda e: e.reduce_sum(bq_col[:, :], bq_4[:, :], axis=AX.X),
                waits=[(s_csp, CSP_T1)])
        sch.add("vector", lambda e: e.tensor_scalar_mul(bq_col[:, :], bq_col[:, :], SCQ),
                drain=True)
        sch.add("vector", lambda e: e.tensor_copy(def_sb[:, :], de_sb[:, :]),
                waits=[(s_csp, CSP_T1)])
        sch.add("vector", lambda e: e.tensor_scalar_mul(de2048r[:, :], def_sb[:, :],
                                                        2048.0), drain=True)
        sch.add("vector", lambda e: e.tensor_scalar_mul(de768r[:, :], def_sb[:, :],
                                                        768.0))
        sch.add("vector", lambda e: e.tensor_copy(mgf_sb[:, :], mg_sb[:, :]),
                waits=[(s_csp, CSP_T1)], drain=True, incs=[(s_dve, 1)])
        dve_rows = s_dve.n

        init_ctx = ExitStack()
        p_bc = init_ctx.enter_context(nc.psum_tensor("p_bc", [128, 64], F32))
        sch.add("tensor", lambda e: e.matmul(p_bc[:, 0:NG], ones_sb[:, :],
                                             de2048r[:, :], start=True, stop=True),
                waits=[(s_dve, dve_rows)])
        sch.add("tensor", lambda e: e.matmul(p_bc[:, 8:8 + NG], ones_sb[:, :],
                                             de768r[:, :], start=True, stop=True))
        sch.add("tensor", lambda e: e.matmul(p_bc[:, 16:16 + NE], ones_sb[:, :],
                                             mgf_sb[:, :], start=True, stop=True))
        sch.add("tensor", lambda e: e.matmul(p_bc[:, 48:49], ones_sb[:, :],
                                             jc_sb[:, :], start=True, stop=True),
                incs=[(s_pe, 1)])
        pe_bc = s_pe.n
        sch.add("vector", lambda e: e.tensor_copy(de2048[:, :], p_bc[:, 0:NG]),
                waits=[(s_pe, pe_bc)])
        sch.add("vector", lambda e: e.tensor_copy(de768[:, :], p_bc[:, 8:8 + NG]))
        sch.add("vector", lambda e: e.tensor_copy(mg_b[:, :], p_bc[:, 16:16 + NE]))
        sch.add("vector", lambda e: e.tensor_copy(jcol[:, :], p_bc[:, 48:49]))
        for g in range(NG):
            sch.add("vector", lambda e, g=g: e.memset(gcol[:, :], float(g)),
                    drain=True)
            sch.add("vector", lambda e, g=g: e.tensor_scalar(
                out=og[:, g * NE:(g + 1) * NE], in0=mg_b[:, :], scalar1=gcol[:, :],
                scalar2=None, op0=ALU.is_equal), drain=True)
        for g in range(NG):
            sch.add("vector", lambda e, g=g: e.tensor_scalar(
                out=idxgu_f[:, g * KC:(g + 1) * KC], in0=rbgu[:, :],
                scalar1=de2048[:, g:g + 1], scalar2=None, op0=ALU.add),
                drain=(g == 0))
            sch.add("vector", lambda e, g=g: e.tensor_scalar(
                out=idxdn_f[:, g * 6:(g + 1) * 6], in0=rbdn[:, :],
                scalar1=de768[:, g:g + 1], scalar2=None, op0=ALU.add))
        sch.add("vector", lambda e: e.tensor_scalar(
            out=idxy_f[:, :], in0=rby[:, :], scalar1=jcol[:, :], scalar2=None,
            op0=ALU.add))
        sch.add("vector", lambda e: e.tensor_copy(idxgu[:, :], idxgu_f[:, :]),
                drain=True)
        sch.add("vector", lambda e: e.tensor_copy(idxdn[:, :], idxdn_f[:, :]))
        sch.add("vector", lambda e: e.tensor_copy(idxy[:, :], idxy_f[:, :]),
                incs=[(s_dve, 1)])
        IDX_DONE = s_dve.n

        with nc.Block() as blk:
            sch.flush(blk)
        init_ctx.close()

        scA = ExitStack()
        qT = scA.enter_context(nc.sbuf_tensor("qT", [128, T], BF16))
        kT = scA.enter_context(nc.sbuf_tensor("kT", [128, T], BF16))
        v_sb = scA.enter_context(nc.sbuf_tensor("v_sb", [128, T], BF16))
        p1 = ExitStack()
        with p1:
            def sb1(name, shape, dt):
                return p1.enter_context(nc.sbuf_tensor(name, shape, dt))

            scr2048 = sb1("scr2048", [128, C], F32)
            wqacc = scr2048
            xsq = scr2048
            wqeff = sb1("wqeff", [128, C], BF16)
            wk_sb = sb1("wk_sb", [128, C], BF16)
            wv_sb = sb1("wv_sb", [128, C], BF16)
            xring = [sb1(f"xr{i}", [128, C], F32) for i in range(4)]
            xln = [sb1(f"xln{i}", [128, C], BF16) for i in range(2)]
            xlnT = [sb1(f"xlnT{i}", [128, KC * 512], BF16) for i in range(2)]
            ct_r = [sb1(f"ct_r{i}", [128, 512], F32) for i in range(2)]
            st_r = [sb1(f"st_r{i}", [128, 512], F32) for i in range(2)]
            qf = sb1("qf", [128, 512], F32)
            kf = sb1("kf", [128, 512], F32)
            vtmp = sb1("vtmp", [128, 512], BF16)
            rt = [sb1(f"rt{i}", [128, 512], F32) for i in range(2)]
            stat = sb1("stat", [128, 8], F32)

            p_tp = [p1.enter_context(nc.psum_tensor(f"p_tp{i}", [128, 512], F32))
                    for i in range(2)]
            p_sw = [p1.enter_context(nc.psum_tensor(f"p_sw{i}", [128, 512], F32))
                    for i in range(2)]
            p_q = p1.enter_context(nc.psum_tensor("p_q", [128, 512], F32))
            p_k = p1.enter_context(nc.psum_tensor("p_k", [128, 512], F32))
            p_v = p1.enter_context(nc.psum_tensor("p_v", [128, 512], F32))
            p_vt = p1.enter_context(nc.psum_tensor("p_vt", [128, 512], F32))

            sch = Sched()

            for g in range(4):
                src = d_wq[:, 128 * g:128 * (g + 1)].rearrange(
                    "(fc p) d -> p fc d", p=128)
                dst = wqacc[:, :].rearrange("p (fc d) -> p fc d", fc=KC)
                if g == 0:
                    sch.add("gpsimd", lambda e, s=src, dst=dst: e.dma_start(
                        out=dst, in_=s), incs=[(s_wq, 16)])
                else:
                    sch.add("gpsimd", lambda e, s=src, dst=dst: e.dma_start(
                        out=dst, in_=s, accum_op=ALU.add),
                        waits=[(s_wq, s_wq.n)], incs=[(s_wq, 16)])
            WQ_T = s_wq.n
            gp_dma(wk_sb[:, :].rearrange("p (fc d) -> p fc d", fc=KC),
                   d_wk.rearrange("(fc p) d -> p fc d", p=128), s_cpl)
            gp_dma(wv_sb[:, :].rearrange("p (fc d) -> p fc d", fc=KC),
                   d_wv.rearrange("(fc p) d -> p fc d", p=128), s_cpl)
            CPL_T2 = s_cpl.n
            CSP_T2 = s_csp.n
            for tt in range(NTT):
                sp_dma(xsave[:, tt * C:(tt + 1) * C],
                       d_xown[128 * tt:128 * (tt + 1), :], s_xsv)
            XSV_T = s_xsv.n

            sch.add("vector", lambda e: e.tensor_copy(wqeff[:, :], wqacc[:, :]),
                    waits=[(s_wq, WQ_T)], incs=[(s_dve, 1)])
            DVE_WQ = s_dve.n

            apply_done = {}
            swap_used = {}
            tp_last = {}
            xlnT_evac = {}
            qkv_mm = {}
            qkv_evac = {}
            v_evac = {}
            rope_done = {}
            xload = {}

            for tg in range(NTG):
                tslot = tg % 2
                wtb = [(s_dve, rope_done[tg - 2])] if tg >= 2 else []
                sch.add("sync", lambda e, tg=tg, tslot=tslot: e.dma_start(
                    out=ct_r[tslot][:, :], in_=d_ct[:, tg * 512:(tg + 1) * 512]),
                    waits=wtb, incs=[(s_tb[tslot], 16)])
                sch.add("sync", lambda e, tg=tg, tslot=tslot: e.dma_start(
                    out=st_r[tslot][:, :], in_=d_st[:, tg * 512:(tg + 1) * 512]),
                    incs=[(s_tb[tslot], 16)])
                tb_cnt = s_tb[tslot].n
                for i in range(4):
                    t = 4 * tg + i
                    slot = t % 4
                    w = [(s_act, apply_done[t - 4])] if t >= 4 else []
                    ssem = s_xs[slot]
                    sch.add("sync", lambda e, t=t, slot=slot: e.dma_start(
                        out=xring[slot][:, :], in_=d_x[128 * t:128 * (t + 1), :]),
                        waits=w, incs=[(ssem, 16)])
                    xload[t] = (ssem, ssem.n)

                for i in range(4):
                    t = 4 * tg + i
                    ssem, scnt = xload[t]
                    xt = xring[t % 4]
                    sch.add("vector", lambda e, xt=xt: e.reduce_sum(
                        stat[:, 0:1], xt[:, :], axis=AX.X, negate=True),
                        waits=[(ssem, scnt)])
                    sch.add("scalar", lambda e, xt=xt: e.activation(
                        out=xsq[:, :], in_=xt[:, :], func=AF.Square,
                        accum_out=stat[:, 1:2]),
                        waits=[(ssem, scnt), (s_dve, DVE_WQ)],
                        incs=[(s_act, 1)])
                    a_sq = s_act.n
                    sch.add("vector", lambda e: e.tensor_scalar_mul(
                        stat[:, 2:3], stat[:, 0:1], INV2048), drain=True)
                    sch.add("vector", lambda e: e.tensor_scalar_mul(
                        stat[:, 3:4], stat[:, 1:2], INV2048), waits=[(s_act, a_sq)])
                    sch.add("vector", lambda e: e.tensor_mul(
                        stat[:, 4:5], stat[:, 2:3], stat[:, 2:3]), drain=True)
                    sch.add("vector", lambda e: e.tensor_sub(
                        stat[:, 5:6], stat[:, 3:4], stat[:, 4:5]), drain=True,
                        incs=[(s_dve, 1)])
                    d_var = s_dve.n
                    sch.add("scalar", lambda e: e.activation(
                        out=stat[:, 6:7], in_=stat[:, 5:6], func=AF.Ln,
                        bias=eps_col[:, :]), waits=[(s_dve, d_var)])
                    sch.add("scalar", lambda e: e.activation(
                        out=stat[:, 6:7], in_=stat[:, 6:7], func=AF.Exp,
                        scale=nh_col[:, :]), drain=True, incs=[(s_act, 1)])
                    a_rstd = s_act.n
                    sch.add("vector", lambda e: e.tensor_mul(
                        stat[:, 7:8], stat[:, 2:3], stat[:, 6:7]),
                        waits=[(s_act, a_rstd)], drain=True, incs=[(s_dve, 1)])
                    d_nmr = s_dve.n
                    xo = xln[t % 2]
                    wap = [(s_dve, d_nmr)]
                    if t >= 2:
                        wap.append((s_pe, tp_last[t - 2]))
                    sch.add("scalar", lambda e, xt=xt, xo=xo: e.activation(
                        out=xo[:, :], in_=xt[:, :], func=AF.Identity,
                        bias=stat[:, 7:8], scale=stat[:, 6:7]),
                        waits=wap, drain=True, incs=[(s_act, 1)])
                    apply_done[t] = s_act.n

                    for fq in range(4):
                        bank = p_tp[(t * 4 + fq) % 2]
                        w = [(s_act, apply_done[t])]
                        pidx = t * 4 + fq - 2
                        if pidx >= 0:
                            w.append((s_dve, xlnT_evac[pidx]))
                        for fs in range(4):
                            fc = fq * 4 + fs
                            sch.add("tensor", lambda e, xo=xo, fc=fc, fs=fs, bank=bank:
                                    e.matmul(bank[:, fs * 128:(fs + 1) * 128],
                                             xo[:, fc * 128:(fc + 1) * 128],
                                             ident[:, :], start=True, stop=True),
                                    waits=w if fs == 0 else [],
                                    incs=[(s_pe, 1)] if fs == 3 else [])
                        pe_c = s_pe.n
                        xTd = xlnT[tg % 2]

                        def ev(e, xTd=xTd, bank=bank, fq=fq, i=i):
                            r = None
                            for fs in range(4):
                                fc = fq * 4 + fs
                                r = e.tensor_copy(
                                    xTd[:, fc * 512 + i * 128: fc * 512 + (i + 1) * 128],
                                    bank[:, fs * 128:(fs + 1) * 128])
                            return r
                        sch.add("vector", ev, waits=[(s_pe, pe_c)],
                                incs=[(s_dve, 1)])
                        xlnT_evac[t * 4 + fq] = s_dve.n
                    tp_last[t] = s_pe.n
                XLNT_TG = s_dve.n

                xTd = xlnT[tg % 2]
                w0 = [(s_dve, XLNT_TG), (s_dve, DVE_WQ), (s_cpl, CPL_T2)]
                if tg >= 1:
                    w0.append((s_act, qkv_evac[tg - 1]))
                for (psum, wmat, nm) in ((p_q, wqeff, "q"), (p_k, wk_sb, "k"),
                                         (p_v, wv_sb, "v")):
                    for kc in range(KC):
                        sch.add("tensor", lambda e, psum=psum, wmat=wmat, kc=kc,
                                xTd=xTd: e.matmul(
                                    psum[:, :], wmat[:, kc * 128:(kc + 1) * 128],
                                    xTd[:, kc * 512:(kc + 1) * 512],
                                    start=(kc == 0), stop=(kc == KC - 1)),
                                waits=w0 if kc == 0 else [],
                                incs=[(s_pe, 1)] if kc == KC - 1 else [])
                    qkv_mm[(tg, nm)] = s_pe.n
                    w0 = []

                wqe = [(s_pe, qkv_mm[(tg, "q")])]
                if tg >= 1:
                    wqe.append((s_dve, rope_done[tg - 1]))
                sch.add("scalar", lambda e: e.activation(
                    out=qf[:, :], in_=p_q[:, :], func=AF.Identity,
                    bias=bq_col[:, :], scale=sc_col[:, :]),
                    waits=wqe, incs=[(s_act, 1)])
                a_q = s_act.n
                sch.add("scalar", lambda e: e.activation(
                    out=kf[:, :], in_=p_k[:, :], func=AF.Identity, bias=bk_col[:, :]),
                    waits=[(s_pe, qkv_mm[(tg, "k")])], incs=[(s_act, 1)])
                a_k = s_act.n
                sch.add("scalar", lambda e: e.activation(
                    out=vtmp[:, :], in_=p_v[:, :], func=AF.Identity, bias=bv_col[:, :]),
                    waits=[(s_pe, qkv_mm[(tg, "v")])], incs=[(s_act, 1)])
                a_v = s_act.n
                qkv_evac[tg] = s_act.n

                w = [(s_act, a_v)]
                if tg >= 1:
                    w.append((s_dve, v_evac[tg - 1]))
                for i in range(4):
                    sch.add("tensor", lambda e, i=i: e.matmul(
                        p_vt[:, i * 128:(i + 1) * 128],
                        vtmp[:, i * 128:(i + 1) * 128], ident[:, :],
                        start=True, stop=True), waits=w if i == 0 else [],
                        incs=[(s_pe, 1)] if i == 3 else [])
                pe_vt = s_pe.n

                def ev_v(e, tg=tg):
                    r = None
                    for i in range(4):
                        tglob = 4 * tg + i
                        r = e.tensor_copy(v_sb[:, tglob * 128:(tglob + 1) * 128],
                                          p_vt[:, i * 128:(i + 1) * 128])
                    return r
                sch.add("vector", ev_v, waits=[(s_pe, pe_vt)], incs=[(s_dve, 1)])
                v_evac[tg] = s_dve.n

                cts = ct_r[tslot][:, :]
                sts = st_r[tslot][:, :]
                for si, (srcf, dst, a_cnt) in enumerate(((qf, qT, a_q),
                                                         (kf, kT, a_k))):
                    swb = p_sw[si]
                    w = [(s_act, a_cnt)]
                    if tg >= 1:
                        w.append((s_dve, swap_used[(tg - 1, si)]))
                    sch.add("tensor", lambda e, srcf=srcf, swb=swb: e.matmul(
                        swb[:, :], swap_sb[:, :], srcf[:, :],
                        start=True, stop=True), waits=w, incs=[(s_pe, 1)])
                    pe_sw = s_pe.n
                    sch.add("vector", lambda e, srcf=srcf, cts=cts: e.tensor_mul(
                        rt[0][:, :], srcf[:, :], cts),
                        waits=[(s_act, a_cnt), (s_tb[tslot], tb_cnt)], drain=True)
                    sch.add("vector", lambda e, swb=swb, sts=sts: e.tensor_mul(
                        rt[1][:, :], swb[:, :], sts), waits=[(s_pe, pe_sw)],
                        incs=[(s_dve, 1)])
                    swap_used[(tg, si)] = s_dve.n
                    sch.add("vector", lambda e, dst=dst, tg=tg: e.tensor_add(
                        dst[:, tg * 512:(tg + 1) * 512], rt[0][:, :], rt[1][:, :]),
                        drain=True, incs=[(s_dve, 1)])
                rope_done[tg] = s_dve.n

            with nc.Block() as blk:
                sch.flush(blk)

        gate_buf = octx.enter_context(nc.sbuf_tensor("gate_buf", [128, KC * FI], BF16, side="right"))
        p2 = ExitStack()
        with p2:
            def sb2(name, shape, dt):
                return p2.enter_context(nc.sbuf_tensor(name, shape, dt))

            y_h = sb2("y_h", [128, T], BF16)
            masks = sb2("masks_s", [128, 4 * 512], BF16)
            Sst = [sb2(f"Sst{i}", [128, T], F32) for i in range(2)]
            Pst = [sb2(f"Pst{i}", [128, T], BF16) for i in range(3)]
            PTst = [sb2(f"PTst{i}", [128, T], BF16) for i in range(2)]
            mx = [sb2(f"mx{i}", [128, 8], F32) for i in range(3)]
            nmf = [sb2(f"nmf{i}", [128, 1], F32) for i in range(3)]
            ls = [sb2(f"ls{i}", [128, 8], F32) for i in range(3)]
            rinv = [sb2(f"rinv{i}", [128, 1], F32) for i in range(3)]

            p_S = [p2.enter_context(nc.psum_tensor(f"p_S{i}", [128, 512], F32))
                   for i in range(4)]
            p_PT = [p2.enter_context(nc.psum_tensor(f"p_PT{i}", [128, 512], F32))
                    for i in range(2)]
            p_y = [p2.enter_context(nc.psum_tensor(f"p_y{i}", [128, 128], F32))
                   for i in range(2)]

            sch = Sched()
            gp_dma(masks[:, :], d_masks[:, :], s_cpl)
            CPL_MASKS = s_cpl.n

            for c in range(KC):
                sch.add("gpsimd", lambda e, c=c: e.indirect_dma_start(
                    out=gate_buf[:, c * FI:(c + 1) * FI], out_offset=None,
                    in_=gup_flat,
                    in_offset=IndirectOffsetOnAxis(idxgu[:, c:c + 1], 0)),
                    waits=[(s_dve, IDX_DONE)] if c == 0 else [],
                    incs=[(s_gate, 16)])

            sbank_free = {}
            s_done = {}
            mask_done = {}
            s_evac = {}
            fmax_done = {}
            exp_done = {}
            t_done = {}
            ptv_done = {}
            av_done = {}
            rinv_done = {}
            yev_done = {}

            def emit_Sphase(qt):
                par = qt % 3
                spar = qt % 2
                nkc = qt // 4 + 1
                for kc in range(nkc):
                    bid = (qt % 2) * 2 + (kc % 2)
                    bank = p_S[bid]
                    w = []
                    if bid in sbank_free:
                        w.extend(sbank_free[bid])
                    sch.add("tensor", lambda e, qt=qt, kc=kc, bank=bank: e.matmul(
                        bank[:, :], qT[:, qt * 128:(qt + 1) * 128],
                        kT[:, kc * 512:(kc + 1) * 512], start=True, stop=True),
                        waits=w, incs=[(s_pe, 1)])
                    s_done[(qt, kc)] = s_pe.n
                    dg = (kc == qt // 4)
                    if dg:
                        m = qt % 4
                        wdg = [(s_pe, s_done[(qt, kc)]), (s_cpl, CPL_MASKS)]
                        if qt >= 2:
                            wdg.append((s_act, exp_done[(qt - 2, qt // 4 - 0 if False else (qt - 2) // 4)]))
                        sch.add("vector", lambda e, bank=bank, m=m, kc=kc, par=spar:
                                e.tensor_add(Sst[par][:, kc * 512:(kc + 1) * 512],
                                             bank[:, :],
                                             masks[:, m * 512:(m + 1) * 512]),
                                waits=wdg, incs=[(s_dve, 1)])
                        dve_ev = s_dve.n
                        sbank_free[bid] = [(s_dve, dve_ev)]
                        sch.add("vector", lambda e, kc=kc, par=spar: e.reduce_max(
                            mx[qt % 3][:, kc:kc + 1],
                            Sst[par][:, kc * 512:(kc + 1) * 512], axis=AX.X),
                            drain=True, incs=[(s_dve, 1)])
                    else:
                        sch.add("vector", lambda e, bank=bank, kc=kc, par=par:
                                e.reduce_max(mx[par][:, kc:kc + 1], bank[:, :],
                                             axis=AX.X),
                                waits=[(s_pe, s_done[(qt, kc)])],
                                incs=[(s_dve, 1)])
                        dmx = s_dve.n
                        sch.add("scalar", lambda e, bank=bank, kc=kc, par=spar: e.copy(
                            out=Sst[par][:, kc * 512:(kc + 1) * 512], in_=bank[:, :]),
                            waits=[(s_pe, s_done[(qt, kc)]), (s_dve, dmx)],
                            incs=[(s_act, 1)])
                        s_evac[(qt, kc)] = s_act.n
                        sbank_free[bid] = [(s_act, s_act.n)]
                    mask_done[(qt, kc)] = s_dve.n
                sch.add("vector", lambda e, par=par, nkc=nkc: e.reduce_max(
                    nmf[par][:, :], mx[par][:, 0:nkc], axis=AX.X, negate=True),
                    drain=True, incs=[(s_dve, 1)])
                fmax_done[qt] = s_dve.n

            def emit_exp(qt):
                par = qt % 3
                spar = qt % 2
                nkc = qt // 4 + 1
                for kc in range(nkc):
                    sch.add("scalar", lambda e, kc=kc, par=par, spar=spar:
                            e.activation(
                                out=Pst[par][:, kc * 512:(kc + 1) * 512],
                                in_=Sst[spar][:, kc * 512:(kc + 1) * 512],
                                func=AF.Exp, bias=nmf[par][:, :],
                                accum_out=ls[par][:, kc:kc + 1]),
                            waits=[(s_dve, fmax_done[qt])], drain=(kc == 0),
                            incs=[(s_act, 1)])
                    exp_done[(qt, kc)] = s_act.n

            def emit_lsum(qt):
                par = qt % 3
                nkc = qt // 4 + 1
                w = [(s_act, exp_done[(qt, nkc - 1)])]
                if qt >= 3:
                    w.append((s_act, yev_done[qt - 3]))
                sch.add("vector", lambda e, par=par, nkc=nkc: e.reduce_sum(
                    rinv[par][:, :], ls[par][:, 0:nkc], axis=AX.X), waits=w)
                sch.add("vector", lambda e, par=par: e.reciprocal(
                    rinv[par][:, :], rinv[par][:, :]), drain=True,
                    incs=[(s_dve, 1)])
                rinv_done[qt] = s_dve.n

            ptbank_free = {}

            def emit_T(qt):
                par = qt % 3
                nkc = qt // 4 + 1
                for kc in range(nkc):
                    bid = kc % 2
                    bank = p_PT[bid]
                    w = [(s_act, exp_done[(qt, kc)])]
                    if bid in ptbank_free:
                        w.append(ptbank_free[bid])
                    for b4 in range(4):
                        sch.add("tensor", lambda e, par=par, kc=kc, b4=b4, bank=bank:
                                e.matmul(bank[:, b4 * 128:(b4 + 1) * 128],
                                         Pst[par][:, kc * 512 + b4 * 128:
                                                   kc * 512 + (b4 + 1) * 128],
                                         ident[:, :], start=True, stop=True),
                                waits=w if b4 == 0 else [],
                                incs=[(s_pe, 1)] if b4 == 3 else [])
                    t_done[(qt, kc)] = s_pe.n
                    sch.add("vector", lambda e, par=qt % 2, kc=kc, bank=bank:
                            e.tensor_copy(PTst[par][:, kc * 512:(kc + 1) * 512],
                                          bank[:, :]),
                            waits=[(s_pe, t_done[(qt, kc)])], incs=[(s_dve, 1)])
                    ptv_done[(qt, kc)] = s_dve.n
                    ptbank_free[bid] = (s_dve, s_dve.n)

            def emit_av(qt):
                par = qt % 3
                nkc = qt // 4 + 1
                first = True
                for kc in range(nkc):
                    w = [(s_dve, ptv_done[(qt, kc)])]
                    if first and qt >= 2:
                        w.append((s_act, yev_done[qt - 2]))
                    for b4 in range(4):
                        blk_i = kc * 4 + b4
                        last = (kc == nkc - 1 and b4 == 3)
                        sch.add("tensor", lambda e, par=par, ppar=qt % 2, blk_i=blk_i,
                                st_=first, last=last: e.matmul(
                                    p_y[ppar][:, :],
                                    PTst[ppar][:, blk_i * 128:(blk_i + 1) * 128],
                                    v_sb[:, blk_i * 128:(blk_i + 1) * 128],
                                    start=st_, stop=last),
                                waits=w if b4 == 0 else [],
                                incs=[(s_pe, 1)] if last else [])
                        first = False
                    w = []
                av_done[qt] = s_pe.n

            def emit_yevac(qt):
                par = qt % 3
                ppar = qt % 2
                sch.add("scalar", lambda e, par=par, ppar=ppar, qt=qt: e.mul(
                    out=y_h[:, qt * 128:(qt + 1) * 128], in_=p_y[ppar][:, :],
                    mul=rinv[par][:, :]),
                    waits=[(s_pe, av_done[qt]), (s_dve, rinv_done[qt])],
                    incs=[(s_act, 1)])
                yev_done[qt] = s_act.n

            emit_Sphase(0)
            emit_Sphase(1)
            emit_exp(0)
            emit_lsum(0)
            for qt in range(NQT):
                if qt + 2 < NQT:
                    emit_Sphase(qt + 2)
                if qt + 1 < NQT:
                    emit_exp(qt + 1)
                    emit_lsum(qt + 1)
                emit_T(qt)
                emit_av(qt)
                emit_yevac(qt)

            sch.add("sync", lambda e: e.dma_start(
                out=d_yb.ap().rearrange("(i p) d -> p i d", p=128),
                in_=y_h[:, :].rearrange("p (i d) -> p i d", i=NQT)),
                waits=[(s_act, yev_done[NQT - 1])], incs=[(s_yb, 16)])
            YB_T = s_yb.n
            if DBG:
                gp_dma(d_dbg_q[:, :], qT[:, :], s_dbg, w=[(s_act, yev_done[NQT - 1])])
                gp_dma(d_dbg_k[:, :], kT[:, :], s_dbg)
                gp_dma(d_dbg_v[:, :], v_sb[:, :], s_dbg)
                gp_dma(d_dbg_yh[:, :], y_h[:, :], s_dbg)
                sch.add("gpsimd", None, waits=[(s_dbg, s_dbg.n)])

            with nc.Block() as blk:
                sch.flush(blk)
        scA.close()

        p2b = ExitStack()
        with p2b:
            def sb2b(name, shape, dt):
                return p2b.enter_context(nc.sbuf_tensor(name, shape, dt))

            wo_sb = sb2b("wo_sb", [128, 4 * C], BF16)
            bo_b = sb2b("bo_b", [128, C], F32)
            bo_row = bo_b[0:1, :]
            yown = sb2b("yown", [128, 4 * TOK], BF16)
            yT = sb2b("yT", [128, 4 * TOK], BF16)
            tmpo = sb2b("tmpo", [128, C], F32)

            p_bo = p2b.enter_context(nc.psum_tensor("p_bo", [128, 512], F32))
            p_yt = [p2b.enter_context(nc.psum_tensor(f"p_yt{i}", [128, 128], F32))
                    for i in range(2)]
            p_op = p2b.enter_context(nc.psum_tensor("p_op", [128, C], F32))

            sch = Sched()
            gp_dma(wo_sb[:, :].rearrange("p (c d) -> p c d", c=4),
                   d_wo.rearrange("(c p) d -> p c d", p=128), s_cpl)
            CPL_T4 = s_cpl.n
            sp_dma(bo_row, d_bo[None, :], s_csp)
            CSP_T4 = s_csp.n

            sch.add("gpsimd", lambda e: e.collective_compute(
                "AllGather", ALU.bypass,
                replica_groups=[[0, 1, 2, 3], [4, 5, 6, 7]],
                ins=[d_yb.ap().opt()], outs=[d_ybg.ap().opt()]),
                waits=[(s_yb, YB_T)], incs=[(s_cc, 1)])

            for col in range(32):
                sch.add("gpsimd", lambda e, col=col: e.indirect_dma_start(
                    out=yown[:, col * 128:(col + 1) * 128], out_offset=None,
                    in_=d_ybg.ap(),
                    in_offset=IndirectOffsetOnAxis(idxy[:, col:col + 1], 0)),
                    waits=[(s_cc, 1)] if col == 0 else [], incs=[(s_yr, 16)])
            YR_T = s_yr.n

            dve_bo = 0
            for nx in range(4):
                w = [(s_csp, CSP_T4)]
                if nx > 0:
                    w.append((s_dve, dve_bo))
                sch.add("tensor", lambda e, nx=nx: e.matmul(
                    p_bo[:, :], ones_sb[:, :], bo_b[0:1, nx * 512:(nx + 1) * 512],
                    start=True, stop=True), waits=w, incs=[(s_pe, 1)])
                pe_c = s_pe.n
                sch.add("vector", lambda e, nx=nx: e.tensor_copy(
                    bo_b[:, nx * 512:(nx + 1) * 512], p_bo[:, :]),
                    waits=[(s_pe, pe_c)], incs=[(s_dve, 1)])
                dve_bo = s_dve.n

            yt_evac = {}
            for col in range(32):
                bank = p_yt[col % 2]
                w = [(s_yr, YR_T)]
                if col >= 2:
                    w.append((s_dve, yt_evac[col - 2]))
                hh, tt = col // NTT, col % NTT
                sch.add("tensor", lambda e, col=col, bank=bank: e.matmul(
                    bank[:, :], yown[:, col * 128:(col + 1) * 128], ident[:, :],
                    start=True, stop=True), waits=w, incs=[(s_pe, 1)])
                pe_c = s_pe.n
                sch.add("vector", lambda e, hh=hh, tt=tt, bank=bank: e.tensor_copy(
                    yT[:, hh * TOK + tt * 128: hh * TOK + (tt + 1) * 128],
                    bank[:, :]), waits=[(s_pe, pe_c)], incs=[(s_dve, 1)])
                yt_evac[col] = s_dve.n

            tmpo_done = {}
            x2_done = {}
            for tt in range(NTT):
                w = [(s_dve, yt_evac[3 * NTT + tt]), (s_cpl, CPL_T4)]
                if tt >= 1:
                    w.append((s_dve, tmpo_done[tt - 1]))
                for hh in range(4):
                    for nx in range(4):
                        sch.add("tensor", lambda e, hh=hh, tt=tt, nx=nx: e.matmul(
                            p_op[:, nx * 512:(nx + 1) * 512],
                            yT[:, hh * TOK + tt * 128: hh * TOK + (tt + 1) * 128],
                            wo_sb[:, hh * C + nx * 512: hh * C + (nx + 1) * 512],
                            start=(hh == 0), stop=(hh == 3)),
                            waits=w if (hh == 0 and nx == 0) else [],
                            incs=[(s_pe, 1)] if (hh == 3 and nx == 3) else [])
                pe_op = s_pe.n
                sch.add("vector", lambda e: e.tensor_add(
                    tmpo[:, :], p_op[:, :], bo_b[:, :]),
                    waits=[(s_pe, pe_op), (s_dve, dve_bo), (s_xsv, XSV_T)],
                    drain=True, incs=[(s_dve, 1)])
                tmpo_done[tt] = s_dve.n
                sch.add("vector", lambda e, tt=tt: e.tensor_add(
                    xsave[:, tt * C:(tt + 1) * C], xsave[:, tt * C:(tt + 1) * C],
                    tmpo[:, :]), drain=True, incs=[(s_dve, 1)])
                x2_done[tt] = s_dve.n
            X2_ALL = s_dve.n
            if DBG:
                sp_dma(d_dbg_x2[:, :], xsave[:, :], s_dbg2, w=[(s_dve, X2_ALL)])
                sch.add("sync", None, waits=[(s_dbg2, s_dbg2.n)])

            with nc.Block() as blk:
                sch.flush(blk)

        up_buf = octx.enter_context(nc.sbuf_tensor("up_buf", [128, KC * FI], BF16, side="right"))
        dn_buf = octx.enter_context(nc.sbuf_tensor("dn_buf", [128, 6 * C], BF16, side="right"))
        xln2T = octx.enter_context(nc.sbuf_tensor("xln2T", [128, KC * TOK], BF16, side="right"))
        p3 = ExitStack()
        with p3:
            def sb3(name, shape, dt):
                return p3.enter_context(nc.sbuf_tensor(name, shape, dt))

            xsq2 = sb3("xsq2", [128, C], F32)
            xln2 = [sb3(f"xln2{i}", [128, C], BF16) for i in range(2)]
            stat2 = sb3("stat2", [128, 8], F32)
            pgate = sb3("pgate", [128, NE], F32)
            m8 = sb3("m8", [128, 8], F32)
            prm = sb3("prm", [128, NE], F32)
            ptop = sb3("ptop", [128, NE], F32)
            wtmp = sb3("wtmp", [128, NE], F32)
            nmx2 = sb3("nmx2", [128, 2], F32)
            den = sb3("den", [128, 1], F32)

            p_t3 = [p3.enter_context(nc.psum_tensor(f"p_t3{i}", [128, 512], F32))
                    for i in range(2)]
            p_gl = p3.enter_context(nc.psum_tensor("p_gl", [128, NE], F32))

            sch = Sched()

            for c in range(KC):
                sch.add("gpsimd", lambda e, c=c: e.indirect_dma_start(
                    out=up_buf[:, c * FI:(c + 1) * FI], out_offset=None,
                    in_=gup_flat,
                    in_offset=IndirectOffsetOnAxis(idxgu[:, c:c + 1], 0),
                    element_offset=FI),
                    incs=[(s_up, 16)])
            for c in range(6):
                sch.add("gpsimd", lambda e, c=c: e.indirect_dma_start(
                    out=dn_buf[:, c * C:(c + 1) * C], out_offset=None,
                    in_=dnt_flat,
                    in_offset=IndirectOffsetOnAxis(idxdn[:, c:c + 1], 0)),
                    incs=[(s_dn, 16)])

            apply2 = {}
            t3_last = {}
            xl2T_evac = {}
            for tt in range(NTT):
                xt = xsave[:, tt * C:(tt + 1) * C]
                sch.add("vector", lambda e, xt=xt: e.reduce_sum(
                    stat2[:, 0:1], xt, axis=AX.X, negate=True),
                    waits=[(s_dve, X2_ALL)])
                sch.add("scalar", lambda e, xt=xt: e.activation(
                    out=xsq2[:, :], in_=xt, func=AF.Square,
                    accum_out=stat2[:, 1:2]), waits=[(s_dve, X2_ALL)],
                    incs=[(s_act, 1)])
                a_sq = s_act.n
                sch.add("vector", lambda e: e.tensor_scalar_mul(
                    stat2[:, 2:3], stat2[:, 0:1], INV2048), drain=True)
                sch.add("vector", lambda e: e.tensor_scalar_mul(
                    stat2[:, 3:4], stat2[:, 1:2], INV2048), waits=[(s_act, a_sq)])
                sch.add("vector", lambda e: e.tensor_mul(
                    stat2[:, 4:5], stat2[:, 2:3], stat2[:, 2:3]), drain=True)
                sch.add("vector", lambda e: e.tensor_sub(
                    stat2[:, 5:6], stat2[:, 3:4], stat2[:, 4:5]), drain=True,
                    incs=[(s_dve, 1)])
                d_var = s_dve.n
                sch.add("scalar", lambda e: e.activation(
                    out=stat2[:, 6:7], in_=stat2[:, 5:6], func=AF.Ln,
                    bias=eps_col[:, :]), waits=[(s_dve, d_var)])
                sch.add("scalar", lambda e: e.activation(
                    out=stat2[:, 6:7], in_=stat2[:, 6:7], func=AF.Exp,
                    scale=nh_col[:, :]), drain=True, incs=[(s_act, 1)])
                a_rstd = s_act.n
                sch.add("vector", lambda e: e.tensor_mul(
                    stat2[:, 7:8], stat2[:, 2:3], stat2[:, 6:7]),
                    waits=[(s_act, a_rstd)], drain=True, incs=[(s_dve, 1)])
                d_nmr = s_dve.n
                xo = xln2[tt % 2]
                wap2 = [(s_dve, d_nmr)]
                if tt >= 2:
                    wap2.append((s_pe, t3_last[tt - 2]))
                sch.add("scalar", lambda e, xt=xt, xo=xo: e.activation(
                    out=xo[:, :], in_=xt, func=AF.Identity,
                    bias=stat2[:, 7:8], scale=stat2[:, 6:7]),
                    waits=wap2, drain=True, incs=[(s_act, 1)])
                apply2[tt] = s_act.n

                for fq in range(4):
                    bank = p_t3[fq % 2]
                    w = [(s_act, apply2[tt])]
                    pidx = tt * 4 + fq - 2
                    if pidx >= 0:
                        w.append((s_dve, xl2T_evac[pidx]))
                    for fs in range(4):
                        fc = fq * 4 + fs
                        sch.add("tensor", lambda e, xo=xo, fc=fc, fs=fs, bank=bank:
                                e.matmul(bank[:, fs * 128:(fs + 1) * 128],
                                         xo[:, fc * 128:(fc + 1) * 128],
                                         ident[:, :], start=True, stop=True),
                                waits=w if fs == 0 else [],
                                incs=[(s_pe, 1)] if fs == 3 else [])
                    pe_c = s_pe.n

                    def ev2(e, bank=bank, fq=fq, tt=tt):
                        r = None
                        for fs in range(4):
                            fc = fq * 4 + fs
                            r = e.tensor_copy(
                                xln2T[:, fc * TOK + tt * 128: fc * TOK + (tt + 1) * 128],
                                bank[:, fs * 128:(fs + 1) * 128])
                        return r
                    sch.add("vector", ev2, waits=[(s_pe, pe_c)], incs=[(s_dve, 1)])
                    xl2T_evac[tt * 4 + fq] = s_dve.n
                t3_last[tt] = s_pe.n
            XL2T_ALL = s_dve.n

            a_pg_prev = 0
            d_sub_prev = 0
            for tt in range(NTT):
                w = [(s_dve, xl2T_evac[tt * 4 + 3]), (s_cpl, s_cpl.n)]
                if a_pg_prev:
                    w.append((s_act, a_pg_prev))
                for kc in range(KC):
                    sch.add("tensor", lambda e, kc=kc, tt=tt: e.matmul(
                        p_gl[:, :],
                        xln2T[:, kc * TOK + tt * 128: kc * TOK + (tt + 1) * 128],
                        gwt_sb[:, kc * NE:(kc + 1) * NE],
                        start=(kc == 0), stop=(kc == KC - 1)),
                        waits=w if kc == 0 else [],
                        incs=[(s_pe, 1)] if kc == KC - 1 else [])
                pe_g = s_pe.n
                par = tt % 2
                sch.add("vector", lambda e, par=par: e.reduce_max(
                    nmx2[:, par:par + 1], p_gl[:, :], axis=AX.X, negate=True),
                    waits=[(s_pe, pe_g)], incs=[(s_dve, 1)])
                d_nm = s_dve.n
                w = [(s_dve, d_nm)]
                if d_sub_prev:
                    w.append((s_dve, d_sub_prev))
                sch.add("scalar", lambda e, par=par: e.activation(
                    out=pgate[:, :], in_=p_gl[:, :], func=AF.Exp,
                    bias=nmx2[:, par:par + 1]), waits=w, incs=[(s_act, 1)])
                a_pg = s_act.n
                a_pg_prev = a_pg
                sch.add("vector", lambda e: e.max(out=m8[:, :], in_=pgate[:, :]),
                        waits=[(s_act, a_pg)], drain=True)
                sch.add("vector", lambda e: e.match_replace(
                    out=prm[:, :], in_to_replace=m8[:, :], in_values=pgate[:, :],
                    imm_value=0.0), drain=True)
                sch.add("vector", lambda e: e.tensor_sub(
                    ptop[:, :], pgate[:, :], prm[:, :]), drain=True,
                    incs=[(s_dve, 1)])
                d_sub_prev = s_dve.n
                sch.add("vector", lambda e: e.reduce_sum(
                    den[:, :], ptop[:, :], axis=AX.X), drain=True)
                sch.add("vector", lambda e: e.reciprocal(den[:, :], den[:, :]),
                        drain=True)
                for g in range(NG):
                    sch.add("vector", lambda e, g=g: e.tensor_mul(
                        wtmp[:, :], ptop[:, :], og[:, g * NE:(g + 1) * NE]),
                        drain=True)
                    sch.add("vector", lambda e, g=g, tt=tt: e.reduce_sum(
                        w_sb[:, tt * NG + g: tt * NG + g + 1], wtmp[:, :],
                        axis=AX.X), drain=True)
                sch.add("vector", lambda e, tt=tt: e.tensor_scalar(
                    out=w_sb[:, tt * NG:(tt + 1) * NG],
                    in0=w_sb[:, tt * NG:(tt + 1) * NG], scalar1=den[:, :],
                    scalar2=None, op0=ALU.mult), drain=True, incs=[(s_dve, 1)])
            GATE_ALL = s_dve.n
            if DBG:
                sp_dma(d_dbg_w[:, :], w_sb[:, :], s_dbg2, w=[(s_dve, GATE_ALL)])
                sch.add("sync", None, waits=[(s_dbg2, s_dbg2.n)])

            with nc.Block() as blk:
                sch.flush(blk)

        p4 = ExitStack()
        with p4:
            def sb4(name, shape, dt):
                return p4.enter_context(nc.sbuf_tensor(name, shape, dt))

            sil_sb = sb4("sil_sb", [128, NTT * FI], BF16)
            act_sb = [sb4(f"act{i}", [128, FI], BF16) for i in range(2)]
            actT = [sb4(f"actT{i}", [128, FI], BF16) for i in range(2)]

            p_h = p4.enter_context(nc.psum_tensor("p_h", [128, FI], F32))
            p_at = p4.enter_context(nc.psum_tensor("p_at", [128, FI], F32))
            p_dn = p4.enter_context(nc.psum_tensor("p_dn", [128, C], F32))

            sch = Sched()

            sil_done = {}
            stt_done = {}
            att_done = {}
            atev_done = {}
            dn_done = {}
            yadd_done = {}
            gate_mm = {}
            up_mm = {}

            for g in range(NG):
                for tt in range(NTT):
                    even = (tt % 2 == 0)
                    psum = p_h if even else p_dn
                    w = [(s_gate, 256 * (g + 1)), (s_dve, XL2T_ALL)]
                    if g > 0:
                        if even:
                            w.append((s_dve, stt_done[(g - 1, NTT - 1)]))
                        else:
                            w.append((s_dve, yadd_done[(g - 1, NTT - 1)]))
                    if tt >= 2:
                        w.append((s_act, sil_done[(g, tt - 2)]))
                    for kc in range(KC):
                        for (o0, nn) in ((0, 512), (512, 256)):
                            sch.add("tensor", lambda e, psum=psum, tt=tt, kc=kc,
                                    o0=o0, nn=nn: e.matmul(
                                        psum[:, o0:o0 + nn],
                                        xln2T[:, kc * TOK + tt * 128:
                                              kc * TOK + (tt + 1) * 128],
                                        gate_buf[:, kc * FI + o0: kc * FI + o0 + nn],
                                        start=(kc == 0), stop=(kc == KC - 1)),
                                    waits=w if (kc == 0 and o0 == 0) else [],
                                    incs=[(s_pe, 1)] if (kc == KC - 1 and o0 == 512)
                                    else [])
                    gate_mm[(g, tt)] = s_pe.n
                    sch.add("scalar", lambda e, psum=psum, tt=tt: e.activation(
                        out=sil_sb[:, tt * FI:(tt + 1) * FI], in_=psum[:, 0:FI],
                        func=AF.Silu), waits=[(s_pe, gate_mm[(g, tt)])],
                        incs=[(s_act, 1)])
                    sil_done[(g, tt)] = s_act.n
                if g + 1 < NG:
                    for c in range(KC):
                        sch.add("gpsimd", lambda e, g=g, c=c: e.indirect_dma_start(
                            out=gate_buf[:, c * FI:(c + 1) * FI], out_offset=None,
                            in_=gup_flat,
                            in_offset=IndirectOffsetOnAxis(
                                idxgu[:, (g + 1) * KC + c:(g + 1) * KC + c + 1], 0)),
                            waits=[(s_pe, gate_mm[(g, NTT - 1)])] if c == 0 else [],
                            incs=[(s_gate, 16)])

                def emit_up(tt):
                    w = [(s_up, 256 * (g + 1)), (s_act, sil_done[(g, NTT - 1)])]
                    if tt > 0:
                        w.append((s_dve, stt_done[(g, tt - 1)]))
                    elif g > 0:
                        w.append((s_dve, stt_done[(g - 1, NTT - 1)]))
                    for kc in range(KC):
                        for (o0, nn) in ((0, 512), (512, 256)):
                            sch.add("tensor", lambda e, tt=tt, kc=kc, o0=o0, nn=nn:
                                    e.matmul(
                                        p_h[:, o0:o0 + nn],
                                        xln2T[:, kc * TOK + tt * 128:
                                              kc * TOK + (tt + 1) * 128],
                                        up_buf[:, kc * FI + o0: kc * FI + o0 + nn],
                                        start=(kc == 0), stop=(kc == KC - 1)),
                                    waits=w if (kc == 0 and o0 == 0) else [],
                                    incs=[(s_pe, 1)] if (kc == KC - 1 and o0 == 512)
                                    else [])
                    up_mm[(g, tt)] = s_pe.n
                    ab = act_sb[tt % 2]
                    sch.add("vector", lambda e, tt=tt, g=g, ab=ab:
                            e.scalar_tensor_tensor(
                                out=ab[:, :], in0=sil_sb[:, tt * FI:(tt + 1) * FI],
                                scalar=w_sb[:, tt * NG + g: tt * NG + g + 1],
                                in1=p_h[:, :], op0=ALU.mult, op1=ALU.mult),
                            waits=[(s_pe, up_mm[(g, tt)]), (s_dve, GATE_ALL)],
                            drain=True, incs=[(s_dve, 1)])
                    stt_done[(g, tt)] = s_dve.n

                def emit_Tact(tt):
                    ab = act_sb[tt % 2]
                    w = [(s_dve, stt_done[(g, tt)])]
                    prev = (g, tt - 1) if tt > 0 else (g - 1, NTT - 1)
                    if prev in atev_done:
                        w.append((s_dve, atev_done[prev]))
                    for b6 in range(6):
                        sch.add("tensor", lambda e, ab=ab, b6=b6: e.matmul(
                            p_at[:, b6 * 128:(b6 + 1) * 128],
                            ab[:, b6 * 128:(b6 + 1) * 128], ident[:, :],
                            start=True, stop=True),
                            waits=w if b6 == 0 else [],
                            incs=[(s_pe, 1)] if b6 == 5 else [])
                    att_done[(g, tt)] = s_pe.n
                    aT = actT[tt % 2]
                    sch.add("vector", lambda e, aT=aT: e.tensor_copy(
                        aT[:, :], p_at[:, :]), waits=[(s_pe, att_done[(g, tt)])],
                        incs=[(s_dve, 1)])
                    atev_done[(g, tt)] = s_dve.n

                def emit_down(tt):
                    aT = actT[tt % 2]
                    w = [(s_dn, 96 * (g + 1)), (s_dve, atev_done[(g, tt)])]
                    prev = (g, tt - 1) if tt > 0 else (g - 1, NTT - 1)
                    if prev in yadd_done:
                        w.append((s_dve, yadd_done[prev]))
                    if tt == 0:
                        w.append((s_act, sil_done[(g, NTT - 1)]))
                    for ofc in range(4):
                        for ic in range(6):
                            sch.add("tensor", lambda e, aT=aT, ofc=ofc, ic=ic:
                                    e.matmul(
                                        p_dn[:, ofc * 512:(ofc + 1) * 512],
                                        aT[:, ic * 128:(ic + 1) * 128],
                                        dn_buf[:, ic * C + ofc * 512:
                                               ic * C + (ofc + 1) * 512],
                                        start=(ic == 0), stop=(ic == 5)),
                                    waits=w if (ofc == 0 and ic == 0) else [],
                                    incs=[(s_pe, 1)] if ic == 5 else [])
                        dn_done[(g, tt, ofc)] = s_pe.n
                    for ofc in range(4):
                        sch.add("vector", lambda e, tt=tt, ofc=ofc: e.tensor_add(
                            xsave[:, tt * C + ofc * 512: tt * C + (ofc + 1) * 512],
                            xsave[:, tt * C + ofc * 512: tt * C + (ofc + 1) * 512],
                            p_dn[:, ofc * 512:(ofc + 1) * 512]),
                            waits=[(s_pe, dn_done[(g, tt, ofc)])],
                            incs=[(s_dve, 1)] if ofc == 3 else [])
                    yadd_done[(g, tt)] = s_dve.n

                for tt in range(NTT):
                    emit_up(tt)
                    if tt >= 1:
                        emit_Tact(tt - 1)
                    if tt >= 2:
                        emit_down(tt - 2)
                emit_Tact(NTT - 1)
                emit_down(NTT - 2)
                emit_down(NTT - 1)
                if g + 1 < NG:
                    for c in range(KC):
                        sch.add("gpsimd", lambda e, g=g, c=c: e.indirect_dma_start(
                            out=up_buf[:, c * FI:(c + 1) * FI], out_offset=None,
                            in_=gup_flat,
                            in_offset=IndirectOffsetOnAxis(
                                idxgu[:, (g + 1) * KC + c:(g + 1) * KC + c + 1], 0),
                            element_offset=FI),
                            waits=[(s_pe, up_mm[(g, NTT - 1)])] if c == 0 else [],
                            incs=[(s_up, 16)])
                    for c in range(6):
                        sch.add("gpsimd", lambda e, g=g, c=c: e.indirect_dma_start(
                            out=dn_buf[:, c * C:(c + 1) * C], out_offset=None,
                            in_=dnt_flat,
                            in_offset=IndirectOffsetOnAxis(
                                idxdn[:, (g + 1) * 6 + c:(g + 1) * 6 + c + 1], 0)),
                            waits=[(s_pe, dn_done[(g, NTT - 1, 3)])] if c == 0 else [],
                            incs=[(s_dn, 16)])

            for tt in range(NTT):
                sch.add("sync", lambda e, tt=tt: e.dma_start(
                    out=d_out[tt * 128:(tt + 1) * 128, :],
                    in_=xsave[:, tt * C:(tt + 1) * C]),
                    waits=[(s_dve, yadd_done[(NG - 1, tt)])], incs=[(s_out, 16)])
            sch.add("sync", None, waits=[(s_out, s_out.n)])

            with nc.Block() as blk:
                sch.flush(blk)

    return nc


def _host_constants(j):
    perm = np.concatenate([np.arange(0, 128, 2), np.arange(1, 128, 2)])
    ident = np.eye(128, dtype=np.float32).astype(ml_dtypes.bfloat16)
    r = np.arange(128)[:, None]
    c = np.arange(512)[None, :]
    masks = np.concatenate(
        [np.where(c <= r + 128 * m, np.float32(0.0), np.float32(MASKVAL)).astype(np.float32)
         for m in range(4)], axis=1)
    inv_freq = (1.0 / (10000.0 ** (np.arange(64, dtype=np.float32) * 2.0 / 128.0))
                ).astype(np.float32)
    emb = (np.arange(T, dtype=np.float32)[:, None] * inv_freq[None, :]).astype(np.float32)
    cosT = np.cos(emb).T.astype(np.float32)
    sinT = np.sin(emb).T.astype(np.float32)
    ct = np.ascontiguousarray(np.concatenate([cosT, cosT], axis=0))
    st = np.ascontiguousarray(np.concatenate([-sinT, sinT], axis=0))
    swap = np.zeros((128, 128), dtype=np.float32)
    swap[np.arange(128), (np.arange(128) + 64) % 128] = 1.0
    ones = np.ones((1, 128), dtype=np.float32)
    rbgu = (np.arange(128)[:, None] + 128 * np.arange(KC)[None, :]).astype(np.float32)
    rbdn = (np.arange(128)[:, None] + 128 * np.arange(6)[None, :]).astype(np.float32)
    hhtt = np.arange(32)
    rby = (np.arange(128)[:, None] +
           ((hhtt // NTT) * T + (hhtt % NTT) * 128)[None, :]).astype(np.float32)
    jc = np.full((1, 1), float(j * TOK), dtype=np.float32)
    return perm, ident, masks, ct, st, ones, rbgu, rbdn, rby, jc, swap


def kernel(L0_attn_x, L0_attn_Wq, L0_attn_bq, L0_attn_Wk, L0_attn_bk,
           L0_attn_Wv, L0_attn_bv, L0_attn_Wo, L0_attn_bo,
           L0_ffn_gate_weight, L0_ffn_gate_up_proj, L0_ffn_down_proj,
           L0_ffn_merge_groups, L0_ffn_dominant_experts):
    x = np.asarray(L0_attn_x, dtype=np.float32)
    Wq = np.asarray(L0_attn_Wq, dtype=np.float32)
    bq = np.asarray(L0_attn_bq, dtype=np.float32)
    Wk = np.asarray(L0_attn_Wk, dtype=np.float32)
    bk = np.asarray(L0_attn_bk, dtype=np.float32)
    Wv = np.asarray(L0_attn_Wv, dtype=np.float32)
    bv = np.asarray(L0_attn_bv, dtype=np.float32)
    Wo = np.ascontiguousarray(np.asarray(L0_attn_Wo, dtype=np.float32))
    bo = np.ascontiguousarray(np.asarray(L0_attn_bo, dtype=np.float32))
    gw = np.asarray(L0_ffn_gate_weight, dtype=np.float32)
    gup = np.asarray(L0_ffn_gate_up_proj, dtype=np.float32)
    dnp = np.asarray(L0_ffn_down_proj, dtype=np.float32)
    mg = np.ascontiguousarray(np.asarray(L0_ffn_merge_groups).astype(np.int32))
    de = np.ascontiguousarray(np.asarray(L0_ffn_dominant_experts).astype(np.int32))

    gwt = np.ascontiguousarray(gw.T)
    gup_t = np.ascontiguousarray(gup.transpose(0, 2, 1))
    dnt = np.ascontiguousarray(dnp.transpose(0, 2, 1))

    nc = build_program()

    in_maps = []
    for core in range(8):
        b, h, j = core // 4, core % 4, core % 4
        (perm, ident, masks, ct, st, ones, rbgu, rbdn, rby, jc,
         swap) = _host_constants(j)
        qcols = np.concatenate([512 * h + 128 * g + perm for g in range(4)])
        kcols = 128 * h + perm
        im = {
            "x_b": np.ascontiguousarray(x[b]),
            "xown": np.ascontiguousarray(x[b, TOK * j:TOK * (j + 1)]),
            "wq": np.ascontiguousarray(Wq[:, qcols]),
            "bq": np.ascontiguousarray(bq[qcols]),
            "wk": np.ascontiguousarray(Wk[:, kcols]),
            "bk": np.ascontiguousarray(bk[kcols]),
            "wv": np.ascontiguousarray(Wv[:, 128 * h:128 * (h + 1)]),
            "bv": np.ascontiguousarray(bv[128 * h:128 * (h + 1)]),
            "wo": Wo, "bo": bo, "gwt": gwt, "gup": gup_t, "dnt": dnt,
            "mg": mg, "de": de,
            "identh": ident, "masks": masks, "ct": ct, "st": st,
            "onescol": ones, "rbgu": rbgu, "rbdn": rbdn, "rby": rby,
            "jconst": jc, "swaph": swap,
        }
        in_maps.append(im)

    res = run_bass_kernel_spmd(nc, in_maps, core_ids=list(range(8)))

    out = np.empty((B, T, C), dtype=np.float32)
    for core in range(8):
        b, j = core // 4, core % 4
        out[b, TOK * j:TOK * (j + 1)] = res.results[core]["out"]
    return out


# version 06
# speedup vs baseline: 1.2366x; 1.1156x over previous
import math
import numpy as np
import ml_dtypes
from contextlib import ExitStack

import concourse.bass as bass
import concourse.mybir as mybir
from concourse.bass import IndirectOffsetOnAxis
from concourse.bass_utils import run_bass_kernel_spmd

F32 = mybir.dt.float32
BF16 = mybir.dt.bfloat16
I32 = mybir.dt.int32
AF = mybir.ActivationFunctionType
ALU = mybir.AluOpType
AX = mybir.AxisListType

B, T, C = 2, 4096, 2048
D = 128
NKV = 4
CKV = 512
TOK = 1024
NTT = TOK // 128
NTG = T // 512
NQT = T // 128
KC = C // 128
NG = 8
NE = 32
FI = 768
MASKVAL = -3.0e38
INV2048 = 1.0 / 2048.0
SCQ = 1.0 / math.sqrt(128.0)

ENGS = ("tensor", "vector", "scalar", "gpsimd", "sync")


class Sem:
    def __init__(self, handle):
        self.h = handle
        self.n = 0


class Sched:

    def __init__(self):
        self.ops = {e: [] for e in ENGS}
        self.wmax = {e: {} for e in ENGS}

    def add(self, eng, fn=None, waits=(), incs=(), drain=False):
        wl = []
        for (sm, cnt) in waits:
            if cnt is None or cnt <= 0:
                continue
            key = id(sm)
            if cnt <= self.wmax[eng].get(key, 0):
                continue
            self.wmax[eng][key] = cnt
            wl.append((sm, cnt))
        for (sm, k) in incs:
            sm.n += k
        self.ops[eng].append((fn, wl, list(incs), drain))

    def flush(self, block):
        def mk(eng_name):
            def run(e):
                for (fn, wl, il, drain) in self.ops[eng_name]:
                    for (sm, cnt) in wl:
                        e.wait_ge(sm.h, cnt)
                    if drain:
                        e.drain()
                    inst = fn(e) if fn is not None else None
                    for (sm, k) in il:
                        if inst is None:
                            inst = e.nop()
                        inst.then_inc(sm.h, k)
                        inst = None
            return run

        for eng_name in ENGS:
            if self.ops[eng_name]:
                getattr(block, eng_name)(mk(eng_name))


def build_program():
    nc = bass.Bass("TRN2", target_bir_lowering=False, debug=False)

    d_x = nc.dram_tensor("x_b", [T, C], F32, kind="ExternalInput").ap()
    d_xown = nc.dram_tensor("xown", [TOK, C], F32, kind="ExternalInput").ap()
    d_wq = nc.dram_tensor("wq", [C, 512], F32, kind="ExternalInput").ap()
    d_bq = nc.dram_tensor("bq", [512], F32, kind="ExternalInput").ap()
    d_wk = nc.dram_tensor("wk", [C, 128], F32, kind="ExternalInput").ap()
    d_bk = nc.dram_tensor("bk", [128], F32, kind="ExternalInput").ap()
    d_wv = nc.dram_tensor("wv", [C, 128], F32, kind="ExternalInput").ap()
    d_bv = nc.dram_tensor("bv", [128], F32, kind="ExternalInput").ap()
    d_wo = nc.dram_tensor("wo", [CKV, C], F32, kind="ExternalInput").ap()
    d_bo = nc.dram_tensor("bo", [C], F32, kind="ExternalInput").ap()
    d_gwt = nc.dram_tensor("gwt", [C, NE], F32, kind="ExternalInput").ap()
    d_gup = nc.dram_tensor("gup", [NE, C, 2 * FI], F32, kind="ExternalInput").ap()
    d_dnt = nc.dram_tensor("dnt", [NE, FI, C], F32, kind="ExternalInput").ap()
    d_mg = nc.dram_tensor("mg", [NE], I32, kind="ExternalInput").ap()
    d_de = nc.dram_tensor("de", [NG], I32, kind="ExternalInput").ap()
    d_ident = nc.dram_tensor("identh", [128, 128], BF16, kind="ExternalInput").ap()
    d_masks = nc.dram_tensor("masks", [128, 4 * 512], F32, kind="ExternalInput").ap()
    d_ct = nc.dram_tensor("ct", [128, T], F32, kind="ExternalInput").ap()
    d_st = nc.dram_tensor("st", [128, T], F32, kind="ExternalInput").ap()
    d_ones = nc.dram_tensor("onescol", [1, 128], F32, kind="ExternalInput").ap()
    d_rbgu = nc.dram_tensor("rbgu", [128, KC], F32, kind="ExternalInput").ap()
    d_rbdn = nc.dram_tensor("rbdn", [128, 6], F32, kind="ExternalInput").ap()
    d_rby = nc.dram_tensor("rby", [128, 32], F32, kind="ExternalInput").ap()
    d_jc = nc.dram_tensor("jconst", [1, 1], F32, kind="ExternalInput").ap()
    d_swap = nc.dram_tensor("swaph", [128, 128], F32, kind="ExternalInput").ap()
    d_out = nc.dram_tensor("out", [TOK, C], F32, kind="ExternalOutput").ap()
    DBG = False
    if DBG:
        d_dbg_q = nc.dram_tensor("dbg_q", [128, T], F32, kind="ExternalOutput").ap()
        d_dbg_k = nc.dram_tensor("dbg_k", [128, T], F32, kind="ExternalOutput").ap()
        d_dbg_v = nc.dram_tensor("dbg_v", [128, T], F32, kind="ExternalOutput").ap()
        d_dbg_yh = nc.dram_tensor("dbg_yh", [128, T], F32, kind="ExternalOutput").ap()
        d_dbg_x2 = nc.dram_tensor("dbg_x2", [128, NTT * C], F32, kind="ExternalOutput").ap()
        d_dbg_w = nc.dram_tensor("dbg_w", [128, NTT * NG], F32, kind="ExternalOutput").ap()

    d_yb = nc.dram_tensor("ybounce", [T, D], BF16, kind="Internal")
    d_ybg = nc.dram_tensor("ybgather", [4 * T, D], BF16, kind="Internal")

    gup_flat = d_gup.rearrange("e c f -> (e c) f")
    dnt_flat = d_dnt.rearrange("e i c -> (e i) c")

    octx = ExitStack()
    with octx:
        def sb(name, shape, dt):
            return octx.enter_context(nc.sbuf_tensor(name, shape, dt))

        def sem(name):
            return Sem(octx.enter_context(nc.semaphore(name)))

        s_cpl = sem("s_cpl")
        s_csp = sem("s_csp")
        s_wq = sem("s_wq")
        s_xs = [sem(f"s_x{i}") for i in range(4)]
        s_tb = [sem(f"s_tb{i}") for i in range(2)]
        s_xsv = sem("s_xsv")
        s_pe = sem("s_pe")
        s_dve = sem("s_dve")
        s_act = sem("s_act")
        s_gate = sem("s_gate")
        s_up = sem("s_up")
        s_dn = sem("s_dn")
        s_yb = sem("s_yb")
        s_cc = sem("s_cc")
        s_yr = sem("s_yr")
        s_out = sem("s_out")
        s_dbg = sem("s_dbg")
        s_dbg2 = sem("s_dbg2")

        ident = sb("ident", [128, 128], BF16)
        swap_sb = sb("swap_sb", [128, 128], F32)
        bq_4 = sb("bq4", [128, 4], F32)
        bq_col = sb("bq_col", [128, 1], F32)
        bk_col = sb("bk_col", [128, 1], F32)
        bv_col = sb("bv_col", [128, 1], F32)
        sc_col = sb("sc_col", [128, 1], F32)
        eps_col = sb("eps_col", [128, 1], F32)
        nh_col = sb("nh_col", [128, 1], F32)
        ones_sb = sb("ones_sb", [1, 128], F32)
        def_sb = sb("def_sb", [1, NG], F32)
        de_sb = sb("de_sb", [1, NG], I32)
        mg_sb = sb("mg_sb", [1, NE], I32)
        mgf_sb = sb("mgf_sb", [1, NE], F32)
        de2048r = sb("de2048r", [1, NG], F32)
        de768r = sb("de768r", [1, NG], F32)
        de2048 = sb("de2048", [128, NG], F32)
        de768 = sb("de768", [128, NG], F32)
        mg_b = sb("mg_b", [128, NE], F32)
        og = sb("og", [128, NG * NE], F32)
        gcol = sb("gcol", [128, 1], F32)
        rbgu = sb("rbgu_s", [128, KC], F32)
        rbdn = sb("rbdn_s", [128, 6], F32)
        rby = sb("rby_s", [128, 32], F32)
        jc_sb = sb("jc_sb", [1, 1], F32)
        jcol = sb("jcol", [128, 1], F32)
        idxgu_f = sb("idxgu_f", [128, NG * KC], F32)
        idxgu = sb("idxgu", [128, NG * KC], I32)
        idxdn_f = sb("idxdn_f", [128, NG * 6], F32)
        idxdn = sb("idxdn", [128, NG * 6], I32)
        idxy_f = sb("idxy_f", [128, 32], F32)
        idxy = sb("idxy", [128, 32], I32)
        w_sb = sb("w_sb", [128, NTT * NG], F32)
        gwt_sb = sb("gwt_sb", [128, KC * NE], BF16)

        xsave = sb("xsave", [128, NTT * C], F32)

        sch = Sched()

        def gp_dma(out_ap, in_ap, s, w=()):
            sch.add("gpsimd", lambda e, o=out_ap, i=in_ap: e.dma_start(out=o, in_=i),
                    waits=w, incs=[(s, 16)])

        def sp_dma(out_ap, in_ap, s, w=(), slow=False):
            sch.add("sync", lambda e, o=out_ap, i=in_ap, slow=slow: e.dma_start(
                out=o, in_=i, allow_slow_non_contiguous=slow),
                waits=w, incs=[(s, 16)])

        sp_dma(ident[:, :], d_ident[:, :], s_csp)
        sp_dma(swap_sb[:, :], d_swap[:, :], s_csp)
        sp_dma(bq_4[:, :], d_bq.rearrange("(g d) -> d g", d=128), s_csp, slow=True)
        sp_dma(bk_col[:, :], d_bk[:, None], s_csp, slow=True)
        sp_dma(bv_col[:, :], d_bv[:, None], s_csp, slow=True)
        sp_dma(ones_sb[:, :], d_ones[:, :], s_csp)
        sp_dma(de_sb[:, :], d_de[None, :], s_csp)
        sp_dma(mg_sb[:, :], d_mg[None, :], s_csp)
        sp_dma(rbgu[:, :], d_rbgu[:, :], s_csp)
        sp_dma(rbdn[:, :], d_rbdn[:, :], s_csp)
        sp_dma(rby[:, :], d_rby[:, :], s_csp)
        sp_dma(jc_sb[:, :], d_jc[:, :], s_csp)
        CSP_T1 = s_csp.n
        gp_dma(gwt_sb[:, :].rearrange("p (c e) -> p c e", c=KC),
               d_gwt.rearrange("(c p) e -> p c e", p=128), s_cpl)

        sch.add("vector", lambda e: e.memset(sc_col[:, :], SCQ))
        sch.add("vector", lambda e: e.memset(eps_col[:, :], 1e-5))
        sch.add("vector", lambda e: e.memset(nh_col[:, :], -0.5))
        sch.add("vector", lambda e: e.reduce_sum(bq_col[:, :], bq_4[:, :], axis=AX.X),
                waits=[(s_csp, CSP_T1)])
        sch.add("vector", lambda e: e.tensor_scalar_mul(bq_col[:, :], bq_col[:, :], SCQ),
                drain=True)
        sch.add("vector", lambda e: e.tensor_copy(def_sb[:, :], de_sb[:, :]),
                waits=[(s_csp, CSP_T1)])
        sch.add("vector", lambda e: e.tensor_scalar_mul(de2048r[:, :], def_sb[:, :],
                                                        2048.0), drain=True)
        sch.add("vector", lambda e: e.tensor_scalar_mul(de768r[:, :], def_sb[:, :],
                                                        768.0))
        sch.add("vector", lambda e: e.tensor_copy(mgf_sb[:, :], mg_sb[:, :]),
                waits=[(s_csp, CSP_T1)], drain=True, incs=[(s_dve, 1)])
        dve_rows = s_dve.n

        init_ctx = ExitStack()
        p_bc = init_ctx.enter_context(nc.psum_tensor("p_bc", [128, 64], F32))
        sch.add("tensor", lambda e: e.matmul(p_bc[:, 0:NG], ones_sb[:, :],
                                             de2048r[:, :], start=True, stop=True),
                waits=[(s_dve, dve_rows)])
        sch.add("tensor", lambda e: e.matmul(p_bc[:, 8:8 + NG], ones_sb[:, :],
                                             de768r[:, :], start=True, stop=True))
        sch.add("tensor", lambda e: e.matmul(p_bc[:, 16:16 + NE], ones_sb[:, :],
                                             mgf_sb[:, :], start=True, stop=True))
        sch.add("tensor", lambda e: e.matmul(p_bc[:, 48:49], ones_sb[:, :],
                                             jc_sb[:, :], start=True, stop=True),
                incs=[(s_pe, 1)])
        pe_bc = s_pe.n
        sch.add("vector", lambda e: e.tensor_copy(de2048[:, :], p_bc[:, 0:NG]),
                waits=[(s_pe, pe_bc)])
        sch.add("vector", lambda e: e.tensor_copy(de768[:, :], p_bc[:, 8:8 + NG]))
        sch.add("vector", lambda e: e.tensor_copy(mg_b[:, :], p_bc[:, 16:16 + NE]))
        sch.add("vector", lambda e: e.tensor_copy(jcol[:, :], p_bc[:, 48:49]))
        for g in range(NG):
            sch.add("vector", lambda e, g=g: e.memset(gcol[:, :], float(g)),
                    drain=True)
            sch.add("vector", lambda e, g=g: e.tensor_scalar(
                out=og[:, g * NE:(g + 1) * NE], in0=mg_b[:, :], scalar1=gcol[:, :],
                scalar2=None, op0=ALU.is_equal), drain=True)
        for g in range(NG):
            sch.add("vector", lambda e, g=g: e.tensor_scalar(
                out=idxgu_f[:, g * KC:(g + 1) * KC], in0=rbgu[:, :],
                scalar1=de2048[:, g:g + 1], scalar2=None, op0=ALU.add),
                drain=(g == 0))
            sch.add("vector", lambda e, g=g: e.tensor_scalar(
                out=idxdn_f[:, g * 6:(g + 1) * 6], in0=rbdn[:, :],
                scalar1=de768[:, g:g + 1], scalar2=None, op0=ALU.add))
        sch.add("vector", lambda e: e.tensor_scalar(
            out=idxy_f[:, :], in0=rby[:, :], scalar1=jcol[:, :], scalar2=None,
            op0=ALU.add))
        sch.add("vector", lambda e: e.tensor_copy(idxgu[:, :], idxgu_f[:, :]),
                drain=True)
        sch.add("vector", lambda e: e.tensor_copy(idxdn[:, :], idxdn_f[:, :]))
        sch.add("vector", lambda e: e.tensor_copy(idxy[:, :], idxy_f[:, :]),
                incs=[(s_dve, 1)])
        IDX_DONE = s_dve.n

        with nc.Block() as blk:
            sch.flush(blk)
        init_ctx.close()

        scA = ExitStack()
        qT = scA.enter_context(nc.sbuf_tensor("qT", [128, T], BF16))
        kT = scA.enter_context(nc.sbuf_tensor("kT", [128, T], BF16))
        v_sb = scA.enter_context(nc.sbuf_tensor("v_sb", [128, T], BF16))
        p1 = ExitStack()
        with p1:
            def sb1(name, shape, dt):
                return p1.enter_context(nc.sbuf_tensor(name, shape, dt))

            scr2048 = sb1("scr2048", [128, C], F32)
            wqacc = scr2048
            xsq = scr2048
            wqeff = sb1("wqeff", [128, C], BF16)
            wk_sb = sb1("wk_sb", [128, C], BF16)
            wv_sb = sb1("wv_sb", [128, C], BF16)
            xring = [sb1(f"xr{i}", [128, C], F32) for i in range(4)]
            xln = [sb1(f"xln{i}", [128, C], BF16) for i in range(2)]
            xlnT = [sb1(f"xlnT{i}", [128, KC * 512], BF16) for i in range(2)]
            ct_r = [sb1(f"ct_r{i}", [128, 512], F32) for i in range(2)]
            st_r = [sb1(f"st_r{i}", [128, 512], F32) for i in range(2)]
            qf = sb1("qf", [128, 512], F32)
            kf = sb1("kf", [128, 512], F32)
            vtmp = sb1("vtmp", [128, 512], BF16)
            rt = [sb1(f"rt{i}", [128, 512], F32) for i in range(2)]
            stat = sb1("stat", [128, 32], F32)

            p_tp = [p1.enter_context(nc.psum_tensor(f"p_tp{i}", [128, 512], F32))
                    for i in range(2)]
            p_sw = [p1.enter_context(nc.psum_tensor(f"p_sw{i}", [128, 512], F32))
                    for i in range(2)]
            p_q = p1.enter_context(nc.psum_tensor("p_q", [128, 512], F32))
            p_k = p1.enter_context(nc.psum_tensor("p_k", [128, 512], F32))
            p_v = p1.enter_context(nc.psum_tensor("p_v", [128, 512], F32))
            p_vt = p1.enter_context(nc.psum_tensor("p_vt", [128, 512], F32))

            sch = Sched()

            for g in range(4):
                src = d_wq[:, 128 * g:128 * (g + 1)].rearrange(
                    "(fc p) d -> p fc d", p=128)
                dst = wqacc[:, :].rearrange("p (fc d) -> p fc d", fc=KC)
                if g == 0:
                    sch.add("gpsimd", lambda e, s=src, dst=dst: e.dma_start(
                        out=dst, in_=s), incs=[(s_wq, 16)])
                else:
                    sch.add("gpsimd", lambda e, s=src, dst=dst: e.dma_start(
                        out=dst, in_=s, accum_op=ALU.add),
                        waits=[(s_wq, s_wq.n)], incs=[(s_wq, 16)])
            WQ_T = s_wq.n
            gp_dma(wk_sb[:, :].rearrange("p (fc d) -> p fc d", fc=KC),
                   d_wk.rearrange("(fc p) d -> p fc d", p=128), s_cpl)
            gp_dma(wv_sb[:, :].rearrange("p (fc d) -> p fc d", fc=KC),
                   d_wv.rearrange("(fc p) d -> p fc d", p=128), s_cpl)
            CPL_T2 = s_cpl.n
            CSP_T2 = s_csp.n
            for tt in range(NTT):
                sp_dma(xsave[:, tt * C:(tt + 1) * C],
                       d_xown[128 * tt:128 * (tt + 1), :], s_xsv)
            XSV_T = s_xsv.n

            sch.add("vector", lambda e: e.tensor_copy(wqeff[:, :], wqacc[:, :]),
                    waits=[(s_wq, WQ_T)], incs=[(s_dve, 1)])
            DVE_WQ = s_dve.n

            apply_done = {}
            swap_used = {}
            tp_last = {}
            xlnT_evac = {}
            qkv_mm = {}
            qkv_evac = {}
            v_evac = {}
            rope_done = {}
            xload = {}

            for tg in range(NTG):
                tslot = tg % 2
                wtb = [(s_dve, rope_done[tg - 2])] if tg >= 2 else []
                sch.add("sync", lambda e, tg=tg, tslot=tslot: e.dma_start(
                    out=ct_r[tslot][:, :], in_=d_ct[:, tg * 512:(tg + 1) * 512]),
                    waits=wtb, incs=[(s_tb[tslot], 16)])
                sch.add("sync", lambda e, tg=tg, tslot=tslot: e.dma_start(
                    out=st_r[tslot][:, :], in_=d_st[:, tg * 512:(tg + 1) * 512]),
                    incs=[(s_tb[tslot], 16)])
                tb_cnt = s_tb[tslot].n
                for i in range(4):
                    t = 4 * tg + i
                    slot = t % 4
                    w = [(s_act, apply_done[t - 4])] if t >= 4 else []
                    ssem = s_xs[slot]
                    sch.add("sync", lambda e, t=t, slot=slot: e.dma_start(
                        out=xring[slot][:, :], in_=d_x[128 * t:128 * (t + 1), :]),
                        waits=w, incs=[(ssem, 16)])
                    xload[t] = (ssem, ssem.n)

                for i in range(4):
                    t = 4 * tg + i
                    ssem, scnt = xload[t]
                    xt = xring[t % 4]
                    sch.add("vector", lambda e, xt=xt, i=i: e.reduce_sum(
                        stat[:, i:i + 1], xt[:, :], axis=AX.X, negate=True),
                        waits=[(ssem, scnt)])
                    sch.add("scalar", lambda e, xt=xt, i=i: e.activation(
                        out=xsq[:, :], in_=xt[:, :], func=AF.Square,
                        accum_out=stat[:, 4 + i:5 + i]),
                        waits=[(ssem, scnt), (s_dve, DVE_WQ)],
                        drain=True, incs=[(s_act, 1)])
                a_sq = s_act.n
                sch.add("vector", lambda e: e.tensor_scalar_mul(
                    stat[:, 8:12], stat[:, 0:4], INV2048), drain=True)
                sch.add("vector", lambda e: e.tensor_scalar_mul(
                    stat[:, 12:16], stat[:, 4:8], INV2048), waits=[(s_act, a_sq)])
                sch.add("vector", lambda e: e.tensor_mul(
                    stat[:, 16:20], stat[:, 8:12], stat[:, 8:12]), drain=True)
                sch.add("vector", lambda e: e.tensor_sub(
                    stat[:, 20:24], stat[:, 12:16], stat[:, 16:20]), drain=True,
                    incs=[(s_dve, 1)])
                d_var = s_dve.n
                sch.add("scalar", lambda e: e.activation(
                    out=stat[:, 24:28], in_=stat[:, 20:24], func=AF.Ln,
                    bias=eps_col[:, :]), waits=[(s_dve, d_var)])
                sch.add("scalar", lambda e: e.activation(
                    out=stat[:, 24:28], in_=stat[:, 24:28], func=AF.Exp,
                    scale=nh_col[:, :]), drain=True, incs=[(s_act, 1)])
                a_rstd = s_act.n
                sch.add("vector", lambda e: e.tensor_mul(
                    stat[:, 28:32], stat[:, 8:12], stat[:, 24:28]),
                    waits=[(s_act, a_rstd)], drain=True, incs=[(s_dve, 1)])
                d_nmr = s_dve.n

                for i in range(4):
                    t = 4 * tg + i
                    xt = xring[t % 4]
                    xo = xln[t % 2]
                    wap = [(s_dve, d_nmr)]
                    if t >= 2:
                        wap.append((s_pe, tp_last[t - 2]))
                    sch.add("scalar", lambda e, xt=xt, xo=xo, i=i: e.activation(
                        out=xo[:, :], in_=xt[:, :], func=AF.Identity,
                        bias=stat[:, 28 + i:29 + i], scale=stat[:, 24 + i:25 + i]),
                        waits=wap, drain=True, incs=[(s_act, 1)])
                    apply_done[t] = s_act.n

                    for fq in range(4):
                        bank = p_tp[(t * 4 + fq) % 2]
                        w = [(s_act, apply_done[t])]
                        pidx = t * 4 + fq - 2
                        if pidx >= 0:
                            w.append((s_dve, xlnT_evac[pidx]))
                        for fs in range(4):
                            fc = fq * 4 + fs
                            sch.add("tensor", lambda e, xo=xo, fc=fc, fs=fs, bank=bank:
                                    e.matmul(bank[:, fs * 128:(fs + 1) * 128],
                                             xo[:, fc * 128:(fc + 1) * 128],
                                             ident[:, :], start=True, stop=True),
                                    waits=w if fs == 0 else [],
                                    incs=[(s_pe, 1)] if fs == 3 else [])
                        pe_c = s_pe.n
                        xTd = xlnT[tg % 2]

                        def ev(e, xTd=xTd, bank=bank, fq=fq, i=i):
                            r = None
                            for fs in range(4):
                                fc = fq * 4 + fs
                                r = e.tensor_copy(
                                    xTd[:, fc * 512 + i * 128: fc * 512 + (i + 1) * 128],
                                    bank[:, fs * 128:(fs + 1) * 128])
                            return r
                        sch.add("vector", ev, waits=[(s_pe, pe_c)],
                                incs=[(s_dve, 1)])
                        xlnT_evac[t * 4 + fq] = s_dve.n
                    tp_last[t] = s_pe.n
                XLNT_TG = s_dve.n

                xTd = xlnT[tg % 2]
                w0 = [(s_dve, XLNT_TG), (s_dve, DVE_WQ), (s_cpl, CPL_T2)]
                if tg >= 1:
                    w0.append((s_act, qkv_evac[tg - 1]))
                for (psum, wmat, nm) in ((p_q, wqeff, "q"), (p_k, wk_sb, "k"),
                                         (p_v, wv_sb, "v")):
                    for kc in range(KC):
                        sch.add("tensor", lambda e, psum=psum, wmat=wmat, kc=kc,
                                xTd=xTd: e.matmul(
                                    psum[:, :], wmat[:, kc * 128:(kc + 1) * 128],
                                    xTd[:, kc * 512:(kc + 1) * 512],
                                    start=(kc == 0), stop=(kc == KC - 1)),
                                waits=w0 if kc == 0 else [],
                                incs=[(s_pe, 1)] if kc == KC - 1 else [])
                    qkv_mm[(tg, nm)] = s_pe.n
                    w0 = []

                wqe = [(s_pe, qkv_mm[(tg, "q")])]
                if tg >= 1:
                    wqe.append((s_dve, rope_done[tg - 1]))
                sch.add("scalar", lambda e: e.activation(
                    out=qf[:, :], in_=p_q[:, :], func=AF.Identity,
                    bias=bq_col[:, :], scale=sc_col[:, :]),
                    waits=wqe, incs=[(s_act, 1)])
                a_q = s_act.n
                sch.add("scalar", lambda e: e.activation(
                    out=kf[:, :], in_=p_k[:, :], func=AF.Identity, bias=bk_col[:, :]),
                    waits=[(s_pe, qkv_mm[(tg, "k")])], incs=[(s_act, 1)])
                a_k = s_act.n
                sch.add("scalar", lambda e: e.activation(
                    out=vtmp[:, :], in_=p_v[:, :], func=AF.Identity, bias=bv_col[:, :]),
                    waits=[(s_pe, qkv_mm[(tg, "v")])], incs=[(s_act, 1)])
                a_v = s_act.n
                qkv_evac[tg] = s_act.n

                w = [(s_act, a_v)]
                if tg >= 1:
                    w.append((s_dve, v_evac[tg - 1]))
                for i in range(4):
                    sch.add("tensor", lambda e, i=i: e.matmul(
                        p_vt[:, i * 128:(i + 1) * 128],
                        vtmp[:, i * 128:(i + 1) * 128], ident[:, :],
                        start=True, stop=True), waits=w if i == 0 else [],
                        incs=[(s_pe, 1)] if i == 3 else [])
                pe_vt = s_pe.n

                def ev_v(e, tg=tg):
                    r = None
                    for i in range(4):
                        tglob = 4 * tg + i
                        r = e.tensor_copy(v_sb[:, tglob * 128:(tglob + 1) * 128],
                                          p_vt[:, i * 128:(i + 1) * 128])
                    return r
                sch.add("vector", ev_v, waits=[(s_pe, pe_vt)], incs=[(s_dve, 1)])
                v_evac[tg] = s_dve.n

                cts = ct_r[tslot][:, :]
                sts = st_r[tslot][:, :]
                for si, (srcf, dst, a_cnt) in enumerate(((qf, qT, a_q),
                                                         (kf, kT, a_k))):
                    swb = p_sw[si]
                    w = [(s_act, a_cnt)]
                    if tg >= 1:
                        w.append((s_dve, swap_used[(tg - 1, si)]))
                    sch.add("tensor", lambda e, srcf=srcf, swb=swb: e.matmul(
                        swb[:, :], swap_sb[:, :], srcf[:, :],
                        start=True, stop=True), waits=w, incs=[(s_pe, 1)])
                    pe_sw = s_pe.n
                    sch.add("vector", lambda e, srcf=srcf, cts=cts: e.tensor_mul(
                        rt[0][:, :], srcf[:, :], cts),
                        waits=[(s_act, a_cnt), (s_tb[tslot], tb_cnt)], drain=True)
                    sch.add("vector", lambda e, swb=swb, sts=sts: e.tensor_mul(
                        rt[1][:, :], swb[:, :], sts), waits=[(s_pe, pe_sw)],
                        incs=[(s_dve, 1)])
                    swap_used[(tg, si)] = s_dve.n
                    sch.add("vector", lambda e, dst=dst, tg=tg: e.tensor_add(
                        dst[:, tg * 512:(tg + 1) * 512], rt[0][:, :], rt[1][:, :]),
                        drain=True, incs=[(s_dve, 1)])
                rope_done[tg] = s_dve.n

            with nc.Block() as blk:
                sch.flush(blk)

        gate_buf = octx.enter_context(nc.sbuf_tensor("gate_buf", [128, KC * FI], BF16, side="right"))
        p2 = ExitStack()
        with p2:
            def sb2(name, shape, dt):
                return p2.enter_context(nc.sbuf_tensor(name, shape, dt))

            y_h = sb2("y_h", [128, T], BF16)
            masks = sb2("masks_s", [128, 4 * 512], BF16)
            Sst = [sb2(f"Sst{i}", [128, T], F32) for i in range(2)]
            Pst = [sb2(f"Pst{i}", [128, T], BF16) for i in range(3)]
            PTst = [sb2(f"PTst{i}", [128, T], BF16) for i in range(2)]
            mx = [sb2(f"mx{i}", [128, 8], F32) for i in range(3)]
            nmf = [sb2(f"nmf{i}", [128, 1], F32) for i in range(3)]
            ls = [sb2(f"ls{i}", [128, 8], F32) for i in range(3)]
            rinv = [sb2(f"rinv{i}", [128, 1], F32) for i in range(3)]

            p_S = [p2.enter_context(nc.psum_tensor(f"p_S{i}", [128, 512], F32))
                   for i in range(4)]
            p_PT = [p2.enter_context(nc.psum_tensor(f"p_PT{i}", [128, 512], F32))
                    for i in range(2)]
            p_y = [p2.enter_context(nc.psum_tensor(f"p_y{i}", [128, 128], F32))
                   for i in range(2)]

            sch = Sched()
            gp_dma(masks[:, :], d_masks[:, :], s_cpl)
            CPL_MASKS = s_cpl.n

            for c in range(KC):
                sch.add("gpsimd", lambda e, c=c: e.indirect_dma_start(
                    out=gate_buf[:, c * FI:(c + 1) * FI], out_offset=None,
                    in_=gup_flat,
                    in_offset=IndirectOffsetOnAxis(idxgu[:, c:c + 1], 0)),
                    waits=[(s_dve, IDX_DONE)] if c == 0 else [],
                    incs=[(s_gate, 16)])

            sbank_free = {}
            s_done = {}
            mask_done = {}
            s_evac = {}
            fmax_done = {}
            exp_done = {}
            t_done = {}
            ptv_done = {}
            av_done = {}
            rinv_done = {}
            yev_done = {}

            def emit_Sphase(qt):
                par = qt % 3
                spar = qt % 2
                nkc = qt // 4 + 1
                for kc in range(nkc):
                    bid = (qt % 2) * 2 + (kc % 2)
                    bank = p_S[bid]
                    w = []
                    if bid in sbank_free:
                        w.extend(sbank_free[bid])
                    sch.add("tensor", lambda e, qt=qt, kc=kc, bank=bank: e.matmul(
                        bank[:, :], qT[:, qt * 128:(qt + 1) * 128],
                        kT[:, kc * 512:(kc + 1) * 512], start=True, stop=True),
                        waits=w, incs=[(s_pe, 1)])
                    s_done[(qt, kc)] = s_pe.n
                    dg = (kc == qt // 4)
                    if dg:
                        m = qt % 4
                        wdg = [(s_pe, s_done[(qt, kc)]), (s_cpl, CPL_MASKS)]
                        if qt >= 2:
                            wdg.append((s_act, exp_done[(qt - 2, qt // 4 - 0 if False else (qt - 2) // 4)]))
                        sch.add("vector", lambda e, bank=bank, m=m, kc=kc, par=spar:
                                e.tensor_add(Sst[par][:, kc * 512:(kc + 1) * 512],
                                             bank[:, :],
                                             masks[:, m * 512:(m + 1) * 512]),
                                waits=wdg, incs=[(s_dve, 1)])
                        dve_ev = s_dve.n
                        sbank_free[bid] = [(s_dve, dve_ev)]
                        sch.add("vector", lambda e, kc=kc, par=spar: e.reduce_max(
                            mx[qt % 3][:, kc:kc + 1],
                            Sst[par][:, kc * 512:(kc + 1) * 512], axis=AX.X),
                            drain=True, incs=[(s_dve, 1)])
                    else:
                        sch.add("vector", lambda e, bank=bank, kc=kc, par=par:
                                e.reduce_max(mx[par][:, kc:kc + 1], bank[:, :],
                                             axis=AX.X),
                                waits=[(s_pe, s_done[(qt, kc)])],
                                incs=[(s_dve, 1)])
                        dmx = s_dve.n
                        sch.add("scalar", lambda e, bank=bank, kc=kc, par=spar: e.copy(
                            out=Sst[par][:, kc * 512:(kc + 1) * 512], in_=bank[:, :]),
                            waits=[(s_pe, s_done[(qt, kc)]), (s_dve, dmx)],
                            incs=[(s_act, 1)])
                        s_evac[(qt, kc)] = s_act.n
                        sbank_free[bid] = [(s_act, s_act.n)]
                    mask_done[(qt, kc)] = s_dve.n
                sch.add("vector", lambda e, par=par, nkc=nkc: e.reduce_max(
                    nmf[par][:, :], mx[par][:, 0:nkc], axis=AX.X, negate=True),
                    drain=True, incs=[(s_dve, 1)])
                fmax_done[qt] = s_dve.n

            def emit_exp(qt):
                par = qt % 3
                spar = qt % 2
                nkc = qt // 4 + 1
                for kc in range(nkc):
                    sch.add("scalar", lambda e, kc=kc, par=par, spar=spar:
                            e.activation(
                                out=Pst[par][:, kc * 512:(kc + 1) * 512],
                                in_=Sst[spar][:, kc * 512:(kc + 1) * 512],
                                func=AF.Exp, bias=nmf[par][:, :],
                                accum_out=ls[par][:, kc:kc + 1]),
                            waits=[(s_dve, fmax_done[qt])], drain=(kc == 0),
                            incs=[(s_act, 1)])
                    exp_done[(qt, kc)] = s_act.n

            def emit_lsum(qt):
                par = qt % 3
                nkc = qt // 4 + 1
                w = [(s_act, exp_done[(qt, nkc - 1)])]
                if qt >= 3:
                    w.append((s_act, yev_done[qt - 3]))
                sch.add("vector", lambda e, par=par, nkc=nkc: e.reduce_sum(
                    rinv[par][:, :], ls[par][:, 0:nkc], axis=AX.X), waits=w)
                sch.add("vector", lambda e, par=par: e.reciprocal(
                    rinv[par][:, :], rinv[par][:, :]), drain=True,
                    incs=[(s_dve, 1)])
                rinv_done[qt] = s_dve.n

            ptbank_free = {}

            def emit_T(qt):
                par = qt % 3
                nkc = qt // 4 + 1
                for kc in range(nkc):
                    bid = kc % 2
                    bank = p_PT[bid]
                    w = [(s_act, exp_done[(qt, kc)])]
                    if bid in ptbank_free:
                        w.append(ptbank_free[bid])
                    for b4 in range(4):
                        sch.add("tensor", lambda e, par=par, kc=kc, b4=b4, bank=bank:
                                e.matmul(bank[:, b4 * 128:(b4 + 1) * 128],
                                         Pst[par][:, kc * 512 + b4 * 128:
                                                   kc * 512 + (b4 + 1) * 128],
                                         ident[:, :], start=True, stop=True),
                                waits=w if b4 == 0 else [],
                                incs=[(s_pe, 1)] if b4 == 3 else [])
                    t_done[(qt, kc)] = s_pe.n
                    sch.add("vector", lambda e, par=qt % 2, kc=kc, bank=bank:
                            e.tensor_copy(PTst[par][:, kc * 512:(kc + 1) * 512],
                                          bank[:, :]),
                            waits=[(s_pe, t_done[(qt, kc)])], incs=[(s_dve, 1)])
                    ptv_done[(qt, kc)] = s_dve.n
                    ptbank_free[bid] = (s_dve, s_dve.n)

            def emit_av(qt):
                par = qt % 3
                nkc = qt // 4 + 1
                first = True
                for kc in range(nkc):
                    w = [(s_dve, ptv_done[(qt, kc)])]
                    if first and qt >= 2:
                        w.append((s_act, yev_done[qt - 2]))
                    for b4 in range(4):
                        blk_i = kc * 4 + b4
                        last = (kc == nkc - 1 and b4 == 3)
                        sch.add("tensor", lambda e, par=par, ppar=qt % 2, blk_i=blk_i,
                                st_=first, last=last: e.matmul(
                                    p_y[ppar][:, :],
                                    PTst[ppar][:, blk_i * 128:(blk_i + 1) * 128],
                                    v_sb[:, blk_i * 128:(blk_i + 1) * 128],
                                    start=st_, stop=last),
                                waits=w if b4 == 0 else [],
                                incs=[(s_pe, 1)] if last else [])
                        first = False
                    w = []
                av_done[qt] = s_pe.n

            def emit_yevac(qt):
                par = qt % 3
                ppar = qt % 2
                sch.add("scalar", lambda e, par=par, ppar=ppar, qt=qt: e.mul(
                    out=y_h[:, qt * 128:(qt + 1) * 128], in_=p_y[ppar][:, :],
                    mul=rinv[par][:, :]),
                    waits=[(s_pe, av_done[qt]), (s_dve, rinv_done[qt])],
                    incs=[(s_act, 1)])
                yev_done[qt] = s_act.n

            def emit_bounce(half):
                lo, hi = half * 16, (half + 1) * 16
                sch.add("sync", lambda e, lo=lo, hi=hi: e.dma_start(
                    out=d_yb.ap()[lo * 128:hi * 128, :].rearrange(
                        "(i p) d -> p i d", p=128),
                    in_=y_h[:, lo * 128:hi * 128].rearrange(
                        "p (i d) -> p i d", i=16)),
                    waits=[(s_act, yev_done[hi - 1]), (s_yb, s_yb.n)],
                    incs=[(s_yb, 16)])
                sch.add("gpsimd", lambda e, half=half, lo=lo: e.collective_compute(
                    "AllGather", ALU.bypass,
                    replica_groups=[[0, 1, 2, 3], [4, 5, 6, 7]],
                    ins=[d_yb.ap()[lo * 128:(lo + 16) * 128, :].opt()],
                    outs=[d_ybg.ap()[half * 4 * 2048:(half + 1) * 4 * 2048, :].opt()]),
                    waits=[(s_yb, s_yb.n)], incs=[(s_cc, 1)])

            emit_Sphase(0)
            emit_Sphase(1)
            emit_exp(0)
            emit_lsum(0)
            for qt in range(NQT):
                if qt + 2 < NQT:
                    emit_Sphase(qt + 2)
                if qt + 1 < NQT:
                    emit_exp(qt + 1)
                emit_T(qt)
                emit_av(qt)
                emit_yevac(qt)
                if qt + 1 < NQT:
                    emit_lsum(qt + 1)
                if qt == 15:
                    emit_bounce(0)

            emit_bounce(1)
            YB_T = s_yb.n
            if DBG:
                gp_dma(d_dbg_q[:, :], qT[:, :], s_dbg, w=[(s_act, yev_done[NQT - 1])])
                gp_dma(d_dbg_k[:, :], kT[:, :], s_dbg)
                gp_dma(d_dbg_v[:, :], v_sb[:, :], s_dbg)
                gp_dma(d_dbg_yh[:, :], y_h[:, :], s_dbg)
                sch.add("gpsimd", None, waits=[(s_dbg, s_dbg.n)])

            with nc.Block() as blk:
                sch.flush(blk)
        scA.close()

        p2b = ExitStack()
        with p2b:
            def sb2b(name, shape, dt):
                return p2b.enter_context(nc.sbuf_tensor(name, shape, dt))

            wo_sb = sb2b("wo_sb", [128, 4 * C], BF16)
            bo_b = sb2b("bo_b", [128, C], F32)
            bo_row = bo_b[0:1, :]
            yown = sb2b("yown", [128, 4 * TOK], BF16)
            yT = sb2b("yT", [128, 4 * TOK], BF16)
            tmpo = sb2b("tmpo", [128, C], F32)

            p_bo = p2b.enter_context(nc.psum_tensor("p_bo", [128, 512], F32))
            p_yt = [p2b.enter_context(nc.psum_tensor(f"p_yt{i}", [128, 128], F32))
                    for i in range(2)]
            p_op = p2b.enter_context(nc.psum_tensor("p_op", [128, C], F32))

            sch = Sched()
            gp_dma(wo_sb[:, :].rearrange("p (c d) -> p c d", c=4),
                   d_wo.rearrange("(c p) d -> p c d", p=128), s_cpl)
            CPL_T4 = s_cpl.n
            sp_dma(bo_row, d_bo[None, :], s_csp)
            CSP_T4 = s_csp.n

            for col in range(32):
                sch.add("gpsimd", lambda e, col=col: e.indirect_dma_start(
                    out=yown[:, col * 128:(col + 1) * 128], out_offset=None,
                    in_=d_ybg.ap(),
                    in_offset=IndirectOffsetOnAxis(idxy[:, col:col + 1], 0)),
                    waits=[(s_cc, 2)] if col == 0 else [], incs=[(s_yr, 16)])
            YR_T = s_yr.n

            dve_bo = 0
            for nx in range(4):
                w = [(s_csp, CSP_T4)]
                if nx > 0:
                    w.append((s_dve, dve_bo))
                sch.add("tensor", lambda e, nx=nx: e.matmul(
                    p_bo[:, :], ones_sb[:, :], bo_b[0:1, nx * 512:(nx + 1) * 512],
                    start=True, stop=True), waits=w, incs=[(s_pe, 1)])
                pe_c = s_pe.n
                sch.add("vector", lambda e, nx=nx: e.tensor_copy(
                    bo_b[:, nx * 512:(nx + 1) * 512], p_bo[:, :]),
                    waits=[(s_pe, pe_c)], incs=[(s_dve, 1)])
                dve_bo = s_dve.n

            yt_evac = {}
            for col in range(32):
                bank = p_yt[col % 2]
                w = [(s_yr, YR_T)]
                if col >= 2:
                    w.append((s_dve, yt_evac[col - 2]))
                hh, tt = col // NTT, col % NTT
                sch.add("tensor", lambda e, col=col, bank=bank: e.matmul(
                    bank[:, :], yown[:, col * 128:(col + 1) * 128], ident[:, :],
                    start=True, stop=True), waits=w, incs=[(s_pe, 1)])
                pe_c = s_pe.n
                sch.add("vector", lambda e, hh=hh, tt=tt, bank=bank: e.tensor_copy(
                    yT[:, hh * TOK + tt * 128: hh * TOK + (tt + 1) * 128],
                    bank[:, :]), waits=[(s_pe, pe_c)], incs=[(s_dve, 1)])
                yt_evac[col] = s_dve.n

            tmpo_done = {}
            x2_done = {}
            for tt in range(NTT):
                w = [(s_dve, yt_evac[3 * NTT + tt]), (s_cpl, CPL_T4)]
                if tt >= 1:
                    w.append((s_dve, tmpo_done[tt - 1]))
                for hh in range(4):
                    for nx in range(4):
                        sch.add("tensor", lambda e, hh=hh, tt=tt, nx=nx: e.matmul(
                            p_op[:, nx * 512:(nx + 1) * 512],
                            yT[:, hh * TOK + tt * 128: hh * TOK + (tt + 1) * 128],
                            wo_sb[:, hh * C + nx * 512: hh * C + (nx + 1) * 512],
                            start=(hh == 0), stop=(hh == 3)),
                            waits=w if (hh == 0 and nx == 0) else [],
                            incs=[(s_pe, 1)] if (hh == 3 and nx == 3) else [])
                pe_op = s_pe.n
                sch.add("vector", lambda e: e.tensor_add(
                    tmpo[:, :], p_op[:, :], bo_b[:, :]),
                    waits=[(s_pe, pe_op), (s_dve, dve_bo), (s_xsv, XSV_T)],
                    drain=True, incs=[(s_dve, 1)])
                tmpo_done[tt] = s_dve.n
                sch.add("vector", lambda e, tt=tt: e.tensor_add(
                    xsave[:, tt * C:(tt + 1) * C], xsave[:, tt * C:(tt + 1) * C],
                    tmpo[:, :]), drain=True, incs=[(s_dve, 1)])
                x2_done[tt] = s_dve.n
            X2_ALL = s_dve.n
            if DBG:
                sp_dma(d_dbg_x2[:, :], xsave[:, :], s_dbg2, w=[(s_dve, X2_ALL)])
                sch.add("sync", None, waits=[(s_dbg2, s_dbg2.n)])

            with nc.Block() as blk:
                sch.flush(blk)

        up_buf = octx.enter_context(nc.sbuf_tensor("up_buf", [128, KC * FI], BF16, side="right"))
        dn_buf = octx.enter_context(nc.sbuf_tensor("dn_buf", [128, 6 * C], BF16, side="right"))
        xln2T = octx.enter_context(nc.sbuf_tensor("xln2T", [128, KC * TOK], BF16, side="right"))
        p3 = ExitStack()
        with p3:
            def sb3(name, shape, dt):
                return p3.enter_context(nc.sbuf_tensor(name, shape, dt))

            xsq2 = sb3("xsq2", [128, C], BF16)
            xln2 = [sb3(f"xln2{i}", [128, C], BF16) for i in range(2)]
            stat2 = sb3("stat2", [128, 8], F32)
            pgate = sb3("pgate", [128, NE], F32)
            m8 = sb3("m8", [128, 8], F32)
            prm = sb3("prm", [128, NE], F32)
            ptop = sb3("ptop", [128, NE], F32)
            wtmp = sb3("wtmp", [128, NE], F32)
            nmx2 = sb3("nmx2", [128, 2], F32)
            den = sb3("den", [128, 1], F32)

            p_h = p3.enter_context(nc.psum_tensor("p_h", [128, FI], F32))
            p_at = p3.enter_context(nc.psum_tensor("p_at", [128, FI], F32))
            p_dn = p3.enter_context(nc.psum_tensor("p_dn", [128, C], F32))
            p_t3 = [p_h, p_at]
            p_gl = p_dn

            sch = Sched()

            for c in range(KC):
                sch.add("gpsimd", lambda e, c=c: e.indirect_dma_start(
                    out=up_buf[:, c * FI:(c + 1) * FI], out_offset=None,
                    in_=gup_flat,
                    in_offset=IndirectOffsetOnAxis(idxgu[:, c:c + 1], 0),
                    element_offset=FI),
                    incs=[(s_up, 16)])
            for c in range(6):
                sch.add("gpsimd", lambda e, c=c: e.indirect_dma_start(
                    out=dn_buf[:, c * C:(c + 1) * C], out_offset=None,
                    in_=dnt_flat,
                    in_offset=IndirectOffsetOnAxis(idxdn[:, c:c + 1], 0)),
                    incs=[(s_dn, 16)])

            apply2 = {}
            t3_last = {}
            xl2T_evac = {}
            for tt in range(NTT):
                xt = xsave[:, tt * C:(tt + 1) * C]
                sch.add("vector", lambda e, xt=xt: e.reduce_sum(
                    stat2[:, 0:1], xt, axis=AX.X, negate=True),
                    waits=[(s_dve, X2_ALL)])
                sch.add("scalar", lambda e, xt=xt: e.activation(
                    out=xsq2[:, :], in_=xt, func=AF.Square,
                    accum_out=stat2[:, 1:2]), waits=[(s_dve, X2_ALL)],
                    incs=[(s_act, 1)])
                a_sq = s_act.n
                sch.add("vector", lambda e: e.tensor_scalar_mul(
                    stat2[:, 2:3], stat2[:, 0:1], INV2048), drain=True)
                sch.add("vector", lambda e: e.tensor_scalar_mul(
                    stat2[:, 3:4], stat2[:, 1:2], INV2048), waits=[(s_act, a_sq)])
                sch.add("vector", lambda e: e.tensor_mul(
                    stat2[:, 4:5], stat2[:, 2:3], stat2[:, 2:3]), drain=True)
                sch.add("vector", lambda e: e.tensor_sub(
                    stat2[:, 5:6], stat2[:, 3:4], stat2[:, 4:5]), drain=True,
                    incs=[(s_dve, 1)])
                d_var = s_dve.n
                sch.add("scalar", lambda e: e.activation(
                    out=stat2[:, 6:7], in_=stat2[:, 5:6], func=AF.Ln,
                    bias=eps_col[:, :]), waits=[(s_dve, d_var)])
                sch.add("scalar", lambda e: e.activation(
                    out=stat2[:, 6:7], in_=stat2[:, 6:7], func=AF.Exp,
                    scale=nh_col[:, :]), drain=True, incs=[(s_act, 1)])
                a_rstd = s_act.n
                sch.add("vector", lambda e: e.tensor_mul(
                    stat2[:, 7:8], stat2[:, 2:3], stat2[:, 6:7]),
                    waits=[(s_act, a_rstd)], drain=True, incs=[(s_dve, 1)])
                d_nmr = s_dve.n
                xo = xln2[tt % 2]
                wap2 = [(s_dve, d_nmr)]
                if tt >= 2:
                    wap2.append((s_pe, t3_last[tt - 2]))
                sch.add("scalar", lambda e, xt=xt, xo=xo: e.activation(
                    out=xo[:, :], in_=xt, func=AF.Identity,
                    bias=stat2[:, 7:8], scale=stat2[:, 6:7]),
                    waits=wap2, drain=True, incs=[(s_act, 1)])
                apply2[tt] = s_act.n

                for fq in range(4):
                    bank = p_t3[fq % 2]
                    w = [(s_act, apply2[tt])]
                    pidx = tt * 4 + fq - 2
                    if pidx >= 0:
                        w.append((s_dve, xl2T_evac[pidx]))
                    for fs in range(4):
                        fc = fq * 4 + fs
                        sch.add("tensor", lambda e, xo=xo, fc=fc, fs=fs, bank=bank:
                                e.matmul(bank[:, fs * 128:(fs + 1) * 128],
                                         xo[:, fc * 128:(fc + 1) * 128],
                                         ident[:, :], start=True, stop=True),
                                waits=w if fs == 0 else [],
                                incs=[(s_pe, 1)] if fs == 3 else [])
                    pe_c = s_pe.n

                    def ev2(e, bank=bank, fq=fq, tt=tt):
                        r = None
                        for fs in range(4):
                            fc = fq * 4 + fs
                            r = e.tensor_copy(
                                xln2T[:, fc * TOK + tt * 128: fc * TOK + (tt + 1) * 128],
                                bank[:, fs * 128:(fs + 1) * 128])
                        return r
                    sch.add("vector", ev2, waits=[(s_pe, pe_c)], incs=[(s_dve, 1)])
                    xl2T_evac[tt * 4 + fq] = s_dve.n
                t3_last[tt] = s_pe.n
            XL2T_ALL = s_dve.n

            a_pg_prev = 0
            d_sub_prev = 0
            for tt in range(NTT):
                w = [(s_dve, xl2T_evac[tt * 4 + 3]), (s_cpl, s_cpl.n)]
                if a_pg_prev:
                    w.append((s_act, a_pg_prev))
                for kc in range(KC):
                    sch.add("tensor", lambda e, kc=kc, tt=tt: e.matmul(
                        p_gl[:, 0:NE],
                        xln2T[:, kc * TOK + tt * 128: kc * TOK + (tt + 1) * 128],
                        gwt_sb[:, kc * NE:(kc + 1) * NE],
                        start=(kc == 0), stop=(kc == KC - 1)),
                        waits=w if kc == 0 else [],
                        incs=[(s_pe, 1)] if kc == KC - 1 else [])
                pe_g = s_pe.n
                par = tt % 2
                sch.add("vector", lambda e, par=par: e.reduce_max(
                    nmx2[:, par:par + 1], p_gl[:, 0:NE], axis=AX.X, negate=True),
                    waits=[(s_pe, pe_g)], incs=[(s_dve, 1)])
                d_nm = s_dve.n
                w = [(s_dve, d_nm)]
                if d_sub_prev:
                    w.append((s_dve, d_sub_prev))
                sch.add("scalar", lambda e, par=par: e.activation(
                    out=pgate[:, :], in_=p_gl[:, 0:NE], func=AF.Exp,
                    bias=nmx2[:, par:par + 1]), waits=w, incs=[(s_act, 1)])
                a_pg = s_act.n
                a_pg_prev = a_pg
                sch.add("vector", lambda e: e.max(out=m8[:, :], in_=pgate[:, :]),
                        waits=[(s_act, a_pg)], drain=True)
                sch.add("vector", lambda e: e.match_replace(
                    out=prm[:, :], in_to_replace=m8[:, :], in_values=pgate[:, :],
                    imm_value=0.0), drain=True)
                sch.add("vector", lambda e: e.tensor_sub(
                    ptop[:, :], pgate[:, :], prm[:, :]), drain=True,
                    incs=[(s_dve, 1)])
                d_sub_prev = s_dve.n
                sch.add("vector", lambda e: e.reduce_sum(
                    den[:, :], ptop[:, :], axis=AX.X), drain=True)
                sch.add("vector", lambda e: e.reciprocal(den[:, :], den[:, :]),
                        drain=True)
                for g in range(NG):
                    sch.add("vector", lambda e, g=g: e.tensor_mul(
                        wtmp[:, :], ptop[:, :], og[:, g * NE:(g + 1) * NE]),
                        drain=True)
                    sch.add("vector", lambda e, g=g, tt=tt: e.reduce_sum(
                        w_sb[:, tt * NG + g: tt * NG + g + 1], wtmp[:, :],
                        axis=AX.X), drain=True)
                sch.add("vector", lambda e, tt=tt: e.tensor_scalar(
                    out=w_sb[:, tt * NG:(tt + 1) * NG],
                    in0=w_sb[:, tt * NG:(tt + 1) * NG], scalar1=den[:, :],
                    scalar2=None, op0=ALU.mult), drain=True, incs=[(s_dve, 1)])
            GATE_ALL = s_dve.n
            GATE_ACT = a_pg_prev
            if DBG:
                sp_dma(d_dbg_w[:, :], w_sb[:, :], s_dbg2, w=[(s_dve, GATE_ALL)])
                sch.add("sync", None, waits=[(s_dbg2, s_dbg2.n)])

            sil_sb = sb3("sil_sb", [128, NTT * FI], BF16)
            act_sb = [sb3(f"act{i}", [128, FI], BF16) for i in range(2)]
            actT = [sb3(f"actT{i}", [128, FI], BF16) for i in range(2)]

            sil_done = {}
            stt_done = {}
            att_done = {}
            atev_done = {}
            dn_done = {}
            yadd_done = {}
            gate_mm = {}
            up_mm = {}

            for g in range(NG):
                for tt in range(NTT):
                    even = (tt % 2 == 0)
                    psum = p_h if even else p_dn
                    w = [(s_gate, 256 * (g + 1)), (s_dve, XL2T_ALL)]
                    if g == 0 and not even:
                        w.append((s_act, GATE_ACT))
                    if g > 0:
                        if even:
                            w.append((s_dve, stt_done[(g - 1, NTT - 1)]))
                        else:
                            w.append((s_dve, yadd_done[(g - 1, NTT - 1)]))
                    if tt >= 2:
                        w.append((s_act, sil_done[(g, tt - 2)]))
                    for kc in range(KC):
                        for (o0, nn) in ((0, 512), (512, 256)):
                            sch.add("tensor", lambda e, psum=psum, tt=tt, kc=kc,
                                    o0=o0, nn=nn: e.matmul(
                                        psum[:, o0:o0 + nn],
                                        xln2T[:, kc * TOK + tt * 128:
                                              kc * TOK + (tt + 1) * 128],
                                        gate_buf[:, kc * FI + o0: kc * FI + o0 + nn],
                                        start=(kc == 0), stop=(kc == KC - 1)),
                                    waits=w if (kc == 0 and o0 == 0) else [],
                                    incs=[(s_pe, 1)] if (kc == KC - 1 and o0 == 512)
                                    else [])
                    gate_mm[(g, tt)] = s_pe.n
                    sch.add("scalar", lambda e, psum=psum, tt=tt: e.activation(
                        out=sil_sb[:, tt * FI:(tt + 1) * FI], in_=psum[:, 0:FI],
                        func=AF.Silu), waits=[(s_pe, gate_mm[(g, tt)])],
                        incs=[(s_act, 1)])
                    sil_done[(g, tt)] = s_act.n
                if g + 1 < NG:
                    for c in range(KC):
                        sch.add("gpsimd", lambda e, g=g, c=c: e.indirect_dma_start(
                            out=gate_buf[:, c * FI:(c + 1) * FI], out_offset=None,
                            in_=gup_flat,
                            in_offset=IndirectOffsetOnAxis(
                                idxgu[:, (g + 1) * KC + c:(g + 1) * KC + c + 1], 0)),
                            waits=[(s_pe, gate_mm[(g, NTT - 1)])] if c == 0 else [],
                            incs=[(s_gate, 16)])

                def emit_up(tt):
                    w = [(s_up, 256 * (g + 1)), (s_act, sil_done[(g, NTT - 1)])]
                    if tt > 0:
                        w.append((s_dve, stt_done[(g, tt - 1)]))
                    elif g > 0:
                        w.append((s_dve, stt_done[(g - 1, NTT - 1)]))
                    for kc in range(KC):
                        for (o0, nn) in ((0, 512), (512, 256)):
                            sch.add("tensor", lambda e, tt=tt, kc=kc, o0=o0, nn=nn:
                                    e.matmul(
                                        p_h[:, o0:o0 + nn],
                                        xln2T[:, kc * TOK + tt * 128:
                                              kc * TOK + (tt + 1) * 128],
                                        up_buf[:, kc * FI + o0: kc * FI + o0 + nn],
                                        start=(kc == 0), stop=(kc == KC - 1)),
                                    waits=w if (kc == 0 and o0 == 0) else [],
                                    incs=[(s_pe, 1)] if (kc == KC - 1 and o0 == 512)
                                    else [])
                    up_mm[(g, tt)] = s_pe.n
                    ab = act_sb[tt % 2]
                    sch.add("vector", lambda e, tt=tt, g=g, ab=ab:
                            e.scalar_tensor_tensor(
                                out=ab[:, :], in0=sil_sb[:, tt * FI:(tt + 1) * FI],
                                scalar=w_sb[:, tt * NG + g: tt * NG + g + 1],
                                in1=p_h[:, :], op0=ALU.mult, op1=ALU.mult),
                            waits=[(s_pe, up_mm[(g, tt)]), (s_dve, GATE_ALL)],
                            drain=True, incs=[(s_dve, 1)])
                    stt_done[(g, tt)] = s_dve.n

                def emit_Tact(tt):
                    ab = act_sb[tt % 2]
                    w = [(s_dve, stt_done[(g, tt)])]
                    prev = (g, tt - 1) if tt > 0 else (g - 1, NTT - 1)
                    if prev in atev_done:
                        w.append((s_dve, atev_done[prev]))
                    for b6 in range(6):
                        sch.add("tensor", lambda e, ab=ab, b6=b6: e.matmul(
                            p_at[:, b6 * 128:(b6 + 1) * 128],
                            ab[:, b6 * 128:(b6 + 1) * 128], ident[:, :],
                            start=True, stop=True),
                            waits=w if b6 == 0 else [],
                            incs=[(s_pe, 1)] if b6 == 5 else [])
                    att_done[(g, tt)] = s_pe.n
                    aT = actT[tt % 2]
                    sch.add("vector", lambda e, aT=aT: e.tensor_copy(
                        aT[:, :], p_at[:, :]), waits=[(s_pe, att_done[(g, tt)])],
                        incs=[(s_dve, 1)])
                    atev_done[(g, tt)] = s_dve.n

                def emit_down(tt):
                    aT = actT[tt % 2]
                    w = [(s_dn, 96 * (g + 1)), (s_dve, atev_done[(g, tt)])]
                    prev = (g, tt - 1) if tt > 0 else (g - 1, NTT - 1)
                    if prev in yadd_done:
                        w.append((s_dve, yadd_done[prev]))
                    if tt == 0:
                        w.append((s_act, sil_done[(g, NTT - 1)]))
                    for ofc in range(4):
                        for ic in range(6):
                            sch.add("tensor", lambda e, aT=aT, ofc=ofc, ic=ic:
                                    e.matmul(
                                        p_dn[:, ofc * 512:(ofc + 1) * 512],
                                        aT[:, ic * 128:(ic + 1) * 128],
                                        dn_buf[:, ic * C + ofc * 512:
                                               ic * C + (ofc + 1) * 512],
                                        start=(ic == 0), stop=(ic == 5)),
                                    waits=w if (ofc == 0 and ic == 0) else [],
                                    incs=[(s_pe, 1)] if ic == 5 else [])
                        dn_done[(g, tt, ofc)] = s_pe.n
                    for ofc in range(4):
                        sch.add("vector", lambda e, tt=tt, ofc=ofc: e.tensor_add(
                            xsave[:, tt * C + ofc * 512: tt * C + (ofc + 1) * 512],
                            xsave[:, tt * C + ofc * 512: tt * C + (ofc + 1) * 512],
                            p_dn[:, ofc * 512:(ofc + 1) * 512]),
                            waits=[(s_pe, dn_done[(g, tt, ofc)])],
                            incs=[(s_dve, 1)] if ofc == 3 else [])
                    yadd_done[(g, tt)] = s_dve.n

                for tt in range(NTT):
                    emit_up(tt)
                    if tt >= 1:
                        emit_Tact(tt - 1)
                    if tt >= 2:
                        emit_down(tt - 2)
                emit_Tact(NTT - 1)
                emit_down(NTT - 2)
                emit_down(NTT - 1)
                if g + 1 < NG:
                    for c in range(KC):
                        sch.add("gpsimd", lambda e, g=g, c=c: e.indirect_dma_start(
                            out=up_buf[:, c * FI:(c + 1) * FI], out_offset=None,
                            in_=gup_flat,
                            in_offset=IndirectOffsetOnAxis(
                                idxgu[:, (g + 1) * KC + c:(g + 1) * KC + c + 1], 0),
                            element_offset=FI),
                            waits=[(s_pe, up_mm[(g, NTT - 1)])] if c == 0 else [],
                            incs=[(s_up, 16)])
                    for c in range(6):
                        sch.add("gpsimd", lambda e, g=g, c=c: e.indirect_dma_start(
                            out=dn_buf[:, c * C:(c + 1) * C], out_offset=None,
                            in_=dnt_flat,
                            in_offset=IndirectOffsetOnAxis(
                                idxdn[:, (g + 1) * 6 + c:(g + 1) * 6 + c + 1], 0)),
                            waits=[(s_pe, dn_done[(g, NTT - 1, 3)])] if c == 0 else [],
                            incs=[(s_dn, 16)])

            for tt in range(NTT):
                sch.add("sync", lambda e, tt=tt: e.dma_start(
                    out=d_out[tt * 128:(tt + 1) * 128, :],
                    in_=xsave[:, tt * C:(tt + 1) * C]),
                    waits=[(s_dve, yadd_done[(NG - 1, tt)])], incs=[(s_out, 16)])
            sch.add("sync", None, waits=[(s_out, s_out.n)])

            with nc.Block() as blk:
                sch.flush(blk)

    return nc


def _host_constants(j):
    perm = np.concatenate([np.arange(0, 128, 2), np.arange(1, 128, 2)])
    ident = np.eye(128, dtype=np.float32).astype(ml_dtypes.bfloat16)
    r = np.arange(128)[:, None]
    c = np.arange(512)[None, :]
    masks = np.concatenate(
        [np.where(c <= r + 128 * m, np.float32(0.0), np.float32(MASKVAL)).astype(np.float32)
         for m in range(4)], axis=1)
    inv_freq = (1.0 / (10000.0 ** (np.arange(64, dtype=np.float32) * 2.0 / 128.0))
                ).astype(np.float32)
    emb = (np.arange(T, dtype=np.float32)[:, None] * inv_freq[None, :]).astype(np.float32)
    cosT = np.cos(emb).T.astype(np.float32)
    sinT = np.sin(emb).T.astype(np.float32)
    ct = np.ascontiguousarray(np.concatenate([cosT, cosT], axis=0))
    st = np.ascontiguousarray(np.concatenate([-sinT, sinT], axis=0))
    swap = np.zeros((128, 128), dtype=np.float32)
    swap[np.arange(128), (np.arange(128) + 64) % 128] = 1.0
    ones = np.ones((1, 128), dtype=np.float32)
    rbgu = (np.arange(128)[:, None] + 128 * np.arange(KC)[None, :]).astype(np.float32)
    rbdn = (np.arange(128)[:, None] + 128 * np.arange(6)[None, :]).astype(np.float32)
    hhtt = np.arange(32)
    half = j // 2
    rby = (np.arange(128)[:, None] +
           (half * 8192 + (hhtt // NTT) * 2048 + (j % 2) * 1024 +
            (hhtt % NTT) * 128)[None, :]).astype(np.float32)
    jc = np.zeros((1, 1), dtype=np.float32)
    return perm, ident, masks, ct, st, ones, rbgu, rbdn, rby, jc, swap


def kernel(L0_attn_x, L0_attn_Wq, L0_attn_bq, L0_attn_Wk, L0_attn_bk,
           L0_attn_Wv, L0_attn_bv, L0_attn_Wo, L0_attn_bo,
           L0_ffn_gate_weight, L0_ffn_gate_up_proj, L0_ffn_down_proj,
           L0_ffn_merge_groups, L0_ffn_dominant_experts):
    x = np.asarray(L0_attn_x, dtype=np.float32)
    Wq = np.asarray(L0_attn_Wq, dtype=np.float32)
    bq = np.asarray(L0_attn_bq, dtype=np.float32)
    Wk = np.asarray(L0_attn_Wk, dtype=np.float32)
    bk = np.asarray(L0_attn_bk, dtype=np.float32)
    Wv = np.asarray(L0_attn_Wv, dtype=np.float32)
    bv = np.asarray(L0_attn_bv, dtype=np.float32)
    Wo = np.ascontiguousarray(np.asarray(L0_attn_Wo, dtype=np.float32))
    bo = np.ascontiguousarray(np.asarray(L0_attn_bo, dtype=np.float32))
    gw = np.asarray(L0_ffn_gate_weight, dtype=np.float32)
    gup = np.asarray(L0_ffn_gate_up_proj, dtype=np.float32)
    dnp = np.asarray(L0_ffn_down_proj, dtype=np.float32)
    mg = np.ascontiguousarray(np.asarray(L0_ffn_merge_groups).astype(np.int32))
    de = np.ascontiguousarray(np.asarray(L0_ffn_dominant_experts).astype(np.int32))

    gwt = np.ascontiguousarray(gw.T)
    gup_t = np.ascontiguousarray(gup.transpose(0, 2, 1))
    dnt = np.ascontiguousarray(dnp.transpose(0, 2, 1))

    nc = build_program()

    in_maps = []
    for core in range(8):
        b, h, j = core // 4, core % 4, core % 4
        (perm, ident, masks, ct, st, ones, rbgu, rbdn, rby, jc,
         swap) = _host_constants(j)
        qcols = np.concatenate([512 * h + 128 * g + perm for g in range(4)])
        kcols = 128 * h + perm
        im = {
            "x_b": np.ascontiguousarray(x[b]),
            "xown": np.ascontiguousarray(x[b, TOK * j:TOK * (j + 1)]),
            "wq": np.ascontiguousarray(Wq[:, qcols]),
            "bq": np.ascontiguousarray(bq[qcols]),
            "wk": np.ascontiguousarray(Wk[:, kcols]),
            "bk": np.ascontiguousarray(bk[kcols]),
            "wv": np.ascontiguousarray(Wv[:, 128 * h:128 * (h + 1)]),
            "bv": np.ascontiguousarray(bv[128 * h:128 * (h + 1)]),
            "wo": Wo, "bo": bo, "gwt": gwt, "gup": gup_t, "dnt": dnt,
            "mg": mg, "de": de,
            "identh": ident, "masks": masks, "ct": ct, "st": st,
            "onescol": ones, "rbgu": rbgu, "rbdn": rbdn, "rby": rby,
            "jconst": jc, "swaph": swap,
        }
        in_maps.append(im)

    res = run_bass_kernel_spmd(nc, in_maps, core_ids=list(range(8)))

    out = np.empty((B, T, C), dtype=np.float32)
    for core in range(8):
        b, j = core // 4, core % 4
        out[b, TOK * j:TOK * (j + 1)] = res.results[core]["out"]
    return out
